# version 07
# baseline (speedup 1.0000x reference)
import contextlib
import numpy as np
import concourse.bass as bass
import concourse.mybir as mybir
from concourse.alu_op_type import AluOpType as op
from concourse.bass_utils import run_bass_kernel_spmd

F32 = mybir.dt.float32
BF16 = mybir.dt.bfloat16
FP16 = mybir.dt.float16
AF = mybir.ActivationFunctionType

Bsz, T, C = 4, 2048, 2048
H, DH, W = 16, 128, 256
NH = 8192
NTOK = 1024
HALO = 256
NTOKH = NTOK + HALO
NSEL = 22
SCALE = float(DH) ** -0.5
LN1E4 = float(np.log(10000.0))
PI = float(np.pi)
NEG = -2.0e9
BIGH = 30000.0

ENGS = ("sync", "scalar", "vector", "tensor", "gpsimd")


class Sem:
    def __init__(self, h):
        self.h = h
        self.n = 0


class Prog:
    def __init__(self):
        self.streams = {e: [] for e in ENGS}

    def emit(self, eng, fn, waits=(), inc=None, inck=1):
        if inc is not None:
            inc.n += inck
        self.streams[eng].append((fn, tuple(waits), inc, inck))

    def dr(self, eng):
        self.streams[eng].append((lambda e: e.drain(), (), None, 1))

    def run_block(self, nc):
        with nc.Block() as blk:
            for eng in ENGS:
                items = self.streams[eng]
                if not items:
                    continue

                def mk(items):
                    def body(e):
                        for fn, waits, inc, inck in items:
                            for (s, v) in waits:
                                if v > 0:
                                    e.wait_ge(s.h, v)
                            instr = fn(e)
                            if inc is not None:
                                instr.then_inc(inc.h, inck)
                    return body
                getattr(blk, eng)(mk(items))
        self.streams = {e: [] for e in ENGS}


def build_nc(debug_taps=(), upto=99, p0mask=7):
    nc = bass.Bass("TRN2", target_bir_lowering=False)
    es = contextlib.ExitStack()

    def sb(name, shape, dt):
        return es.enter_context(nc.sbuf_tensor(name, shape, dt))

    def mksem(name):
        return Sem(es.enter_context(nc.semaphore(name)))

    xs = nc.dram_tensor("xs", [NTOKH, C], F32, kind="ExternalInput")
    wqT = nc.dram_tensor("wqT", [4, C, 512], F32, kind="ExternalInput")
    wkT = nc.dram_tensor("wkT", [4, C, 512], F32, kind="ExternalInput")
    wvT = nc.dram_tensor("wvT", [4, C, 512], F32, kind="ExternalInput")
    woT = nc.dram_tensor("woT", [C, C], F32, kind="ExternalInput")
    wsT = nc.dram_tensor("wsT", [64, C, 128], F32, kind="ExternalInput")
    woutT = nc.dram_tensor("woutT", [16, NH, 128], F32, kind="ExternalInput")
    bsynt = nc.dram_tensor("bsynt", [1, NH], F32, kind="ExternalInput")
    boutt = nc.dram_tensor("boutt", [1, C], F32, kind="ExternalInput")
    gammat = nc.dram_tensor("gammat", [1, C], F32, kind="ExternalInput")
    betat = nc.dram_tensor("betat", [1, C], F32, kind="ExternalInput")
    maskft = nc.dram_tensor("maskft", [128, 1], F32, kind="ExternalInput")
    outt = nc.dram_tensor("out", [NTOK, C], F32, kind="ExternalOutput")

    xnsp = nc.dram_tensor("xnsp", [8, 128, C], BF16, kind="Internal")
    qsp = nc.dram_tensor("qsp", [16, 128, NTOK], BF16, kind="Internal")
    ksp = nc.dram_tensor("ksp", [16, 128, NTOKH], BF16, kind="Internal")
    vsp = nc.dram_tensor("vsp", [NTOKH, C], BF16, kind="Internal")
    asp = nc.dram_tensor("asp", [16, 128, NTOK], BF16, kind="Internal")
    osp = nc.dram_tensor("osp", [C, NTOK], BF16, kind="Internal")

    taps = {}
    for name, shape, dt in debug_taps:
        taps[name] = nc.dram_tensor(name, shape, dt, kind="ExternalOutput")

    ident = sb("ident", [128, 128], BF16)
    identh = sb("identh", [128, 128], FP16)
    onesb = sb("onesb", [1, 128], BF16)
    onesf = sb("onesf", [1, 128], F32)
    onesh = sb("onesh", [1, 512], FP16)
    stats = sb("stats", [128, 120], F32)
    scal = sb("scal", [128, 8], F32)
    maskfb = sb("maskfb", [128, 1], F32)
    macc = sb("macc", [128, 2], F32)
    cstb = sb("cstb", [128, 2], F32)
    pX = contextlib.ExitStack()
    xnTc = pX.enter_context(nc.sbuf_tensor("xnTc", [128, 16 * NTOK], BF16))
    pA = contextlib.ExitStack()
    xnTh = pA.enter_context(nc.sbuf_tensor("xnTh", [128, 16 * HALO], BF16))
    cosT = pA.enter_context(nc.sbuf_tensor("cosT", [128, NTOKH], BF16))
    sinsT = pA.enter_context(nc.sbuf_tensor("sinsT", [128, NTOKH], BF16))

    s_dx = mksem("s_dx")
    s_dw = mksem("s_dw")
    s_dm = mksem("s_dm")
    s_ds = mksem("s_ds")
    s_dr = mksem("s_dr")
    s_do = [mksem("s_do0"), mksem("s_do1")]
    s_mm = mksem("s_mm")
    s_ve = mksem("s_ve")
    s_ac = mksem("s_ac")
    s_gp = mksem("s_gp")
    s_tt = mksem("s_tt")
    s_mf = mksem("s_mf")
    s_x = [mksem("s_x0"), mksem("s_x1")]
    s_sp = [mksem("s_sp0"), mksem("s_sp1")]
    s_wb = [mksem("s_wb0"), mksem("s_wb1")]
    s_ms = [mksem("s_ms0"), mksem("s_ms1")]
    s_qsp = mksem("s_qsp")
    s_vb = [mksem("s_vb0"), mksem("s_vb1")]
    s_wd = [mksem("s_wd0"), mksem("s_wd1")]
    s_at = mksem("s_at")
    s_sg = [mksem("s_sg0"), mksem("s_sg1")]
    s_bs = mksem("s_bs")
    s_s5 = [mksem("s_s50"), mksem("s_s51")]
    s_bo = mksem("s_bo")
    s_os = [mksem("s_os0"), mksem("s_os1")]
    s_l = [mksem("s_l0"), mksem("s_l1")]
    s_wo = mksem("s_wo")
    s_gb2 = mksem("s_gb2")
    s_sw = mksem("s_sw")

    DMASEMS = [s_ds, s_dr, s_do[0], s_do[1], s_mf, s_x[0], s_x[1], s_sp[0], s_sp[1],
               s_wb[0], s_wb[1], s_ms[0], s_ms[1], s_qsp, s_vb[0], s_vb[1],
               s_wd[0], s_wd[1], s_at, s_sg[0], s_sg[1], s_bs, s_s5[0],
               s_s5[1], s_bo, s_os[0], s_os[1], s_l[0], s_l[1], s_wo, s_gb2, s_sw,
               s_dx, s_dw, s_dm]

    def quiesce():
        ws = [(sm, sm.n) for sm in DMASEMS if sm.n > 0]
        if ws:
            P.emit("sync", lambda e: e.nop(), waits=ws)

    P = Prog()

    with contextlib.ExitStack() as p0:
        def p0sb(name, shape, dt):
            return p0.enter_context(nc.sbuf_tensor(name, shape, dt))
        ct = p0sb("ct", [128, NTOKH], F32)
        st = p0sb("st", [128, NTOKH], F32)
        ang = p0sb("ang", [128, 24], F32)
        tmpl = p0sb("tmpl", [128, 2048], F32)
        xb = [p0sb(f"xb{i}", [128, C], F32) for i in range(2)]
        xnb = [p0sb(f"xnb{i}", [128, C], BF16) for i in range(2)]
        trash = p0sb("trash", [128, 2 * C], FP16)
        ptp = [p0.enter_context(nc.psum_tensor(f"ptp{i}", [128, 1024], BF16))
               for i in range(2)]

        P.emit("gpsimd", lambda e: e.memset(stats[:, :], 0.0))
        P.emit("gpsimd", lambda e: e.memset(scal[:, :], 0.0))
        P.emit("gpsimd", lambda e: e.memset(ident[:, :], 0.0), inc=s_gp)
        P.emit("gpsimd", lambda e: e.affine_select(
            out=ident[:, :], in_=ident[:, :], compare_op=op.not_equal,
            fill=1.0, base=0, pattern=[[-1, 128]], channel_multiplier=1),
            waits=[(s_gp, s_gp.n)])
        P.emit("gpsimd", lambda e: e.memset(identh[:, :], 0.0), inc=s_gp)
        P.emit("gpsimd", lambda e: e.affine_select(
            out=identh[:, :], in_=identh[:, :], compare_op=op.not_equal,
            fill=1.0, base=0, pattern=[[-1, 128]], channel_multiplier=1),
            waits=[(s_gp, s_gp.n)])
        P.emit("gpsimd", lambda e: e.memset(onesb[:, :], 1.0))
        P.emit("gpsimd", lambda e: e.memset(onesf[:, :], 1.0))
        P.emit("gpsimd", lambda e: e.memset(onesh[:, :], 1.0))
        P.emit("gpsimd", lambda e: e.memset(macc[:, :], 0.0))
        P.emit("gpsimd", lambda e: e.memset(stats[:, 10:11], 1.0))
        P.emit("gpsimd", lambda e: e.memset(cstb[:, 0:1], PI / 2))
        P.emit("gpsimd", lambda e: e.memset(cstb[:, 1:2], 1e-5))
        P.emit("gpsimd", lambda e: e.iota(
            stats[:, 14:15], [[1, 1]], channel_multiplier=1,
            allow_small_or_imprecise_dtypes=True), inc=s_gp)

        P.emit("sync", lambda e: e.dma_start(maskfb[:, :], maskft[:, :]),
               inc=s_mf, inck=16)

        P.emit("vector", lambda e: e.tensor_scalar(
            stats[:, 15:16], stats[:, 14:15], 64.0, None, op0=op.is_ge),
            waits=[(s_gp, s_gp.n)])
        P.dr("vector")
        P.emit("vector", lambda e: e.scalar_tensor_tensor(
            stats[:, 13:14], stats[:, 15:16], -64.0, stats[:, 14:15],
            op0=op.mult, op1=op.add))
        P.emit("vector", lambda e: e.tensor_scalar(
            stats[:, 12:13], stats[:, 15:16], 2.0, -1.0, op0=op.mult, op1=op.add),
            inc=s_ve)
        P.dr("vector")

        P.emit("scalar", lambda e: e.activation(
            stats[:, 11:12], stats[:, 13:14], AF.Exp, scale=-LN1E4 / 64.0),
            waits=[(s_ve, s_ve.n)])
        P.dr("scalar")
        P.emit("scalar", lambda e: e.activation(
            ct[:, 1:2], stats[:, 11:12], AF.Sin, bias=cstb[:, 0:1]))
        P.emit("scalar", lambda e: e.activation(
            st[:, 1:2], stats[:, 11:12], AF.Sin), inc=s_ac)

        def ladder(e):
            e.memset(ct[:, 0:1], 1.0)
            e.memset(st[:, 0:1], 0.0)
            e.drain()
            wl = 2
            while wl < NTOKH:
                L = min(wl, NTOKH - wl)
                c1, s1 = ct[:, 1:2], st[:, 1:2]
                cp, sp = ct[:, wl - 1:wl], st[:, wl - 1:wl]
                e.tensor_tensor(tmpl[:, 0:1], cp, c1, op=op.mult)
                e.tensor_tensor(tmpl[:, 1:2], sp, s1, op=op.mult)
                e.tensor_tensor(tmpl[:, 2:3], cp, s1, op=op.mult)
                e.tensor_tensor(tmpl[:, 3:4], sp, c1, op=op.mult)
                e.drain()
                e.tensor_tensor(ang[:, 0:1], tmpl[:, 0:1], tmpl[:, 1:2],
                                op=op.subtract)
                e.tensor_tensor(ang[:, 1:2], tmpl[:, 2:3], tmpl[:, 3:4],
                                op=op.add)
                e.drain()
                cr, sr = ang[:, 0:1], ang[:, 1:2]
                ta, tb = tmpl[:, 0:L], tmpl[:, 1024:1024 + L]
                e.tensor_scalar(ta, st[:, 0:L], sr, None, op0=op.mult)
                e.tensor_scalar(tb, st[:, 0:L], cr, None, op0=op.mult)
                e.drain()
                e.scalar_tensor_tensor(ct[:, wl:wl + L], ct[:, 0:L], cr,
                                       ta, op0=op.mult, op1=op.subtract)
                e.scalar_tensor_tensor(st[:, wl:wl + L], ct[:, 0:L], sr,
                                       tb, op0=op.mult, op1=op.add)
                e.drain()
                wl *= 2
            e.tensor_copy(cosT[:, :], ct[:, :])
            e.drain()
            return e.tensor_scalar(sinsT[:, :], st[:, :], stats[:, 12:13], None,
                                   op0=op.mult)
        if p0mask & 1:
            P.emit("vector", ladder, waits=[(s_ac, s_ac.n)], inc=s_ve)
        else:
            P.emit("vector", lambda e: e.tensor_copy(cosT[:, :], ct[:, :]),
                   waits=[(s_ac, s_ac.n)])
            P.emit("vector", lambda e: e.tensor_copy(sinsT[:, :], st[:, :]),
                   inc=s_ve)

        cast_done = {}
        drain_done = {}
        tp_done = {}
        spill_xn = {}
        for i in range(10):
            b, nb = xb[i % 2], xnb[i % 2]
            w = [(s_ac, cast_done[i - 2])] if i >= 2 else []
            P.emit("sync", lambda e, b=b, i=i: e.dma_start(
                b[:, :], xs[128 * i:128 * (i + 1), :]), waits=w,
                inc=s_x[i % 2], inck=16)
            P.emit("vector", lambda e, b=b: e.tensor_scalar(
                trash[:, 0:C], b[:, :], 0.0, None, op0=op.add, op1=op.add,
                accum_out=stats[:, 0:1]), waits=[(s_x[i % 2], s_x[i % 2].n)])
            P.emit("scalar", lambda e, b=b: e.activation(
                trash[:, C:2 * C], b[:, :], AF.Square,
                accum_out=stats[:, 1:2]),
                waits=[(s_x[i % 2], s_x[i % 2].n)], inc=s_ac)
            P.dr("vector")
            P.emit("vector", lambda e: e.tensor_scalar(
                stats[:, 2:3], stats[:, 0:1], 1.0 / C, None, op0=op.mult),
                waits=[(s_ac, s_ac.n)])
            P.emit("vector", lambda e: e.tensor_scalar(
                stats[:, 3:4], stats[:, 1:2], 1.0 / C, None, op0=op.mult))
            P.dr("vector")
            P.emit("vector", lambda e: e.tensor_scalar(
                stats[:, 28:29], stats[:, 2:3], -1.0, None, op0=op.mult))
            P.emit("vector", lambda e: e.tensor_tensor(
                stats[:, 4:5], stats[:, 2:3], stats[:, 2:3], op=op.mult))
            P.dr("vector")
            P.emit("vector", lambda e: e.tensor_tensor(
                stats[:, 5:6], stats[:, 3:4], stats[:, 4:5], op=op.subtract),
                inc=s_ve)
            P.emit("scalar", lambda e: e.activation(
                stats[:, 6:7], stats[:, 5:6], AF.Sqrt, bias=cstb[:, 1:2]),
                waits=[(s_ve, s_ve.n)], inc=s_ac)
            P.emit("vector", lambda e: e.reciprocal(stats[:, 7:8], stats[:, 6:7]),
                   waits=[(s_ac, s_ac.n)])
            P.dr("vector")
            P.emit("vector", lambda e: e.tensor_tensor(
                stats[:, 9:10], stats[:, 28:29], stats[:, 7:8], op=op.mult),
                inc=s_ve)
            wcast = [(s_ve, s_ve.n)]
            if i >= 2:
                wcast.append((s_tt, tp_done[i - 2]))
                if (i - 2) in spill_xn:
                    wcast.append(spill_xn[i - 2])
            P.emit("scalar", lambda e, b=b, nb=nb: e.activation(
                nb[:, :], b[:, :], AF.Identity, bias=stats[:, 9:10],
                scale=stats[:, 7:8]), waits=wcast, inc=s_ac)
            cast_done[i] = s_ac.n
            if i >= 2:
                P.emit("sync", lambda e, nb=nb, i=i: e.dma_start(
                    xnsp[i - 2, :, :], nb[:, :]), waits=[(s_ac, s_ac.n)],
                    inc=s_sp[i % 2], inck=16)
                spill_xn[i] = (s_sp[i % 2], s_sp[i % 2].n)
            dstf = xnTc if i >= 2 else xnTh
            ncols = NTOK if i >= 2 else HALO
            toff = (i - 2) * 128 if i >= 2 else i * 128
            for g in range(2):
                pt = ptp[g]
                w = [(s_ac, cast_done[i])]
                if (i, g) != (0, 0) and (i, g) != (0, 1):
                    w.append((s_ve, drain_done[(i - 1, g)]))
                for kk in range(8):
                    k = 8 * g + kk
                    P.emit("tensor", lambda e, pt=pt, nb=nb, k=k, kk=kk: e.transpose(
                        pt[:, 128 * kk:128 * (kk + 1)],
                        nb[:, 128 * k:128 * (k + 1)], ident[:, :]),
                        waits=(w if kk == 0 else []), inc=s_tt)

                def drain(e, pt=pt, dstf=dstf, ncols=ncols, toff=toff, g=g):
                    dst = dstf[:, :].rearrange("p (k t) -> p k t", t=ncols)[
                        :, 8 * g:8 * g + 8, toff:toff + 128]
                    return e.tensor_copy(dst, pt[:, :])
                P.emit("vector", drain, waits=[(s_tt, s_tt.n)], inc=s_ve)
                drain_done[(i, g)] = s_ve.n
            tp_done[i] = s_tt.n
        quiesce()
        P.run_block(nc)
    if upto <= 0:
        pA.close(); pX.close(); es.close()
        return nc

    with contextlib.ExitStack() as p2:
        def p2sb(name, shape, dt):
            return p2.enter_context(nc.sbuf_tensor(name, shape, dt))
        wbuf = [p2sb(f"wbuf{i}", [128, 16 * 512], BF16) for i in range(2)]
        qkb = p2sb("qkb", [128, NTOKH], BF16)
        rsw = p2sb("rsw", [128, NTOKH], BF16)
        rt1 = p2sb("rt1", [128, NTOKH], BF16)
        rt2 = p2sb("rt2", [128, NTOKH], BF16)
        vb = [p2sb(f"vb{i}", [128, 512], BF16) for i in range(2)]
        mst = [p2sb(f"mst{i}", [128, 2048], F32) for i in range(2)]
        mtr = p2sb("mtr", [128, 2048], FP16)
        pj = [p2.enter_context(nc.psum_tensor(f"pj{i}", [128, 512], F32))
              for i in range(4)]

        mean_ve = {}
        for i in range(128):
            ms = mst[i % 2]
            w = [(s_ve, mean_ve[i - 2])] if i >= 2 else []
            if i < 64:
                P.emit("scalar", lambda e, ms=ms, i=i: e.dma_start(
                    ms[:, :].rearrange("p (k o) -> p k o", o=128),
                    wsT[i].rearrange("(k p) o -> p k o", p=128)),
                    waits=w, inc=s_ms[i % 2], inck=16)
            else:
                d, q4 = (i - 64) // 4, (i - 64) % 4
                P.emit("scalar", lambda e, ms=ms, d=d, q4=q4: e.dma_start(
                    ms[:, :].rearrange("p (k o) -> p k o", o=128),
                    woutT[d, q4 * 2048:(q4 + 1) * 2048, :]
                    .rearrange("(k p) o -> p k o", p=128)),
                    waits=w, inc=s_ms[i % 2], inck=16)
            col = 0 if i < 64 else 1
            tc = 20 + (i % 2)
            P.emit("scalar", lambda e, ms=ms, tc=tc: e.activation(
                mtr[:, :], ms[:, :], AF.Abs, accum_out=stats[:, tc:tc + 1]),
                waits=[(s_ms[i % 2], s_ms[i % 2].n)], inc=s_ac)
            P.emit("vector", lambda e, col=col, tc=tc: e.tensor_tensor(
                macc[:, col:col + 1], macc[:, col:col + 1], stats[:, tc:tc + 1],
                op=op.add), waits=[(s_ac, s_ac.n)], inc=s_ve)
            P.dr("vector")
            mean_ve[i] = s_ve.n

        wts = {"q": wqT, "k": wkT, "v": wvT}
        NGR = {"q": [(256, 512), (768, 512)],
               "k": [(0, 256), (256, 512), (768, 512)]}
        chunk_use = {}
        pj_drain = {}
        qkb_spill = [0]
        vb_spill = {}

        def xnT_slice(k, t0, tl):
            if t0 < HALO:
                assert t0 + tl <= HALO
                return xnTh[:, k * HALO + t0: k * HALO + t0 + tl]
            return xnTc[:, k * NTOK + (t0 - HALO): k * NTOK + (t0 - HALO) + tl]

        cidx = 0
        for wi, wn in enumerate("qkv"):
            for c in range(4):
                wb = wbuf[cidx % 2]
                w = [(s_mm, chunk_use.get(cidx - 2, 0))]
                P.emit("gpsimd", lambda e, wb=wb, wn=wn, c=c: e.dma_start(
                    wb[:, :].rearrange("p (k o) -> p k o", o=512),
                    wts[wn][c].rearrange("(k p) o -> p k o", p=128)),
                    waits=w, inc=s_wb[cidx % 2], inck=16)
                wb_ready = (s_wb[cidx % 2], s_wb[cidx % 2].n)
                if wn in "qk":
                    ntok = NTOK if wn == "q" else NTOKH
                    spt = qsp if wn == "q" else ksp
                    for m in range(4):
                        mg = 4 * c + m
                        pjs_used = []
                        for gi, (t0, tl) in enumerate(NGR[wn]):
                            pji = (mg * 3 + gi) % 4
                            pjs_used.append((pji, t0, tl))
                            wfirst = [wb_ready,
                                      (s_ve, pj_drain.get(pji, 0))]
                            for k in range(16):
                                P.emit("tensor",
                                       lambda e, pji=pji, tl=tl, k=k, t0=t0,
                                       wb=wb, m=m: e.matmul(
                                           pj[pji][:, 0:tl],
                                           wb[:, k * 512 + m * 128:
                                              k * 512 + (m + 1) * 128],
                                           xnT_slice(k, t0, tl),
                                           start=(k == 0), stop=(k == 15)),
                                       waits=(wfirst if k == 0 else []),
                                       inc=(s_mm if k == 15 else None))
                        mm_here = s_mm.n
                        first_drain = True
                        for (pji, t0, tl) in pjs_used:
                            off = t0 if wn == "k" else t0 - HALO
                            wd = [(s_mm, mm_here)]
                            if first_drain:
                                wd.append((s_qsp, qkb_spill[0]))
                                first_drain = False
                            lastd = (pji == pjs_used[-1][0])
                            P.emit("vector", lambda e, pji=pji, off=off, tl=tl:
                                   e.tensor_copy(qkb[:, off:off + tl],
                                                 pj[pji][:, 0:tl]),
                                   waits=wd,
                                   inc=(s_ve if (pji, t0, tl) == pjs_used[-1]
                                        else None))
                        for (pji, t0, tl) in pjs_used:
                            pj_drain[pji] = s_ve.n
                        P.emit("gpsimd", lambda e, ntok=ntok: e.dma_start(
                            rsw[0:64, 0:ntok], qkb[64:128, 0:ntok]),
                            waits=[(s_ve, s_ve.n), (s_qsp, qkb_spill[0])],
                            inc=s_sw, inck=16)
                        P.emit("gpsimd", lambda e, ntok=ntok: e.dma_start(
                            rsw[64:128, 0:ntok], qkb[0:64, 0:ntok]),
                            inc=s_sw, inck=16)
                        tsl = (HALO, NTOKH) if wn == "q" else (0, NTOKH)
                        P.emit("vector", lambda e, ntok=ntok, tsl=tsl: e.tensor_tensor(
                            rt1[:, 0:ntok], rsw[:, 0:ntok],
                            sinsT[:, tsl[0]:tsl[1]], op=op.mult),
                            waits=[(s_sw, s_sw.n)])
                        P.emit("vector", lambda e, ntok=ntok, tsl=tsl: e.tensor_tensor(
                            rt2[:, 0:ntok], qkb[:, 0:ntok],
                            cosT[:, tsl[0]:tsl[1]], op=op.mult))
                        P.dr("vector")
                        P.emit("vector", lambda e, ntok=ntok: e.tensor_tensor(
                            qkb[:, 0:ntok], rt2[:, 0:ntok], rt1[:, 0:ntok],
                            op=op.add), inc=s_ve)
                        P.emit("sync", lambda e, spt=spt, mg=mg, ntok=ntok:
                               e.dma_start(spt[mg, :, 0:ntok], qkb[:, 0:ntok]),
                               waits=[(s_ve, s_ve.n)], inc=s_qsp, inck=16)
                        qkb_spill[0] = s_qsp.n
                        chunk_use[cidx] = s_mm.n
                else:
                    for m in range(10):
                        pji = m % 2
                        vbi = vb[m % 2]
                        wfirst = [wb_ready, (s_ve, pj_drain.get(pji, 0))]
                        for k in range(16):
                            t0 = m * 128
                            P.emit("tensor",
                                   lambda e, pji=pji, k=k, t0=t0, wb=wb:
                                   e.matmul(pj[pji][:, :],
                                            xnT_slice(k, t0, 128),
                                            wb[:, k * 512:(k + 1) * 512],
                                            start=(k == 0), stop=(k == 15)),
                                   waits=(wfirst if k == 0 else []),
                                   inc=(s_mm if k == 15 else None))
                        P.emit("vector", lambda e, pji=pji, vbi=vbi: e.tensor_copy(
                            vbi[:, :], pj[pji][:, :]),
                            waits=[(s_mm, s_mm.n)]
                            + ([vb_spill[m % 2]] if (m % 2) in vb_spill else []),
                            inc=s_ve)
                        pj_drain[pji] = s_ve.n
                        P.emit("sync", lambda e, vbi=vbi, m=m, c=c: e.dma_start(
                            vsp[m * 128:(m + 1) * 128, c * 512:(c + 1) * 512],
                            vbi[:, :]), waits=[(s_ve, s_ve.n)],
                            inc=s_vb[m % 2], inck=16)
                        vb_spill[m % 2] = (s_vb[m % 2], s_vb[m % 2].n)
                    chunk_use[cidx] = s_mm.n
                cidx += 1
        quiesce()
        P.run_block(nc)
    pA.close()
    if upto <= 2:
        pX.close(); es.close()
        return nc

    with contextlib.ExitStack() as p3:
        def p3sb(name, shape, dt):
            return p3.enter_context(nc.sbuf_tensor(name, shape, dt))
        qwin = [p3sb(f"qwin{i}", [128, 16 * 256], BF16) for i in range(2)]
        kwin = [p3sb(f"kwin{i}", [128, 16 * 512], BF16) for i in range(2)]
        vwin = [p3sb(f"vwin{i}", [128, 4 * 2048], BF16) for i in range(2)]
        probs = p3sb("probs", [128, 1024], BF16)
        pTs = p3sb("pTs", [128, 1024], BF16)
        atile = p3sb("atile", [128, 16 * 256], BF16)
        mt = [p3sb(f"mt{i}", [128, 256], F32) for i in range(2)]
        psc = [p3.enter_context(nc.psum_tensor(f"psc{i}", [128, 512], F32))
               for i in range(4)]
        ppt = [p3.enter_context(nc.psum_tensor(f"ppt{i}", [128, 1024], BF16))
               for i in range(2)]
        pao = [p3.enter_context(nc.psum_tensor(f"pao{i}", [128, 512], F32))
               for i in range(2)]

        for j in range(2):
            P.emit("gpsimd", lambda e, j=j: e.memset(mt[j][:, :], 0.0), inc=s_gp)
            P.emit("gpsimd", lambda e, j=j: e.affine_select(
                out=mt[j][:, :], in_=mt[j][:, :], compare_op=op.is_ge,
                fill=NEG, base=j * 128, pattern=[[-1, 256]],
                channel_multiplier=1), waits=[(s_gp, s_gp.n)], inc=s_gp)
        mask_ready = s_gp.n

        win_used = {}
        sc_read = {}
        pt_drain = {}
        ao_drain = {}
        probs_tt = [0]
        pts_mm = [0]
        asp_prev = [0]
        spills_done = [(s_qsp, s_qsp.n), vb_spill[0], vb_spill[1]]
        for ci in range(4):
            qw, kw, vw = qwin[ci % 2], kwin[ci % 2], vwin[ci % 2]
            wreuse = [(s_mm, win_used.get(ci - 2, 0))]
            if ci == 0:
                wreuse += spills_done
            P.emit("sync", lambda e, qw=qw, ci=ci: e.dma_start(
                qw[:, :].rearrange("p (m t) -> p m t", t=256),
                qsp[:, :, ci * 256:(ci + 1) * 256]
                .rearrange("m p t -> p m t")), waits=wreuse,
                inc=s_wd[ci % 2], inck=16)
            P.emit("sync", lambda e, kw=kw, ci=ci: e.dma_start(
                kw[:, :].rearrange("p (m t) -> p m t", t=512),
                ksp[:, :, ci * 256:ci * 256 + 512]
                .rearrange("m p t -> p m t")), inc=s_wd[ci % 2], inck=16)
            P.emit("sync", lambda e, vw=vw, ci=ci: e.dma_start(
                vw[:, :].rearrange("p (m t) -> p m t", t=2048),
                vsp[ci * 256:ci * 256 + 512, :]
                .rearrange("(m p) t -> p m t", p=128)),
                inc=s_wd[ci % 2], inck=16)
            dr_here = (s_wd[ci % 2], s_wd[ci % 2].n)
            for h in range(16):
                it = ci * 16 + h
                b0, b1 = (2 * (h % 2)), (2 * (h % 2) + 1)
                for j, bj in enumerate((b0, b1)):
                    wfirst = [dr_here, (s_ac, sc_read.get(bj, 0))]
                    P.emit("tensor", lambda e, bj=bj, qw=qw, kw=kw, h=h, j=j:
                           e.matmul(psc[bj][:, :],
                                    qw[:, h * 256 + j * 128:h * 256 + (j + 1) * 128],
                                    kw[:, h * 512:(h + 1) * 512],
                                    start=True, stop=True), waits=wfirst,
                           inc=(s_mm if j == 1 else None))
                mm_here = s_mm.n
                nmask = 4 if ci == 0 else 2
                cnt_m = [0]
                for j, bj in enumerate((b0, b1)):
                    cnt_m[0] += 1
                    P.emit("vector", lambda e, bj=bj, j=j: e.tensor_tensor(
                        psc[bj][:, 256:512], psc[bj][:, 256:512], mt[j][:, :],
                        op=op.add), waits=([(s_mm, mm_here), (s_gp, mask_ready),
                                            (s_mf, 16)]
                                           if j == 0 else []),
                        inc=(s_ve if cnt_m[0] == nmask else None))
                    if ci == 0:
                        cnt_m[0] += 1
                        P.emit("vector", lambda e, bj=bj: e.tensor_scalar(
                            psc[bj][:, 0:256], psc[bj][:, 0:256], maskfb[:, :],
                            None, op0=op.add),
                            inc=(s_ve if cnt_m[0] == nmask else None))
                for j, bj in enumerate((b0, b1)):
                    P.emit("scalar", lambda e, bj=bj, j=j: e.activation(
                        probs[:, j * 512:(j + 1) * 512], psc[bj][:, :], AF.Exp,
                        scale=SCALE, accum_out=stats[:, 18 + j:19 + j]),
                        waits=([(s_ve, s_ve.n), (s_tt, probs_tt[0])]
                               if j == 0 else []),
                        inc=(s_ac if j == 1 else None))
                for j, bj in enumerate((b0, b1)):
                    sc_read[bj] = s_ac.n
                P.dr("vector")
                P.emit("vector", lambda e: e.reciprocal(
                    stats[:, 22:24], stats[:, 18:20]), waits=[(s_ac, s_ac.n)])
                P.dr("vector")
                for j in range(2):
                    P.emit("vector", lambda e, j=j: e.tensor_scalar(
                        probs[:, j * 512:(j + 1) * 512],
                        probs[:, j * 512:(j + 1) * 512],
                        stats[:, 22 + j:23 + j], None, op0=op.mult),
                        inc=(s_ve if j == 1 else None))
                norm_here = s_ve.n
                pb = ppt[h % 2]
                wfirst = [(s_ve, norm_here), (s_ve, pt_drain.get(h % 2, 0))]
                for j in range(2):
                    for bb in range(4):
                        P.emit("tensor", lambda e, pb=pb, j=j, bb=bb: e.transpose(
                            pb[:, bb * 256 + j * 128: bb * 256 + (j + 1) * 128],
                            probs[:, j * 512 + bb * 128: j * 512 + (bb + 1) * 128],
                            ident[:, :]),
                            waits=(wfirst if (j, bb) == (0, 0) else []),
                            inc=(s_tt if (j, bb) == (1, 3) else None))
                probs_tt[0] = s_tt.n
                P.emit("vector", lambda e, pb=pb: e.tensor_copy(pTs[:, :], pb[:, :]),
                       waits=[(s_tt, s_tt.n), (s_mm, pts_mm[0])], inc=s_ve)
                pt_drain[h % 2] = s_ve.n
                po = pao[h % 2]
                wfirst = [(s_ve, s_ve.n), (s_ve, ao_drain.get(h % 2, 0))]
                for bb in range(4):
                    P.emit("tensor", lambda e, po=po, vw=vw, h=h, bb=bb: e.matmul(
                        po[:, 0:256], vw[:, bb * 2048 + h * 128:
                                         bb * 2048 + (h + 1) * 128],
                        pTs[:, bb * 256:(bb + 1) * 256],
                        start=(bb == 0), stop=(bb == 3)),
                        waits=(wfirst if bb == 0 else []),
                        inc=(s_mm if bb == 3 else None))
                pts_mm[0] = s_mm.n
                wad = [(s_mm, s_mm.n)]
                if h == 0:
                    wad.append((s_at, asp_prev[0]))
                P.emit("vector", lambda e, po=po, h=h: e.tensor_copy(
                    atile[:, h * 256:(h + 1) * 256], po[:, 0:256]),
                    waits=wad, inc=s_ve)
                ao_drain[h % 2] = s_ve.n
            win_used[ci] = s_mm.n
            P.emit("sync", lambda e, ci=ci: e.dma_start(
                asp[:, :, ci * 256:(ci + 1) * 256]
                .rearrange("m p t -> p m t"),
                atile[:, :].rearrange("p (m t) -> p m t", t=256)),
                waits=[(s_ve, s_ve.n)], inc=s_at, inck=16)
            asp_prev[0] = s_at.n
        quiesce()
        P.run_block(nc)
    if upto <= 3:
        pX.close(); es.close()
        return nc

    pArena = contextlib.ExitStack()
    arena_t = pArena.enter_context(nc.sbuf_tensor("arena", [128, 8 * NH], FP16, side="right"))
    p4 = contextlib.ExitStack()
    with p4:
        def p4sb(name, shape, dt):
            return p4.enter_context(nc.sbuf_tensor(name, shape, dt))
        selscr = p4sb("selscr", [128, 4096], FP16)
        gbuf = p4sb("gbuf", [128, 2048], BF16)
        stg = [p4sb(f"stg{i}", [128, 2048], F32) for i in range(2)]
        tern = [p4sb(f"tern{i}", [128, 16 * 256], BF16) for i in range(2)]
        bsb = p4sb("bsb", [1, 256], BF16)
        bsb2 = p4sb("bsb2", [1, 256], BF16)
        pz = [p4.enter_context(nc.psum_tensor(f"pz{i}", [128, 512], F32))
              for i in range(4)]
        phh = [p4.enter_context(nc.psum_tensor(f"phh{i}", [128, 1024], FP16))
               for i in range(2)]
        pm = p4.enter_context(nc.psum_tensor("pm", [1, 4], F32))
        pm2 = p4.enter_context(nc.psum_tensor("pm2", [128, 8], F32))

        P.emit("tensor", lambda e: e.matmul(pm[:, 0:2], stats[:, 10:11],
                                            macc[:, :], start=True, stop=True),
               inc=s_mm)
        P.emit("vector", lambda e: e.tensor_copy(scal[0:1, 0:2], pm[:, 0:2]),
               waits=[(s_mm, s_mm.n)])
        P.dr("vector")
        P.emit("vector", lambda e: e.tensor_scalar(
            scal[0:1, 2:4], scal[0:1, 0:2], 1.0 / (NH * C), None, op0=op.mult))
        P.dr("vector")
        P.emit("vector", lambda e: e.tensor_scalar(
            scal[0:1, 4:6], scal[0:1, 2:4], 1e-8, 0.5, op0=op.add, op1=op.mult))
        P.emit("vector", lambda e: e.tensor_scalar(
            scal[0:1, 6:8], scal[0:1, 2:4], -1.0, None, op0=op.mult), inc=s_ve)
        P.dr("vector")
        P.emit("tensor", lambda e: e.matmul(
            pm2[:, 0:6], onesf[:, :], scal[0:1, 2:8], start=True, stop=True),
            waits=[(s_ve, s_ve.n)], inc=s_mm)
        P.emit("vector", lambda e: e.tensor_copy(scal[:, 2:8], pm2[:, 0:6]),
               waits=[(s_mm, s_mm.n)], inc=s_ve)
        P.emit("gpsimd", lambda e: e.memset(stats[:, 31:32], 0.0),
               waits=[(s_ve, s_ve.n)], inc=s_gp)
        P.emit("vector", lambda e: e.tensor_scalar(
            stats[:, 24:25], scal[:, 4:5], -1.0, None, op0=op.mult),
            waits=[(s_gp, s_gp.n)])
        P.emit("vector", lambda e: e.tensor_scalar(
            stats[:, 26:27], scal[:, 5:6], -1.0, None, op0=op.mult))
        P.emit("vector", lambda e: e.reciprocal(stats[:, 25:26], scal[:, 6:7]))
        P.emit("vector", lambda e: e.reciprocal(stats[:, 27:28], scal[:, 7:8]),
               inc=s_ve)
        mean_ready = s_ve.n

        q_done = {}
        stg_used = {}
        prev_stt = [0]
        bias_prev = [0]
        tap_done = [0]
        z_drained = {}
        sel_done = {}
        pz_drain = {}
        sig_ready = 0

        def zq(t, q):
            return arena_t[:, t * NH + q * 2048: t * NH + (q + 1) * 2048]

        def shalf(i):
            return selscr[:, (i % 2) * 2048:(i % 2) * 2048 + 2048]

        def emit_sel_all():
            if "zraw" in taps:
                for t in range(8):
                    P.emit("sync", lambda e, t=t: e.dma_start(
                        taps["zraw"][t, :, :],
                        arena_t[:, t * NH:(t + 1) * NH]),
                        waits=[(s_ac, z_drained[7])], inc=s_ds, inck=16)
                tap_done[0] = s_ds.n
            wz = [(s_ac, z_drained[7]), (s_ds, tap_done[0])]
            CQ, S01, S23, CNT, GE, LO, HI, MID, MU = 64, 96, 104, 112, 16, 32, 40, 48, 56
            first = True
            for t in range(8):
                for q in range(4):
                    c = CQ + 4 * t + q
                    if (4 * t + q) % 2 == 0 and not first:
                        P.dr("vector")
                    P.emit("vector", lambda e, t=t, q=q, c=c: e.tensor_scalar(
                        shalf(4 * t + q), zq(t, q), 0.0, None, op0=op.add,
                        op1=op.add, accum_out=stats[:, c:c + 1]),
                        waits=(wz if first else []))
                    first = False
            P.dr("vector")
            for t in range(8):
                P.emit("vector", lambda e, t=t: e.tensor_tensor(
                    stats[:, S01 + t:S01 + t + 1], stats[:, CQ + 4 * t:CQ + 4 * t + 1],
                    stats[:, CQ + 4 * t + 1:CQ + 4 * t + 2], op=op.add))
            for t in range(8):
                P.emit("vector", lambda e, t=t: e.tensor_tensor(
                    stats[:, S23 + t:S23 + t + 1],
                    stats[:, CQ + 4 * t + 2:CQ + 4 * t + 3],
                    stats[:, CQ + 4 * t + 3:CQ + 4 * t + 4], op=op.add))
            P.dr("vector")
            for t in range(8):
                P.emit("vector", lambda e, t=t: e.tensor_tensor(
                    stats[:, CNT + t:CNT + t + 1], stats[:, S01 + t:S01 + t + 1],
                    stats[:, S23 + t:S23 + t + 1], op=op.add))
            P.dr("vector")
            for t in range(8):
                P.emit("vector", lambda e, t=t: e.tensor_scalar(
                    stats[:, MU + t:MU + t + 1], stats[:, CNT + t:CNT + t + 1],
                    1.0 / NH, None, op0=op.mult),
                    inc=(s_ve if t == 7 else None))
            P.dr("vector")
            P.dr("vector")
            for q in range(4):
                if q == 2:
                    P.dr("scalar")
                P.emit("scalar", lambda e, q=q: e.activation(
                    shalf(q), zq(0, q), AF.Square,
                    accum_out=stats[:, CQ + q:CQ + q + 1]),
                    waits=([(s_ve, s_ve.n)] if q == 0 else []),
                    inc=(s_ac if q == 3 else None))
            P.emit("vector", lambda e: e.tensor_tensor(
                stats[:, 0:1], stats[:, CQ:CQ + 1], stats[:, CQ + 1:CQ + 2],
                op=op.add), waits=[(s_ac, s_ac.n)])
            P.emit("vector", lambda e: e.tensor_tensor(
                stats[:, 1:2], stats[:, CQ + 2:CQ + 3], stats[:, CQ + 3:CQ + 4],
                op=op.add))
            P.emit("vector", lambda e: e.tensor_tensor(
                stats[:, 3:4], stats[:, MU:MU + 1], stats[:, MU:MU + 1],
                op=op.mult))
            P.dr("vector")
            P.emit("vector", lambda e: e.tensor_tensor(
                stats[:, 2:3], stats[:, 0:1], stats[:, 1:2], op=op.add))
            P.dr("vector")
            P.emit("vector", lambda e: e.tensor_scalar(
                stats[:, 4:5], stats[:, 2:3], 1.0 / NH, None, op0=op.mult))
            P.dr("vector")
            P.emit("vector", lambda e: e.tensor_tensor(
                stats[:, 5:6], stats[:, 4:5], stats[:, 3:4], op=op.subtract),
                inc=s_ve)
            P.emit("tensor", lambda e: e.matmul(
                pm[:, 2:3], stats[:, 10:11], stats[:, 5:6],
                start=True, stop=True), waits=[(s_ve, s_ve.n)], inc=s_mm)
            P.emit("vector", lambda e: e.tensor_scalar(
                scal[0:1, 0:1], pm[:, 2:3], 1.0 / 128, None, op0=op.mult),
                waits=[(s_mm, s_mm.n)], inc=s_ve)
            P.emit("scalar", lambda e: e.activation(
                scal[0:1, 1:2], scal[0:1, 0:1], AF.Sqrt),
                waits=[(s_ve, s_ve.n)], inc=s_ac)
            P.emit("tensor", lambda e: e.matmul(
                pm2[:, 0:2], onesf[:, :], scal[0:1, 0:2], start=True, stop=True),
                waits=[(s_ac, s_ac.n)], inc=s_mm)
            P.emit("vector", lambda e: e.tensor_copy(scal[:, 0:2], pm2[:, 0:2]),
                   waits=[(s_mm, s_mm.n)], inc=s_ve)
            P.emit("gpsimd", lambda e: e.memset(stats[:, 30:31], 0.0),
                   waits=[(s_ve, s_ve.n)], inc=s_gp)
            sig_ready = s_gp.n
            for t in range(8):
                P.emit("vector", lambda e, t=t: e.scalar_tensor_tensor(
                    stats[:, LO + t:LO + t + 1], scal[:, 1:2], 1.0,
                    stats[:, MU + t:MU + t + 1], op0=op.mult, op1=op.add),
                    waits=([(s_gp, sig_ready)] if t == 0 else []))
            for t in range(8):
                P.emit("vector", lambda e, t=t: e.scalar_tensor_tensor(
                    stats[:, HI + t:HI + t + 1], scal[:, 1:2], 1.6,
                    stats[:, MU + t:MU + t + 1], op0=op.mult, op1=op.add))
            for it in range(NSEL + 1):
                last = (it == NSEL)
                P.dr("vector")
                if not last:
                    for t in range(8):
                        P.emit("vector", lambda e, t=t: e.tensor_tensor(
                            stats[:, MID + t:MID + t + 1],
                            stats[:, LO + t:LO + t + 1],
                            stats[:, HI + t:HI + t + 1], op=op.add))
                    P.dr("vector")
                    for t in range(8):
                        P.emit("vector", lambda e, t=t: e.tensor_scalar(
                            stats[:, MID + t:MID + t + 1],
                            stats[:, MID + t:MID + t + 1], 0.5, None,
                            op0=op.mult))
                    P.dr("vector")
                pc = MID if not last else LO
                for t in range(8):
                    for q in range(4):
                        c = CQ + 4 * t + q
                        if (4 * t + q) % 2 == 0:
                            P.dr("vector")
                        P.emit("vector", lambda e, t=t, q=q, c=c, pc=pc:
                               e.tensor_scalar(
                                   shalf(4 * t + q), zq(t, q),
                                   stats[:, pc + t:pc + t + 1], None,
                                   op0=op.is_ge, op1=op.add,
                                   accum_out=stats[:, c:c + 1]))
                P.dr("vector")
                for t in range(8):
                    P.emit("vector", lambda e, t=t: e.tensor_tensor(
                        stats[:, S01 + t:S01 + t + 1],
                        stats[:, CQ + 4 * t:CQ + 4 * t + 1],
                        stats[:, CQ + 4 * t + 1:CQ + 4 * t + 2], op=op.add))
                for t in range(8):
                    P.emit("vector", lambda e, t=t: e.tensor_tensor(
                        stats[:, S23 + t:S23 + t + 1],
                        stats[:, CQ + 4 * t + 2:CQ + 4 * t + 3],
                        stats[:, CQ + 4 * t + 3:CQ + 4 * t + 4], op=op.add))
                P.dr("vector")
                for t in range(8):
                    P.emit("vector", lambda e, t=t: e.tensor_tensor(
                        stats[:, CNT + t:CNT + t + 1],
                        stats[:, S01 + t:S01 + t + 1],
                        stats[:, S23 + t:S23 + t + 1], op=op.add),
                        inc=(s_ve if (last and t == 7) else None))
                P.dr("vector")
                if last and "seldbg2" in taps:
                    P.emit("sync", lambda e: e.dma_start(
                        taps["seldbg2"][:, :], stats[:, :]),
                        waits=[(s_ve, s_ve.n)], inc=s_ds, inck=16)
                    tap2 = s_ds.n
                    P.emit("vector", lambda e: e.tensor_copy(
                        stats[:, 30:31], stats[:, 10:11]),
                        waits=[(s_ds, tap2)])
                if not last:
                    for t in range(8):
                        P.emit("vector", lambda e, t=t: e.tensor_scalar(
                            stats[:, GE + t:GE + t + 1],
                            stats[:, CNT + t:CNT + t + 1], 818.5, None,
                            op0=op.is_gt))
                    for t in range(8):
                        P.emit("vector", lambda e, t=t: e.tensor_scalar(
                            stats[:, MU + t:MU + t + 1],
                            stats[:, CNT + t:CNT + t + 1], 818.5, None,
                            op0=op.is_le))
                    P.dr("vector")
                    for t in range(8):
                        P.emit("vector", lambda e, t=t: e.tensor_tensor(
                            stats[:, S01 + t:S01 + t + 1],
                            stats[:, MID + t:MID + t + 1],
                            stats[:, LO + t:LO + t + 1], op=op.subtract))
                    for t in range(8):
                        P.emit("vector", lambda e, t=t: e.tensor_tensor(
                            stats[:, S23 + t:S23 + t + 1],
                            stats[:, MID + t:MID + t + 1],
                            stats[:, HI + t:HI + t + 1], op=op.subtract))
                    P.dr("vector")
                    for t in range(8):
                        P.emit("vector", lambda e, t=t: e.tensor_tensor(
                            stats[:, S01 + t:S01 + t + 1],
                            stats[:, S01 + t:S01 + t + 1],
                            stats[:, GE + t:GE + t + 1], op=op.mult))
                    for t in range(8):
                        P.emit("vector", lambda e, t=t: e.tensor_tensor(
                            stats[:, S23 + t:S23 + t + 1],
                            stats[:, S23 + t:S23 + t + 1],
                            stats[:, MU + t:MU + t + 1], op=op.mult))
                    P.dr("vector")
                    for t in range(8):
                        P.emit("vector", lambda e, t=t: e.tensor_tensor(
                            stats[:, LO + t:LO + t + 1],
                            stats[:, LO + t:LO + t + 1],
                            stats[:, S01 + t:S01 + t + 1], op=op.add))
                    for t in range(8):
                        P.emit("vector", lambda e, t=t: e.tensor_tensor(
                            stats[:, HI + t:HI + t + 1],
                            stats[:, HI + t:HI + t + 1],
                            stats[:, S23 + t:S23 + t + 1], op=op.add))
            for t in range(8):
                for qp in range(2):
                    q0, q1 = 2 * qp, 2 * qp + 1
                    P.dr("vector")
                    P.emit("vector", lambda e, t=t, q=q0: e.tensor_scalar(
                        shalf(q), zq(t, q), stats[:, LO + t:LO + t + 1], None,
                        op0=op.is_lt))
                    P.emit("vector", lambda e, t=t, q=q1: e.tensor_scalar(
                        shalf(q), zq(t, q), stats[:, LO + t:LO + t + 1], None,
                        op0=op.is_lt))
                    P.dr("vector")
                    P.emit("vector", lambda e, t=t, q=q0: e.scalar_tensor_tensor(
                        shalf(q), shalf(q), BIGH, zq(t, q),
                        op0=op.mult, op1=op.add))
                    P.emit("vector", lambda e, t=t, q=q1: e.scalar_tensor_tensor(
                        shalf(q), shalf(q), BIGH, zq(t, q),
                        op0=op.mult, op1=op.add))
                    P.dr("vector")
                    P.emit("vector", lambda e, t=t, q=q0: e.tensor_scalar(
                        shalf(q), shalf(q), 0.0, None, op0=op.add, op1=op.min,
                        accum_out=stats[:, CQ + 4 * t + q0:CQ + 4 * t + q0 + 1]))
                    P.emit("vector", lambda e, t=t, q=q1: e.tensor_scalar(
                        shalf(q), shalf(q), 0.0, None, op0=op.add, op1=op.min,
                        accum_out=stats[:, CQ + 4 * t + q1:CQ + 4 * t + q1 + 1]))
            P.dr("vector")
            for t in range(8):
                P.emit("vector", lambda e, t=t: e.tensor_tensor(
                    stats[:, S01 + t:S01 + t + 1],
                    stats[:, CQ + 4 * t:CQ + 4 * t + 1],
                    stats[:, CQ + 4 * t + 1:CQ + 4 * t + 2], op=op.min))
            for t in range(8):
                P.emit("vector", lambda e, t=t: e.tensor_tensor(
                    stats[:, S23 + t:S23 + t + 1],
                    stats[:, CQ + 4 * t + 2:CQ + 4 * t + 3],
                    stats[:, CQ + 4 * t + 3:CQ + 4 * t + 4], op=op.min))
            P.dr("vector")
            for t in range(8):
                P.emit("vector", lambda e, t=t: e.tensor_tensor(
                    stats[:, GE + t:GE + t + 1], stats[:, S01 + t:S01 + t + 1],
                    stats[:, S23 + t:S23 + t + 1], op=op.min))
            P.dr("vector")
            for t in range(8):
                P.emit("vector", lambda e, t=t: e.tensor_scalar(
                    stats[:, S23 + t:S23 + t + 1],
                    stats[:, CNT + t:CNT + t + 1], 819.5, None, op0=op.is_gt))
            for t in range(8):
                P.emit("vector", lambda e, t=t: e.tensor_scalar(
                    stats[:, MID + t:MID + t + 1],
                    stats[:, LO + t:LO + t + 1], -1e-12, None, op0=op.add))
            P.dr("vector")
            for t in range(8):
                P.emit("vector", lambda e, t=t: e.tensor_tensor(
                    stats[:, S01 + t:S01 + t + 1],
                    stats[:, GE + t:GE + t + 1],
                    stats[:, MID + t:MID + t + 1], op=op.subtract))
            P.dr("vector")
            for t in range(8):
                P.emit("vector", lambda e, t=t: e.tensor_tensor(
                    stats[:, S01 + t:S01 + t + 1],
                    stats[:, S01 + t:S01 + t + 1],
                    stats[:, S23 + t:S23 + t + 1], op=op.mult))
            P.dr("vector")
            for t in range(8):
                P.emit("vector", lambda e, t=t: e.tensor_tensor(
                    stats[:, CNT + t:CNT + t + 1],
                    stats[:, MID + t:MID + t + 1],
                    stats[:, S01 + t:S01 + t + 1], op=op.add))
            P.dr("vector")
            for t in range(8):
                for qp in range(2):
                    q0, q1 = 2 * qp, 2 * qp + 1
                    P.emit("vector", lambda e, t=t, q=q0: e.tensor_scalar(
                        shalf(q), zq(t, q), stats[:, CNT + t:CNT + t + 1],
                        None, op0=op.is_gt))
                    P.emit("vector", lambda e, t=t, q=q1: e.tensor_scalar(
                        shalf(q), zq(t, q), stats[:, CNT + t:CNT + t + 1],
                        None, op0=op.is_gt))
                    P.dr("vector")
                    P.emit("vector", lambda e, t=t, q=q0: e.tensor_tensor(
                        zq(t, q), zq(t, q), shalf(q), op=op.mult))
                    P.emit("vector", lambda e, t=t, q=q1: e.tensor_tensor(
                        zq(t, q), zq(t, q), shalf(q), op=op.mult),
                        inc=(s_ve if (t, qp) == (7, 1) else None))
                    P.dr("vector")
            for t in range(8):
                sel_done[t] = s_ve.n
            if "seldbg" in taps:
                P.emit("sync", lambda e: e.dma_start(
                    taps["seldbg"][0:128, :], stats[:, :]),
                    waits=[(s_ve, sel_done[7])], inc=s_ds, inck=16)
                P.emit("sync", lambda e: e.dma_start(
                    taps["seldbg"][128:129, 0:8], scal[0:1, :]),
                    inc=s_ds, inck=16)
                tap_done[0] = s_ds.n
            if "zdbg" in taps:
                for t in range(8):
                    P.emit("sync", lambda e, t=t: e.dma_start(
                        taps["zdbg"][t, :, :],
                        arena_t[:, t * NH:(t + 1) * NH]),
                        waits=[(s_ve, sel_done[t])], inc=s_ds, inck=16)
                tap_done[0] = s_ds.n

        for ch in range(32):
            tn = tern[ch % 2]
            for p2i in range(2):
                pi = ch * 2 + p2i
                sg = stg[pi % 2]
                w = [stg_used[pi - 2]] if (pi - 2) in stg_used else []
                P.emit("sync", lambda e, sg=sg, pi=pi: e.dma_start(
                    sg[:, :].rearrange("p (k o) -> p k o", o=128),
                    wsT[pi].rearrange("(k p) o -> p k o", p=128)),
                    waits=w, inc=s_sg[pi % 2], inck=16)
                wq = [(s_sg[pi % 2], s_sg[pi % 2].n), (s_ve, mean_ready)]
                if p2i == 0:
                    wq.append((s_mm, q_done.get(ch - 2, 0)))

                def tslice(tn=tn, p2i=p2i):
                    return tn[:, :].rearrange("p (k o) -> p k o", o=256)[
                        :, :, p2i * 128:(p2i + 1) * 128]
                P.emit("vector", lambda e, sg=sg: e.tensor_scalar(
                    gbuf[:, :], sg[:, :], scal[:, 4:5], None, op0=op.is_gt),
                    waits=wq, inc=s_ve)
                P.dr("vector")
                P.emit("vector", lambda e, sg=sg, tn=tn, p2i=p2i:
                       e.scalar_tensor_tensor(
                           tslice(tn, p2i), sg[:, :], stats[:, 24:25],
                           gbuf[:, :], op0=op.is_lt, op1=op.subtract),
                       inc=s_ve)
                P.dr("vector")
                stg_used[pi] = (s_ve, s_ve.n)
                prev_stt[0] = s_ve.n
            tern_ready = s_ve.n
            P.emit("gpsimd", lambda e, ch=ch: e.dma_start(
                bsb[:, :], bsynt[:, ch * 256:(ch + 1) * 256]),
                waits=[(s_ve, bias_prev[0])], inc=s_bs, inck=16)
            P.emit("vector", lambda e: e.tensor_scalar(
                bsb2[:, :], bsb[:, :], stats[0:1, 25:26], None, op0=op.mult),
                waits=[(s_bs, s_bs.n), (s_ve, mean_ready),
                       (s_mm, q_done.get(ch - 1, 0))], inc=s_ve)
            bias_ready = s_ve.n
            bias_prev[0] = s_ve.n
            for tile in range(8):
                pzi = (ch * 8 + tile) % 4
                wfirst = [(s_ve, tern_ready), (s_ve, bias_ready),
                          (s_ac, pz_drain.get(pzi, 0))]
                for k in range(16):
                    P.emit("tensor", lambda e, pzi=pzi, tn=tn, k=k, tile=tile:
                           e.matmul(pz[pzi][:, 0:256],
                                    xnTc[:, k * NTOK + tile * 128:
                                         k * NTOK + (tile + 1) * 128],
                                    tn[:, k * 256:(k + 1) * 256],
                                    start=(k == 0), stop=False),
                           waits=(wfirst if k == 0 else []))
                P.emit("tensor", lambda e, pzi=pzi: e.matmul(
                    pz[pzi][:, 0:256], onesb[0:1, 0:128], bsb2[:, :],
                    start=False, stop=True), inc=s_mm)
                P.emit("scalar", lambda e, pzi=pzi, tile=tile, ch=ch:
                       e.activation(
                           arena_t[:, tile * NH + ch * 256:
                                   tile * NH + (ch + 1) * 256],
                           pz[pzi][:, 0:256], AF.Identity, scale=scal[:, 6:7]),
                       waits=[(s_mm, s_mm.n)], inc=s_ac)
                pz_drain[pzi] = s_ac.n
                if ch == 31:
                    z_drained[tile] = s_ac.n
            q_done[ch] = s_mm.n
        emit_sel_all()
        hh_drain = {}
        for tile in range(8):
            for g in range(8):
                ph = phh[g % 2]
                wfirst = [(s_ve, sel_done[tile]),
                          (s_ve, hh_drain.get(g % 2, 0)),
                          (s_ds, tap_done[0])]
                for kk in range(8):
                    k = g * 8 + kk
                    P.emit("tensor", lambda e, ph=ph, tile=tile, k=k, kk=kk:
                           e.transpose(
                               ph[:, kk * 128:(kk + 1) * 128],
                               arena_t[:, tile * NH + k * 128:
                                       tile * NH + (k + 1) * 128],
                               identh[:, :]),
                           waits=(wfirst if kk == 0 else []),
                           inc=(s_tt if kk == 7 else None))
                P.emit("vector", lambda e, ph=ph, tile=tile, g=g: e.tensor_copy(
                    arena_t[:, tile * NH + g * 1024:tile * NH + (g + 1) * 1024],
                    ph[:, :]), waits=[(s_tt, s_tt.n)], inc=s_ve)
                hh_drain[g % 2] = s_ve.n
        quiesce()
        P.run_block(nc)
    pX.close()
    if upto <= 4:
        pArena.close(); es.close()
        return nc

    if True:
        with contextlib.ExitStack() as p5:
            def p5sb(name, shape, dt):
                return p5.enter_context(nc.sbuf_tensor(name, shape, dt))
            stg5 = [p5sb(f"stg5{i}", [128, 2048], F32) for i in range(2)]
            gb5 = p5sb("gb5", [128, 2048], FP16)
            ternD = [p5sb(f"ternD{i}", [128, 8192], FP16) for i in range(2)]
            bob = p5sb("bob", [1, 128], FP16)
            bob2 = p5sb("bob2", [1, 128], FP16)
            obuf = [p5sb(f"obuf{i}", [128, 1024], BF16) for i in range(2)]
            po5 = [p5.enter_context(nc.psum_tensor(f"po5{i}", [128, 512], F32))
                   for i in range(4)]

            stg_used5 = {}
            tern_used5 = {}
            po_drain5 = {}
            ob_used = {}
            prev_stt5 = [0]
            b5_prev = [0]
            for d in range(16):
                tn = ternD[d % 2]
                wtn = [(s_mm, tern_used5.get(d - 2, 0))]
                for q4 in range(4):
                    pi = d * 4 + q4
                    sg = stg5[pi % 2]
                    w = [stg_used5[pi - 2]] if (pi - 2) in stg_used5 else []
                    P.emit("sync", lambda e, sg=sg, d=d, q4=q4: e.dma_start(
                        sg[:, :].rearrange("p (k o) -> p k o", o=128),
                        woutT[d, q4 * 2048:(q4 + 1) * 2048, :]
                        .rearrange("(k p) o -> p k o", p=128)),
                        waits=w, inc=s_s5[pi % 2], inck=16)
                    P.emit("vector", lambda e, sg=sg: e.tensor_scalar(
                        gb5[:, :], sg[:, :], scal[:, 5:6], None, op0=op.is_gt),
                        waits=[(s_s5[pi % 2], s_s5[pi % 2].n)]
                        + (wtn if q4 == 0 else []), inc=s_ve)
                    P.dr("vector")
                    P.emit("vector", lambda e, sg=sg, tn=tn, q4=q4:
                           e.scalar_tensor_tensor(
                               tn[:, q4 * 2048:(q4 + 1) * 2048], sg[:, :],
                               stats[:, 26:27], gb5[:, :],
                               op0=op.is_lt, op1=op.subtract),
                           inc=s_ve)
                    P.dr("vector")
                    stg_used5[pi] = (s_ve, s_ve.n)
                t5_ready = s_ve.n
                P.emit("gpsimd", lambda e, d=d: e.dma_start(
                    bob[:, :], boutt[:, d * 128:(d + 1) * 128]),
                    waits=[(s_ve, b5_prev[0])], inc=s_bo, inck=16)
                P.emit("vector", lambda e: e.tensor_scalar(
                    bob2[:, :], bob[:, :], stats[0:1, 27:28], None, op0=op.mult),
                    waits=[(s_bo, s_bo.n), (s_mm, tern_used5.get(d - 1, 0))],
                    inc=s_ve)
                b5_ready = s_ve.n
                b5_prev[0] = s_ve.n
                ob = obuf[d % 2]
                for n in range(2):
                    pzi = (d * 2 + n) % 4
                    wfirst = [(s_ve, t5_ready), (s_ve, b5_ready),
                              (s_ve, po_drain5.get(pzi, 0))]
                    for j in range(64):
                        def rhs(e, j=j, n=n):
                            return arena_t[:, :].rearrange(
                                "p (i f) -> p i f", f=NH)[
                                :, 4 * n:4 * n + 4, j * 128:(j + 1) * 128]
                        P.emit("tensor", lambda e, pzi=pzi, tn=tn, j=j, n=n:
                               e.matmul(po5[pzi][:, :],
                                        tn[:, j * 128:(j + 1) * 128],
                                        arena_t[:, :].rearrange(
                                            "p (i f) -> p i f", f=NH)[
                                            :, 4 * n:4 * n + 4,
                                            j * 128:(j + 1) * 128],
                                        start=(j == 0), stop=False),
                               waits=(wfirst if j == 0 else []))
                    P.emit("tensor", lambda e, pzi=pzi: e.matmul(
                        po5[pzi][:, :], bob2[:, :], onesh[0:1, 0:512],
                        start=False, stop=True), inc=s_mm)
                    P.emit("vector", lambda e, pzi=pzi, ob=ob, n=n:
                           e.tensor_scalar(
                               ob[:, n * 512:(n + 1) * 512], po5[pzi][:, :],
                               scal[:, 7:8], None, op0=op.mult),
                           waits=[(s_mm, s_mm.n)]
                           + ([ob_used[d - 2]]
                              if (n == 0 and (d - 2) in ob_used) else []),
                           inc=s_ve)
                    po_drain5[pzi] = s_ve.n
                tern_used5[d] = s_mm.n
                P.emit("sync", lambda e, ob=ob, d=d: e.dma_start(
                    osp[d * 128:(d + 1) * 128, :], ob[:, :]),
                    waits=[(s_ve, s_ve.n)], inc=s_os[d % 2], inck=16)
                ob_used[d] = (s_os[d % 2], s_os[d % 2].n)
            quiesce()
        P.run_block(nc)
    pArena.close()

    with contextlib.ExitStack() as p6:
        def p6sb(name, shape, dt):
            return p6.enter_context(nc.sbuf_tensor(name, shape, dt))
        wob = p6sb("wob", [128, 16 * 2048], BF16)
        gbb = p6sb("gbb", [128, 2048], F32)
        beb = p6sb("beb", [128, 2048], F32)
        xnl = [p6sb(f"xnl{i}", [128, 2048], BF16) for i in range(2)]
        onl = [p6sb(f"onl{i}", [128, 2048], BF16) for i in range(2)]
        anl = [p6sb(f"anl{i}", [128, 2048], BF16) for i in range(2)]
        xl = [p6sb(f"xl{i}", [128, 2048], F32) for i in range(2)]
        ybuf = p6sb("ybuf", [128, 2048], BF16)
        lnf = p6sb("lnf", [128, 2048], F32)
        t2buf = p6sb("t2buf", [128, 2048], F32)
        fout = [p6sb(f"fout{i}", [128, 2048], F32) for i in range(2)]
        trash6 = p6sb("trash6", [128, 4096], FP16)
        pa6 = [p6.enter_context(nc.psum_tensor(f"pa6{i}", [128, 512], F32))
               for i in range(4)]

        P.emit("sync", lambda e: e.dma_start(gbb[0:1, :], gammat[:, :]),
               inc=s_gb2, inck=16)
        P.emit("sync", lambda e: e.dma_start(beb[0:1, :], betat[:, :]),
               inc=s_gb2, inck=16)
        for n in range(4):
            P.emit("tensor", lambda e, n=n: e.matmul(
                pa6[n][:, :], onesf[:, :], gbb[0:1, n * 512:(n + 1) * 512],
                start=True, stop=True),
                waits=([(s_gb2, s_gb2.n)] if n == 0 else []),
                inc=(s_mm if n == 3 else None))
        for n in range(4):
            P.emit("vector", lambda e, n=n: e.tensor_copy(
                gbb[:, n * 512:(n + 1) * 512], pa6[n][:, :]),
                waits=([(s_mm, s_mm.n)] if n == 0 else []),
                inc=(s_ve if n == 3 else None))
        gbb_ve = s_ve.n
        for n in range(4):
            P.emit("tensor", lambda e, n=n: e.matmul(
                pa6[n][:, :], onesf[:, :], beb[0:1, n * 512:(n + 1) * 512],
                start=True, stop=True),
                waits=([(s_ve, gbb_ve)] if n == 0 else []),
                inc=(s_mm if n == 3 else None))
        for n in range(4):
            P.emit("vector", lambda e, n=n: e.tensor_copy(
                beb[:, n * 512:(n + 1) * 512], pa6[n][:, :]),
                waits=([(s_mm, s_mm.n)] if n == 0 else []),
                inc=(s_ve if n == 3 else None))
        P.emit("gpsimd", lambda e: e.memset(stats[:, 15:16], 0.0),
               waits=[(s_ve, s_ve.n)], inc=s_gp)
        gb_ready = s_gp.n
        for k in range(16):
            P.emit("gpsimd", lambda e, k=k: e.dma_start(
                wob[:, k * 2048:(k + 1) * 2048], woT[k * 128:(k + 1) * 128, :]),
                inc=s_wo, inck=16)
        wo_ready = s_wo.n

        buf_free = {}
        out_done = {}
        for i in range(8):
            xnb6, onb, anb, xb6, fo = (xnl[i % 2], onl[i % 2], anl[i % 2],
                                       xl[i % 2], fout[i % 2])
            w = [(s_ve, buf_free.get(i - 2, 0))]
            if i == 0:
                w += [(s_sp[0], s_sp[0].n), (s_sp[1], s_sp[1].n),
                      (s_os[0], s_os[0].n), (s_os[1], s_os[1].n),
                      (s_at, s_at.n)]
            P.emit("sync", lambda e, xnb6=xnb6, i=i: e.dma_start(
                xnb6[:, :], xnsp[i, :, :]), waits=w, inc=s_l[i % 2], inck=16)
            P.emit("sync", lambda e, onb=onb, i=i: e.dma_start(
                onb[:, :], osp[:, i * 128:(i + 1) * 128], transpose=True),
                inc=s_l[i % 2], inck=16)
            P.emit("sync", lambda e, anb=anb, i=i: e.dma_start(
                anb[:, :].rearrange("p (m t) -> p m t", t=128),
                asp[:, :, i * 128:(i + 1) * 128]
                .rearrange("m p t -> p m t")), inc=s_l[i % 2], inck=16)
            P.emit("scalar", lambda e, xb6=xb6, i=i: e.dma_start(
                xb6[:, :], xs[HALO + i * 128:HALO + (i + 1) * 128, :]),
                inc=s_l[i % 2], inck=16)
            dr6 = (s_l[i % 2], s_l[i % 2].n)
            for n in range(4):
                pzi = n
                wfirst = [dr6, (s_wo, wo_ready),
                          (s_ve, buf_free.get(i - 1, 0))]
                for k in range(16):
                    P.emit("tensor", lambda e, pzi=pzi, anb=anb, k=k, n=n:
                           e.matmul(pa6[pzi][:, :],
                                    anb[:, k * 128:(k + 1) * 128],
                                    wob[:, k * 2048 + n * 512:
                                        k * 2048 + (n + 1) * 512],
                                    start=(k == 0), stop=(k == 15)),
                           waits=(wfirst if (k == 0 and n == 0) else []),
                           inc=(s_mm if (k == 15 and n == 3) else None))
            mm_here = s_mm.n
            HF = C // 2
            P.emit("vector", lambda e, xnb6=xnb6, onb=onb: e.tensor_tensor(
                ybuf[:, 0:HF], xnb6[:, 0:HF], onb[:, 0:HF], op=op.add),
                waits=[dr6])
            P.emit("vector", lambda e, xnb6=xnb6, onb=onb: e.tensor_tensor(
                ybuf[:, HF:C], xnb6[:, HF:C], onb[:, HF:C], op=op.add))
            P.dr("vector")
            P.emit("vector", lambda e: e.tensor_scalar(
                trash6[:, 0:HF], ybuf[:, 0:HF], 0.0, None, op0=op.add,
                op1=op.add, accum_out=stats[:, 0:1]))
            P.emit("vector", lambda e: e.tensor_scalar(
                trash6[:, HF:C], ybuf[:, HF:C], 0.0, None, op0=op.add,
                op1=op.add, accum_out=stats[:, 10:11]))
            P.dr("vector")
            P.emit("vector", lambda e: e.tensor_copy(
                stats[:, 30:31], stats[:, 10:11]), inc=s_ve)
            P.emit("scalar", lambda e: e.activation(
                trash6[:, C:C + HF], ybuf[:, 0:HF], AF.Square,
                accum_out=stats[:, 1:2]), waits=[(s_ve, s_ve.n)])
            P.emit("scalar", lambda e: e.activation(
                trash6[:, C + HF:2 * C], ybuf[:, HF:C], AF.Square,
                accum_out=stats[:, 11:12]), inc=s_ac)
            P.dr("vector")
            P.emit("vector", lambda e: e.tensor_tensor(
                stats[:, 12:13], stats[:, 0:1], stats[:, 10:11], op=op.add),
                waits=[(s_ac, s_ac.n)])
            P.emit("vector", lambda e: e.tensor_tensor(
                stats[:, 13:14], stats[:, 1:2], stats[:, 11:12], op=op.add))
            P.dr("vector")
            P.emit("vector", lambda e: e.tensor_scalar(
                stats[:, 2:3], stats[:, 12:13], 1.0 / C, None, op0=op.mult))
            P.emit("vector", lambda e: e.tensor_scalar(
                stats[:, 3:4], stats[:, 13:14], 1.0 / C, None, op0=op.mult))
            P.dr("vector")
            P.emit("vector", lambda e: e.tensor_scalar(
                stats[:, 28:29], stats[:, 2:3], -1.0, None, op0=op.mult))
            P.emit("vector", lambda e: e.tensor_tensor(
                stats[:, 4:5], stats[:, 2:3], stats[:, 2:3], op=op.mult))
            P.dr("vector")
            P.emit("vector", lambda e: e.tensor_tensor(
                stats[:, 5:6], stats[:, 3:4], stats[:, 4:5], op=op.subtract),
                inc=s_ve)
            P.emit("scalar", lambda e: e.activation(
                stats[:, 6:7], stats[:, 5:6], AF.Sqrt, bias=cstb[:, 1:2]),
                waits=[(s_ve, s_ve.n)], inc=s_ac)
            P.emit("vector", lambda e: e.reciprocal(stats[:, 7:8], stats[:, 6:7]),
                   waits=[(s_ac, s_ac.n)])
            P.dr("vector")
            P.emit("vector", lambda e: e.tensor_tensor(
                stats[:, 9:10], stats[:, 28:29], stats[:, 7:8], op=op.mult),
                inc=s_ve)
            P.emit("scalar", lambda e: e.activation(
                lnf[:, :], ybuf[:, :], AF.Identity, bias=stats[:, 9:10],
                scale=stats[:, 7:8]), waits=[(s_ve, s_ve.n)], inc=s_ac)
            P.emit("vector", lambda e: e.tensor_tensor(
                lnf[:, :], lnf[:, :], gbb[:, :], op=op.mult),
                waits=[(s_ac, s_ac.n), (s_gp, gb_ready)])
            P.emit("vector", lambda e, xb6=xb6: e.tensor_tensor(
                t2buf[:, :], xb6[:, :], beb[:, :], op=op.add))
            P.dr("vector")
            for n in range(4):
                P.emit("vector", lambda e, fo=fo, n=n: e.tensor_tensor(
                    fo[:, n * 512:(n + 1) * 512], lnf[:, n * 512:(n + 1) * 512],
                    pa6[n][:, :], op=op.add),
                    waits=([(s_mm, mm_here)]
                           + ([out_done[i - 2]]
                              if (i - 2) in out_done else [])
                           if n == 0 else []))
            P.dr("vector")
            for n in range(4):
                P.emit("vector", lambda e, fo=fo, n=n: e.tensor_tensor(
                    fo[:, n * 512:(n + 1) * 512], fo[:, n * 512:(n + 1) * 512],
                    t2buf[:, n * 512:(n + 1) * 512], op=op.add),
                    inc=(s_ve if n == 3 else None))
            buf_free[i] = s_ve.n
            P.emit("sync", lambda e, fo=fo, i=i: e.dma_start(
                outt[i * 128:(i + 1) * 128, :], fo[:, :]),
                waits=[(s_ve, s_ve.n)], inc=s_do[i % 2], inck=16)
            out_done[i] = (s_do[i % 2], s_do[i % 2].n)
        P.emit("sync", lambda e: e.nop(),
               waits=[(s_do[0], s_do[0].n), (s_do[1], s_do[1].n)])
        quiesce()
        P.run_block(nc)

    es.close()
    return nc


def _prep_weights(inputs):
    Wq, Wk, Wv, Wo = (inputs["L0_attn_Wq"], inputs["L0_attn_Wk"],
                      inputs["L0_attn_Wv"], inputs["L0_attn_Wo"])
    h = np.arange(C) // 128
    j = np.arange(C) % 128
    perm = h * 128 + 2 * (j % 64) + (j // 64)
    def tile4(WT):
        return np.ascontiguousarray(WT.reshape(C, 4, 512).transpose(1, 0, 2))
    wqT = tile4(np.ascontiguousarray(Wq.T)[:, perm])
    wkT = tile4(np.ascontiguousarray(Wk.T)[:, perm])
    wvT = tile4(np.ascontiguousarray(Wv.T))
    woT = np.ascontiguousarray(Wo.T)
    wsT = np.ascontiguousarray(
        inputs["L0_ffn_W_syn"].T.reshape(C, 64, 128).transpose(1, 0, 2))
    woutT = np.ascontiguousarray(
        inputs["L0_ffn_W_out"].T.reshape(NH, 16, 128).transpose(1, 0, 2))
    return dict(wqT=wqT, wkT=wkT, wvT=wvT, woT=woT, wsT=wsT, woutT=woutT,
                bsynt=inputs["L0_ffn_b_syn"].reshape(1, NH),
                boutt=inputs["L0_ffn_b_out"].reshape(1, C),
                gammat=inputs["L0_ffn_gamma"].reshape(1, C),
                betat=inputs["L0_ffn_beta"].reshape(1, C))


def kernel(**inputs):
    inputs = {k: np.asarray(v, dtype=np.float32) for k, v in inputs.items()}
    x = inputs["L0_attn_x"]
    wd = _prep_weights(inputs)
    nc = build_nc()
    in_maps = []
    for c in range(8):
        b, hh = c // 2, c % 2
        if hh == 0:
            xsh = np.concatenate([x[b, :HALO], x[b, :NTOK]], axis=0)
            mf = np.full((128, 1), NEG, dtype=np.float32)
        else:
            xsh = x[b, NTOK - HALO:2 * NTOK]
            mf = np.zeros((128, 1), dtype=np.float32)
        m = {"xs": np.ascontiguousarray(xsh), "maskft": mf}
        m.update(wd)
        in_maps.append(m)
    res = run_bass_kernel_spmd(nc, in_maps, core_ids=list(range(8)))
    outp = np.empty((Bsz, T, C), dtype=np.float32)
    for c in range(8):
        b, hh = c // 2, c % 2
        outp[b, hh * NTOK:(hh + 1) * NTOK] = res.results[c]["out"]
    return outp


# version 10
# speedup vs baseline: 1.1368x; 1.1368x over previous
import contextlib
import numpy as np
import concourse.bass as bass
import concourse.mybir as mybir
from concourse.alu_op_type import AluOpType as op
from concourse.bass_utils import run_bass_kernel_spmd

F32 = mybir.dt.float32
BF16 = mybir.dt.bfloat16
FP16 = mybir.dt.float16
AF = mybir.ActivationFunctionType

Bsz, T, C = 4, 2048, 2048
H, DH, W = 16, 128, 256
NH = 8192
NTOK = 1024
HALO = 256
NTOKH = NTOK + HALO
NSEL = 22
SCALE = float(DH) ** -0.5
LN1E4 = float(np.log(10000.0))
PI = float(np.pi)
NEG = -2.0e9
BIGH = 30000.0

ENGS = ("sync", "scalar", "vector", "tensor", "gpsimd")


class Sem:
    def __init__(self, h):
        self.h = h
        self.n = 0


class Prog:
    def __init__(self):
        self.streams = {e: [] for e in ENGS}

    def emit(self, eng, fn, waits=(), inc=None, inck=1):
        if inc is not None:
            inc.n += inck
        self.streams[eng].append((fn, tuple(waits), inc, inck))

    def dr(self, eng):
        self.streams[eng].append((lambda e: e.drain(), (), None, 1))

    def run_block(self, nc):
        with nc.Block() as blk:
            for eng in ENGS:
                items = self.streams[eng]
                if not items:
                    continue

                def mk(items):
                    def body(e):
                        for fn, waits, inc, inck in items:
                            for (s, v) in waits:
                                if v > 0:
                                    e.wait_ge(s.h, v)
                            instr = fn(e)
                            if inc is not None:
                                instr.then_inc(inc.h, inck)
                    return body
                getattr(blk, eng)(mk(items))
        self.streams = {e: [] for e in ENGS}


def build_nc(debug_taps=(), upto=99, p0mask=7):
    nc = bass.Bass("TRN2", target_bir_lowering=False)
    es = contextlib.ExitStack()

    def sb(name, shape, dt):
        return es.enter_context(nc.sbuf_tensor(name, shape, dt))

    def mksem(name):
        return Sem(es.enter_context(nc.semaphore(name)))

    xs = nc.dram_tensor("xs", [NTOKH, C], F32, kind="ExternalInput")
    wqT = nc.dram_tensor("wqT", [4, 128, 16 * 512], F32, kind="ExternalInput")
    wkT = nc.dram_tensor("wkT", [4, 128, 16 * 512], F32, kind="ExternalInput")
    wvT = nc.dram_tensor("wvT", [4, 128, 16 * 512], F32, kind="ExternalInput")
    woT = nc.dram_tensor("woT", [C, C], F32, kind="ExternalInput")
    wsT = nc.dram_tensor("wsT", [64, 128, C], F32, kind="ExternalInput")
    woutT = nc.dram_tensor("woutT", [64, 128, C], F32, kind="ExternalInput")
    bsynt = nc.dram_tensor("bsynt", [1, NH], F32, kind="ExternalInput")
    boutt = nc.dram_tensor("boutt", [1, C], F32, kind="ExternalInput")
    gammat = nc.dram_tensor("gammat", [1, C], F32, kind="ExternalInput")
    betat = nc.dram_tensor("betat", [1, C], F32, kind="ExternalInput")
    maskft = nc.dram_tensor("maskft", [128, 1], F32, kind="ExternalInput")
    outt = nc.dram_tensor("out", [NTOK, C], F32, kind="ExternalOutput")

    xnsp = nc.dram_tensor("xnsp", [8, 128, C], BF16, kind="Internal")
    qsp = nc.dram_tensor("qsp", [16, 128, NTOK], BF16, kind="Internal")
    ksp = nc.dram_tensor("ksp", [16, 128, NTOKH], BF16, kind="Internal")
    vsp = nc.dram_tensor("vsp", [NTOKH, C], BF16, kind="Internal")
    asp = nc.dram_tensor("asp", [16, 128, NTOK], BF16, kind="Internal")
    osp = nc.dram_tensor("osp", [C, NTOK], BF16, kind="Internal")

    taps = {}
    for name, shape, dt in debug_taps:
        taps[name] = nc.dram_tensor(name, shape, dt, kind="ExternalOutput")

    ident = sb("ident", [128, 128], BF16)
    identh = sb("identh", [128, 128], FP16)
    onesb = sb("onesb", [1, 128], BF16)
    onesf = sb("onesf", [1, 128], F32)
    stats = sb("stats", [128, 120], F32)
    scal = sb("scal", [128, 8], F32)
    maskfb = sb("maskfb", [128, 1], F32)
    mcols = sb("mcols", [128, 64], F32)
    cstb = sb("cstb", [128, 2], F32)
    pX = contextlib.ExitStack()
    xnTc = pX.enter_context(nc.sbuf_tensor("xnTc", [128, 16 * NTOK], BF16))
    pA = contextlib.ExitStack()
    xnTh = pA.enter_context(nc.sbuf_tensor("xnTh", [128, 16 * HALO], BF16))
    cosT = pA.enter_context(nc.sbuf_tensor("cosT", [128, NTOKH], BF16))
    sinsT = pA.enter_context(nc.sbuf_tensor("sinsT", [128, NTOKH], BF16))

    s_dx = mksem("s_dx")
    s_dw = mksem("s_dw")
    s_dm = mksem("s_dm")
    s_ds = mksem("s_ds")
    s_dr = mksem("s_dr")
    s_do = [mksem("s_do0"), mksem("s_do1")]
    s_mm = mksem("s_mm")
    s_ve = mksem("s_ve")
    s_ac = mksem("s_ac")
    s_gp = mksem("s_gp")
    s_tt = mksem("s_tt")
    s_mf = mksem("s_mf")
    s_x = [mksem("s_x0"), mksem("s_x1")]
    s_sp = [mksem("s_sp0"), mksem("s_sp1")]
    s_wb = [mksem("s_wb0"), mksem("s_wb1")]
    s_ms = [mksem("s_ms0"), mksem("s_ms1")]
    s_qsp = mksem("s_qsp")
    s_vb = [mksem("s_vb0"), mksem("s_vb1")]
    s_wd = [mksem("s_wd0"), mksem("s_wd1")]
    s_at = mksem("s_at")
    s_sg = [mksem("s_sg0"), mksem("s_sg1")]
    s_bs = mksem("s_bs")
    s_s5 = [mksem("s_s50"), mksem("s_s51")]
    s_bo = mksem("s_bo")
    s_os = [mksem("s_os0"), mksem("s_os1")]
    s_l = [mksem("s_l0"), mksem("s_l1")]
    s_wo = mksem("s_wo")
    s_gb2 = mksem("s_gb2")
    s_sw = mksem("s_sw")

    DMASEMS = [s_ds, s_dr, s_do[0], s_do[1], s_mf, s_x[0], s_x[1], s_sp[0], s_sp[1],
               s_wb[0], s_wb[1], s_ms[0], s_ms[1], s_qsp, s_vb[0], s_vb[1],
               s_wd[0], s_wd[1], s_at, s_sg[0], s_sg[1], s_bs, s_s5[0],
               s_s5[1], s_bo, s_os[0], s_os[1], s_l[0], s_l[1], s_wo, s_gb2, s_sw,
               s_dx, s_dw, s_dm]

    def quiesce():
        ws = [(sm, sm.n) for sm in DMASEMS if sm.n > 0]
        if ws:
            P.emit("sync", lambda e: e.nop(), waits=ws)

    P = Prog()

    with contextlib.ExitStack() as p0:
        def p0sb(name, shape, dt):
            return p0.enter_context(nc.sbuf_tensor(name, shape, dt))
        ct = p0sb("ct", [128, NTOKH], F32)
        st = p0sb("st", [128, NTOKH], F32)
        ang = p0sb("ang", [128, 24], F32)
        tmpl = p0sb("tmpl", [128, 2048], F32)
        xb = [p0sb(f"xb{i}", [128, C], F32) for i in range(2)]
        xnb = [p0sb(f"xnb{i}", [128, C], BF16) for i in range(2)]
        trash = p0sb("trash", [128, 2 * C], FP16)
        ptp = [p0.enter_context(nc.psum_tensor(f"ptp{i}", [128, 1024], BF16))
               for i in range(2)]

        P.emit("gpsimd", lambda e: e.memset(stats[:, :], 0.0))
        P.emit("gpsimd", lambda e: e.memset(scal[:, :], 0.0))
        P.emit("gpsimd", lambda e: e.memset(ident[:, :], 0.0), inc=s_gp)
        P.emit("gpsimd", lambda e: e.affine_select(
            out=ident[:, :], in_=ident[:, :], compare_op=op.not_equal,
            fill=1.0, base=0, pattern=[[-1, 128]], channel_multiplier=1),
            waits=[(s_gp, s_gp.n)])
        P.emit("gpsimd", lambda e: e.memset(identh[:, :], 0.0), inc=s_gp)
        P.emit("gpsimd", lambda e: e.affine_select(
            out=identh[:, :], in_=identh[:, :], compare_op=op.not_equal,
            fill=1.0, base=0, pattern=[[-1, 128]], channel_multiplier=1),
            waits=[(s_gp, s_gp.n)])
        P.emit("gpsimd", lambda e: e.memset(onesb[:, :], 1.0))
        P.emit("gpsimd", lambda e: e.memset(onesf[:, :], 1.0))
        P.emit("gpsimd", lambda e: e.memset(stats[:, 10:11], 1.0))
        P.emit("gpsimd", lambda e: e.memset(cstb[:, 0:1], PI / 2))
        P.emit("gpsimd", lambda e: e.memset(cstb[:, 1:2], 1e-5))
        P.emit("gpsimd", lambda e: e.iota(
            stats[:, 14:15], [[1, 1]], channel_multiplier=1,
            allow_small_or_imprecise_dtypes=True), inc=s_gp)

        P.emit("sync", lambda e: e.dma_start(maskfb[:, :], maskft[:, :]),
               inc=s_mf, inck=16)

        P.emit("vector", lambda e: e.tensor_scalar(
            stats[:, 15:16], stats[:, 14:15], 64.0, None, op0=op.is_ge),
            waits=[(s_gp, s_gp.n)])
        P.dr("vector")
        P.emit("vector", lambda e: e.scalar_tensor_tensor(
            stats[:, 13:14], stats[:, 15:16], -64.0, stats[:, 14:15],
            op0=op.mult, op1=op.add))
        P.emit("vector", lambda e: e.tensor_scalar(
            stats[:, 12:13], stats[:, 15:16], 2.0, -1.0, op0=op.mult, op1=op.add),
            inc=s_ve)
        P.dr("vector")

        P.emit("scalar", lambda e: e.activation(
            stats[:, 11:12], stats[:, 13:14], AF.Exp, scale=-LN1E4 / 64.0),
            waits=[(s_ve, s_ve.n)])
        P.dr("scalar")
        P.emit("scalar", lambda e: e.activation(
            ct[:, 1:2], stats[:, 11:12], AF.Sin, bias=cstb[:, 0:1]))
        P.emit("scalar", lambda e: e.activation(
            st[:, 1:2], stats[:, 11:12], AF.Sin), inc=s_ac)

        def ladder(e):
            e.memset(ct[:, 0:1], 1.0)
            e.memset(st[:, 0:1], 0.0)
            e.drain()
            wl = 2
            while wl < NTOKH:
                L = min(wl, NTOKH - wl)
                c1, s1 = ct[:, 1:2], st[:, 1:2]
                cp, sp = ct[:, wl - 1:wl], st[:, wl - 1:wl]
                e.tensor_tensor(tmpl[:, 0:1], cp, c1, op=op.mult)
                e.tensor_tensor(tmpl[:, 1:2], sp, s1, op=op.mult)
                e.tensor_tensor(tmpl[:, 2:3], cp, s1, op=op.mult)
                e.tensor_tensor(tmpl[:, 3:4], sp, c1, op=op.mult)
                e.drain()
                e.tensor_tensor(ang[:, 0:1], tmpl[:, 0:1], tmpl[:, 1:2],
                                op=op.subtract)
                e.tensor_tensor(ang[:, 1:2], tmpl[:, 2:3], tmpl[:, 3:4],
                                op=op.add)
                e.drain()
                cr, sr = ang[:, 0:1], ang[:, 1:2]
                ta, tb = tmpl[:, 0:L], tmpl[:, 1024:1024 + L]
                e.tensor_scalar(ta, st[:, 0:L], sr, None, op0=op.mult)
                e.tensor_scalar(tb, st[:, 0:L], cr, None, op0=op.mult)
                e.drain()
                e.scalar_tensor_tensor(ct[:, wl:wl + L], ct[:, 0:L], cr,
                                       ta, op0=op.mult, op1=op.subtract)
                e.scalar_tensor_tensor(st[:, wl:wl + L], ct[:, 0:L], sr,
                                       tb, op0=op.mult, op1=op.add)
                e.drain()
                wl *= 2
            e.tensor_copy(cosT[:, :], ct[:, :])
            e.drain()
            return e.tensor_scalar(sinsT[:, :], st[:, :], stats[:, 12:13], None,
                                   op0=op.mult)
        if p0mask & 1:
            P.emit("vector", ladder, waits=[(s_ac, s_ac.n)], inc=s_ve)
        else:
            P.emit("vector", lambda e: e.tensor_copy(cosT[:, :], ct[:, :]),
                   waits=[(s_ac, s_ac.n)])
            P.emit("vector", lambda e: e.tensor_copy(sinsT[:, :], st[:, :]),
                   inc=s_ve)

        cast_done = {}
        drain_done = {}
        tp_done = {}
        spill_xn = {}
        for i in range(10):
            b, nb = xb[i % 2], xnb[i % 2]
            w = [(s_ac, cast_done[i - 2])] if i >= 2 else []
            P.emit("sync", lambda e, b=b, i=i: e.dma_start(
                b[:, :], xs[128 * i:128 * (i + 1), :]), waits=w,
                inc=s_x[i % 2], inck=16)
            P.emit("vector", lambda e, b=b: e.tensor_scalar(
                trash[:, 0:C], b[:, :], 0.0, None, op0=op.add, op1=op.add,
                accum_out=stats[:, 0:1]), waits=[(s_x[i % 2], s_x[i % 2].n)])
            P.emit("scalar", lambda e, b=b: e.activation(
                trash[:, C:2 * C], b[:, :], AF.Square,
                accum_out=stats[:, 1:2]),
                waits=[(s_x[i % 2], s_x[i % 2].n)], inc=s_ac)
            P.dr("vector")
            P.emit("vector", lambda e: e.tensor_scalar(
                stats[:, 2:3], stats[:, 0:1], 1.0 / C, None, op0=op.mult),
                waits=[(s_ac, s_ac.n)])
            P.emit("vector", lambda e: e.tensor_scalar(
                stats[:, 3:4], stats[:, 1:2], 1.0 / C, None, op0=op.mult))
            P.dr("vector")
            P.emit("vector", lambda e: e.tensor_scalar(
                stats[:, 28:29], stats[:, 2:3], -1.0, None, op0=op.mult))
            P.emit("vector", lambda e: e.tensor_tensor(
                stats[:, 4:5], stats[:, 2:3], stats[:, 2:3], op=op.mult))
            P.dr("vector")
            P.emit("vector", lambda e: e.tensor_tensor(
                stats[:, 5:6], stats[:, 3:4], stats[:, 4:5], op=op.subtract),
                inc=s_ve)
            P.emit("scalar", lambda e: e.activation(
                stats[:, 6:7], stats[:, 5:6], AF.Sqrt, bias=cstb[:, 1:2]),
                waits=[(s_ve, s_ve.n)], inc=s_ac)
            P.emit("vector", lambda e: e.reciprocal(stats[:, 7:8], stats[:, 6:7]),
                   waits=[(s_ac, s_ac.n)])
            P.dr("vector")
            P.emit("vector", lambda e: e.tensor_tensor(
                stats[:, 9:10], stats[:, 28:29], stats[:, 7:8], op=op.mult),
                inc=s_ve)
            wcast = [(s_ve, s_ve.n)]
            if i >= 2:
                wcast.append((s_tt, tp_done[i - 2]))
                if (i - 2) in spill_xn:
                    wcast.append(spill_xn[i - 2])
            P.emit("scalar", lambda e, b=b, nb=nb: e.activation(
                nb[:, :], b[:, :], AF.Identity, bias=stats[:, 9:10],
                scale=stats[:, 7:8]), waits=wcast, inc=s_ac)
            cast_done[i] = s_ac.n
            if i >= 2:
                P.emit("sync", lambda e, nb=nb, i=i: e.dma_start(
                    xnsp[i - 2, :, :], nb[:, :]), waits=[(s_ac, s_ac.n)],
                    inc=s_sp[i % 2], inck=16)
                spill_xn[i] = (s_sp[i % 2], s_sp[i % 2].n)
            dstf = xnTc if i >= 2 else xnTh
            ncols = NTOK if i >= 2 else HALO
            toff = (i - 2) * 128 if i >= 2 else i * 128
            for g in range(2):
                pt = ptp[g]
                w = [(s_ac, cast_done[i])]
                if (i, g) != (0, 0) and (i, g) != (0, 1):
                    w.append((s_ve, drain_done[(i - 1, g)]))
                for kk in range(8):
                    k = 8 * g + kk
                    P.emit("tensor", lambda e, pt=pt, nb=nb, k=k, kk=kk: e.transpose(
                        pt[:, 128 * kk:128 * (kk + 1)],
                        nb[:, 128 * k:128 * (k + 1)], ident[:, :]),
                        waits=(w if kk == 0 else []), inc=s_tt)

                def drain(e, pt=pt, dstf=dstf, ncols=ncols, toff=toff, g=g):
                    dst = dstf[:, :].rearrange("p (k t) -> p k t", t=ncols)[
                        :, 8 * g:8 * g + 8, toff:toff + 128]
                    return e.tensor_copy(dst, pt[:, :])
                P.emit("vector", drain, waits=[(s_tt, s_tt.n)], inc=s_ve)
                drain_done[(i, g)] = s_ve.n
            tp_done[i] = s_tt.n
        quiesce()
        P.run_block(nc)
    if upto <= 0:
        pA.close(); pX.close(); es.close()
        return nc

    with contextlib.ExitStack() as p2:
        def p2sb(name, shape, dt):
            return p2.enter_context(nc.sbuf_tensor(name, shape, dt))
        wbuf = [p2sb(f"wbuf{i}", [128, 16 * 512], BF16) for i in range(2)]
        qkb = p2sb("qkb", [128, NTOKH], BF16)
        rsw = p2sb("rsw", [128, NTOKH], BF16)
        rt1 = p2sb("rt1", [128, NTOKH], BF16)
        rt2 = p2sb("rt2", [128, NTOKH], BF16)
        vb = [p2sb(f"vb{i}", [128, 512], BF16) for i in range(2)]
        mst = [p2sb(f"mst{i}", [128, 2 * C], F32) for i in range(2)]
        mtr = p2sb("mtr", [128, 2 * C], FP16)
        pj = [p2.enter_context(nc.psum_tensor(f"pj{i}", [128, 512], F32))
              for i in range(4)]

        mean_ms = {}
        for i in range(65):
            if i < 64:
                ms = mst[i % 2]
                src = (wsT[2 * i:2 * i + 2] if i < 32
                       else woutT[2 * (i - 32):2 * (i - 32) + 2])
                P.emit("scalar", lambda e, ms=ms, src=src: e.dma_start(
                    ms[:, :].rearrange("p (j o) -> p j o", o=C),
                    src.rearrange("j p o -> p j o")),
                    inc=s_ms[i % 2], inck=16)
                mean_ms[i] = s_ms[i % 2].n
            if i >= 1:
                j = i - 1
                msr = mst[j % 2]
                P.emit("scalar", lambda e, msr=msr, j=j: e.activation(
                    mtr[:, :], msr[:, :], AF.Abs, accum_out=mcols[:, j:j + 1]),
                    waits=[(s_ms[j % 2], mean_ms[j])], inc=s_ac)
        mean_done_ac = s_ac.n

        wts = {"q": wqT, "k": wkT, "v": wvT}
        NGR = {"q": [(256, 512), (768, 512)],
               "k": [(0, 256), (256, 512), (768, 512)]}
        chunk_use = {}
        pj_drain = {}
        qkb_spill = [0]
        vb_spill = {}

        def xnT_slice(k, t0, tl):
            if t0 < HALO:
                assert t0 + tl <= HALO
                return xnTh[:, k * HALO + t0: k * HALO + t0 + tl]
            return xnTc[:, k * NTOK + (t0 - HALO): k * NTOK + (t0 - HALO) + tl]

        cidx = 0
        for wi, wn in enumerate("qkv"):
            for c in range(4):
                wb = wbuf[cidx % 2]
                w = [(s_mm, chunk_use.get(cidx - 2, 0))]
                P.emit("gpsimd", lambda e, wb=wb, wn=wn, c=c: e.dma_start(
                    wb[:, :], wts[wn][c]),
                    waits=w, inc=s_wb[cidx % 2], inck=16)
                wb_ready = (s_wb[cidx % 2], s_wb[cidx % 2].n)
                if wn in "qk":
                    ntok = NTOK if wn == "q" else NTOKH
                    spt = qsp if wn == "q" else ksp
                    for m in range(4):
                        mg = 4 * c + m
                        pjs_used = []
                        for gi, (t0, tl) in enumerate(NGR[wn]):
                            pji = (mg * 3 + gi) % 4
                            pjs_used.append((pji, t0, tl))
                            wfirst = [wb_ready,
                                      (s_ve, pj_drain.get(pji, 0))]
                            for k in range(16):
                                P.emit("tensor",
                                       lambda e, pji=pji, tl=tl, k=k, t0=t0,
                                       wb=wb, m=m: e.matmul(
                                           pj[pji][:, 0:tl],
                                           wb[:, k * 512 + m * 128:
                                              k * 512 + (m + 1) * 128],
                                           xnT_slice(k, t0, tl),
                                           start=(k == 0), stop=(k == 15)),
                                       waits=(wfirst if k == 0 else []),
                                       inc=(s_mm if k == 15 else None))
                        mm_here = s_mm.n
                        first_drain = True
                        for (pji, t0, tl) in pjs_used:
                            off = t0 if wn == "k" else t0 - HALO
                            wd = [(s_mm, mm_here)]
                            if first_drain:
                                wd.append((s_qsp, qkb_spill[0]))
                                first_drain = False
                            lastd = (pji == pjs_used[-1][0])
                            P.emit("vector", lambda e, pji=pji, off=off, tl=tl:
                                   e.tensor_copy(qkb[:, off:off + tl],
                                                 pj[pji][:, 0:tl]),
                                   waits=wd,
                                   inc=(s_ve if (pji, t0, tl) == pjs_used[-1]
                                        else None))
                        for (pji, t0, tl) in pjs_used:
                            pj_drain[pji] = s_ve.n
                        P.emit("sync", lambda e, ntok=ntok: e.dma_start(
                            rsw[0:64, 0:ntok], qkb[64:128, 0:ntok]),
                            waits=[(s_ve, s_ve.n), (s_qsp, qkb_spill[0])],
                            inc=s_sw, inck=16)
                        P.emit("sync", lambda e, ntok=ntok: e.dma_start(
                            rsw[64:128, 0:ntok], qkb[0:64, 0:ntok]),
                            inc=s_sw, inck=16)
                        tsl = (HALO, NTOKH) if wn == "q" else (0, NTOKH)
                        P.emit("vector", lambda e, ntok=ntok, tsl=tsl: e.tensor_tensor(
                            rt1[:, 0:ntok], rsw[:, 0:ntok],
                            sinsT[:, tsl[0]:tsl[1]], op=op.mult),
                            waits=[(s_sw, s_sw.n)])
                        P.emit("vector", lambda e, ntok=ntok, tsl=tsl: e.tensor_tensor(
                            rt2[:, 0:ntok], qkb[:, 0:ntok],
                            cosT[:, tsl[0]:tsl[1]], op=op.mult))
                        P.dr("vector")
                        P.emit("vector", lambda e, ntok=ntok: e.tensor_tensor(
                            qkb[:, 0:ntok], rt2[:, 0:ntok], rt1[:, 0:ntok],
                            op=op.add), inc=s_ve)
                        P.emit("sync", lambda e, spt=spt, mg=mg, ntok=ntok:
                               e.dma_start(spt[mg, :, 0:ntok], qkb[:, 0:ntok]),
                               waits=[(s_ve, s_ve.n)], inc=s_qsp, inck=16)
                        qkb_spill[0] = s_qsp.n
                        chunk_use[cidx] = s_mm.n
                else:
                    for m in range(10):
                        pji = m % 2
                        vbi = vb[m % 2]
                        wfirst = [wb_ready, (s_ve, pj_drain.get(pji, 0))]
                        for k in range(16):
                            t0 = m * 128
                            P.emit("tensor",
                                   lambda e, pji=pji, k=k, t0=t0, wb=wb:
                                   e.matmul(pj[pji][:, :],
                                            xnT_slice(k, t0, 128),
                                            wb[:, k * 512:(k + 1) * 512],
                                            start=(k == 0), stop=(k == 15)),
                                   waits=(wfirst if k == 0 else []),
                                   inc=(s_mm if k == 15 else None))
                        P.emit("vector", lambda e, pji=pji, vbi=vbi: e.tensor_copy(
                            vbi[:, :], pj[pji][:, :]),
                            waits=[(s_mm, s_mm.n)]
                            + ([vb_spill[m % 2]] if (m % 2) in vb_spill else []),
                            inc=s_ve)
                        pj_drain[pji] = s_ve.n
                        P.emit("sync", lambda e, vbi=vbi, m=m, c=c: e.dma_start(
                            vsp[m * 128:(m + 1) * 128, c * 512:(c + 1) * 512],
                            vbi[:, :]), waits=[(s_ve, s_ve.n)],
                            inc=s_vb[m % 2], inck=16)
                        vb_spill[m % 2] = (s_vb[m % 2], s_vb[m % 2].n)
                    chunk_use[cidx] = s_mm.n
                cidx += 1
        quiesce()
        P.run_block(nc)
    pA.close()
    if upto <= 2:
        pX.close(); es.close()
        return nc

    with contextlib.ExitStack() as p3:
        def p3sb(name, shape, dt):
            return p3.enter_context(nc.sbuf_tensor(name, shape, dt))
        qwin = [p3sb(f"qwin{i}", [128, 16 * 256], BF16) for i in range(2)]
        kwin = [p3sb(f"kwin{i}", [128, 16 * 512], BF16) for i in range(2)]
        vwin = [p3sb(f"vwin{i}", [128, 4 * 2048], BF16) for i in range(2)]
        probs = p3sb("probs", [128, 1024], BF16)
        pTs = p3sb("pTs", [128, 1024], BF16)
        atile = p3sb("atile", [128, 16 * 256], BF16)
        mt = [p3sb(f"mt{i}", [128, 256], F32) for i in range(2)]
        psc = [p3.enter_context(nc.psum_tensor(f"psc{i}", [128, 512], F32))
               for i in range(4)]
        ppt = [p3.enter_context(nc.psum_tensor(f"ppt{i}", [128, 1024], BF16))
               for i in range(2)]
        pao = [p3.enter_context(nc.psum_tensor(f"pao{i}", [128, 512], F32))
               for i in range(2)]

        for j in range(2):
            P.emit("gpsimd", lambda e, j=j: e.memset(mt[j][:, :], 0.0), inc=s_gp)
            P.emit("gpsimd", lambda e, j=j: e.affine_select(
                out=mt[j][:, :], in_=mt[j][:, :], compare_op=op.is_ge,
                fill=NEG, base=j * 128, pattern=[[-1, 256]],
                channel_multiplier=1), waits=[(s_gp, s_gp.n)], inc=s_gp)
        mask_ready = s_gp.n

        win_used = {}
        sc_read = {}
        pt_drain = {}
        ao_drain = {}
        probs_tt = [0]
        pts_mm = [0]
        asp_prev = [0]
        spills_done = [(s_qsp, s_qsp.n), vb_spill[0], vb_spill[1]]
        for ci in range(4):
            qw, kw, vw = qwin[ci % 2], kwin[ci % 2], vwin[ci % 2]
            wreuse = [(s_mm, win_used.get(ci - 2, 0))]
            if ci == 0:
                wreuse += spills_done
            P.emit("sync", lambda e, qw=qw, ci=ci: e.dma_start(
                qw[:, :].rearrange("p (m t) -> p m t", t=256),
                qsp[:, :, ci * 256:(ci + 1) * 256]
                .rearrange("m p t -> p m t")), waits=wreuse,
                inc=s_wd[ci % 2], inck=16)
            P.emit("sync", lambda e, kw=kw, ci=ci: e.dma_start(
                kw[:, :].rearrange("p (m t) -> p m t", t=512),
                ksp[:, :, ci * 256:ci * 256 + 512]
                .rearrange("m p t -> p m t")), inc=s_wd[ci % 2], inck=16)
            P.emit("sync", lambda e, vw=vw, ci=ci: e.dma_start(
                vw[:, :].rearrange("p (m t) -> p m t", t=2048),
                vsp[ci * 256:ci * 256 + 512, :]
                .rearrange("(m p) t -> p m t", p=128)),
                inc=s_wd[ci % 2], inck=16)
            dr_here = (s_wd[ci % 2], s_wd[ci % 2].n)
            for h in range(16):
                it = ci * 16 + h
                b0, b1 = (2 * (h % 2)), (2 * (h % 2) + 1)
                for j, bj in enumerate((b0, b1)):
                    wfirst = [dr_here, (s_ac, sc_read.get(bj, 0))]
                    P.emit("tensor", lambda e, bj=bj, qw=qw, kw=kw, h=h, j=j:
                           e.matmul(psc[bj][:, :],
                                    qw[:, h * 256 + j * 128:h * 256 + (j + 1) * 128],
                                    kw[:, h * 512:(h + 1) * 512],
                                    start=True, stop=True), waits=wfirst,
                           inc=(s_mm if j == 1 else None))
                mm_here = s_mm.n
                nmask = 4 if ci == 0 else 2
                cnt_m = [0]
                for j, bj in enumerate((b0, b1)):
                    cnt_m[0] += 1
                    P.emit("vector", lambda e, bj=bj, j=j: e.tensor_tensor(
                        psc[bj][:, 256:512], psc[bj][:, 256:512], mt[j][:, :],
                        op=op.add), waits=([(s_mm, mm_here), (s_gp, mask_ready),
                                            (s_mf, 16)]
                                           if j == 0 else []),
                        inc=(s_ve if cnt_m[0] == nmask else None))
                    if ci == 0:
                        cnt_m[0] += 1
                        P.emit("vector", lambda e, bj=bj: e.tensor_scalar(
                            psc[bj][:, 0:256], psc[bj][:, 0:256], maskfb[:, :],
                            None, op0=op.add),
                            inc=(s_ve if cnt_m[0] == nmask else None))
                for j, bj in enumerate((b0, b1)):
                    P.emit("scalar", lambda e, bj=bj, j=j: e.activation(
                        probs[:, j * 512:(j + 1) * 512], psc[bj][:, :], AF.Exp,
                        scale=SCALE, accum_out=stats[:, 18 + j:19 + j]),
                        waits=([(s_ve, s_ve.n), (s_tt, probs_tt[0])]
                               if j == 0 else []),
                        inc=(s_ac if j == 1 else None))
                for j, bj in enumerate((b0, b1)):
                    sc_read[bj] = s_ac.n
                P.dr("vector")
                P.emit("vector", lambda e: e.reciprocal(
                    stats[:, 22:24], stats[:, 18:20]), waits=[(s_ac, s_ac.n)])
                P.dr("vector")
                for j in range(2):
                    P.emit("vector", lambda e, j=j: e.tensor_scalar(
                        probs[:, j * 512:(j + 1) * 512],
                        probs[:, j * 512:(j + 1) * 512],
                        stats[:, 22 + j:23 + j], None, op0=op.mult),
                        inc=(s_ve if j == 1 else None))
                norm_here = s_ve.n
                pb = ppt[h % 2]
                wfirst = [(s_ve, norm_here), (s_ve, pt_drain.get(h % 2, 0))]
                for j in range(2):
                    for bb in range(4):
                        P.emit("tensor", lambda e, pb=pb, j=j, bb=bb: e.transpose(
                            pb[:, bb * 256 + j * 128: bb * 256 + (j + 1) * 128],
                            probs[:, j * 512 + bb * 128: j * 512 + (bb + 1) * 128],
                            ident[:, :]),
                            waits=(wfirst if (j, bb) == (0, 0) else []),
                            inc=(s_tt if (j, bb) == (1, 3) else None))
                probs_tt[0] = s_tt.n
                P.emit("vector", lambda e, pb=pb: e.tensor_copy(pTs[:, :], pb[:, :]),
                       waits=[(s_tt, s_tt.n), (s_mm, pts_mm[0])], inc=s_ve)
                pt_drain[h % 2] = s_ve.n
                po = pao[h % 2]
                wfirst = [(s_ve, s_ve.n), (s_ve, ao_drain.get(h % 2, 0))]
                for bb in range(4):
                    P.emit("tensor", lambda e, po=po, vw=vw, h=h, bb=bb: e.matmul(
                        po[:, 0:256], vw[:, bb * 2048 + h * 128:
                                         bb * 2048 + (h + 1) * 128],
                        pTs[:, bb * 256:(bb + 1) * 256],
                        start=(bb == 0), stop=(bb == 3)),
                        waits=(wfirst if bb == 0 else []),
                        inc=(s_mm if bb == 3 else None))
                pts_mm[0] = s_mm.n
                wad = [(s_mm, s_mm.n)]
                if h == 0:
                    wad.append((s_at, asp_prev[0]))
                P.emit("vector", lambda e, po=po, h=h: e.tensor_copy(
                    atile[:, h * 256:(h + 1) * 256], po[:, 0:256]),
                    waits=wad, inc=s_ve)
                ao_drain[h % 2] = s_ve.n
            win_used[ci] = s_mm.n
            P.emit("sync", lambda e, ci=ci: e.dma_start(
                asp[:, :, ci * 256:(ci + 1) * 256]
                .rearrange("m p t -> p m t"),
                atile[:, :].rearrange("p (m t) -> p m t", t=256)),
                waits=[(s_ve, s_ve.n)], inc=s_at, inck=16)
            asp_prev[0] = s_at.n
        quiesce()
        P.run_block(nc)
    if upto <= 3:
        pX.close(); es.close()
        return nc

    pArena = contextlib.ExitStack()
    arena_t = pArena.enter_context(nc.sbuf_tensor("arena", [128, 8 * NH], FP16, side="right"))
    p4 = contextlib.ExitStack()
    with p4:
        def p4sb(name, shape, dt):
            return p4.enter_context(nc.sbuf_tensor(name, shape, dt))
        selscr = p4sb("selscr", [128, 4096], FP16)
        gbuf = p4sb("gbuf", [128, 2048], BF16)
        stg = [p4sb(f"stg{i}", [128, 2048], F32) for i in range(2)]
        tern = [p4sb(f"tern{i}", [128, 16 * 256], BF16) for i in range(2)]
        bsb = p4sb("bsb", [1, 256], BF16)
        bsb2 = p4sb("bsb2", [1, 256], BF16)
        pz = [p4.enter_context(nc.psum_tensor(f"pz{i}", [128, 512], F32))
              for i in range(4)]
        phh = [p4.enter_context(nc.psum_tensor(f"phh{i}", [128, 1024], FP16))
               for i in range(2)]
        pm = p4.enter_context(nc.psum_tensor("pm", [1, 128], F32))
        pm2 = p4.enter_context(nc.psum_tensor("pm2", [128, 64], F32))

        P.emit("tensor", lambda e: e.matmul(pm[:, 0:64], stats[:, 10:11],
                                            mcols[:, :], start=True, stop=True),
               waits=[(s_ac, mean_done_ac)], inc=s_mm)
        P.emit("vector", lambda e: e.tensor_scalar(
            pm2[0:1, 0:32], pm[0:1, 0:32], 0.0, None, op0=op.add,
            op1=op.add, accum_out=scal[0:1, 0:1]), waits=[(s_mm, s_mm.n)])
        P.dr("vector")
        P.emit("vector", lambda e: e.tensor_scalar(
            pm2[0:1, 0:32], pm[0:1, 32:64], 0.0, None, op0=op.add,
            op1=op.add, accum_out=scal[0:1, 1:2]))
        P.dr("vector")
        P.emit("vector", lambda e: e.tensor_scalar(
            scal[0:1, 2:4], scal[0:1, 0:2], 1.0 / (NH * C), None, op0=op.mult))
        P.dr("vector")
        P.emit("vector", lambda e: e.tensor_scalar(
            scal[0:1, 4:6], scal[0:1, 2:4], 1e-8, 0.5, op0=op.add, op1=op.mult))
        P.emit("vector", lambda e: e.tensor_scalar(
            scal[0:1, 6:8], scal[0:1, 2:4], -1.0, None, op0=op.mult), inc=s_ve)
        P.dr("vector")
        P.emit("tensor", lambda e: e.matmul(
            pm2[:, 0:6], onesf[:, :], scal[0:1, 2:8], start=True, stop=True),
            waits=[(s_ve, s_ve.n)], inc=s_mm)
        P.emit("vector", lambda e: e.tensor_copy(scal[:, 2:8], pm2[:, 0:6]),
               waits=[(s_mm, s_mm.n)], inc=s_ve)
        P.emit("gpsimd", lambda e: e.memset(stats[:, 31:32], 0.0),
               waits=[(s_ve, s_ve.n)], inc=s_gp)
        P.emit("vector", lambda e: e.tensor_scalar(
            stats[:, 24:25], scal[:, 4:5], -1.0, None, op0=op.mult),
            waits=[(s_gp, s_gp.n)])
        P.emit("vector", lambda e: e.tensor_scalar(
            stats[:, 26:27], scal[:, 5:6], -1.0, None, op0=op.mult))
        P.emit("vector", lambda e: e.reciprocal(stats[:, 25:26], scal[:, 6:7]))
        P.emit("vector", lambda e: e.reciprocal(stats[:, 27:28], scal[:, 7:8]),
               inc=s_ve)
        mean_ready = s_ve.n

        q_done = {}
        stg_used = {}
        prev_stt = [0]
        bias_prev = [0]
        tap_done = [0]
        z_drained = {}
        sel_done = {}
        pz_drain = {}
        sig_ready = 0

        def zq(t, q):
            return arena_t[:, t * NH + q * 2048: t * NH + (q + 1) * 2048]

        def shalf(i):
            return selscr[:, (i % 2) * 2048:(i % 2) * 2048 + 2048]

        def emit_sel_all():
            if "zraw" in taps:
                for t in range(8):
                    P.emit("sync", lambda e, t=t: e.dma_start(
                        taps["zraw"][t, :, :],
                        arena_t[:, t * NH:(t + 1) * NH]),
                        waits=[(s_ac, z_drained[7])], inc=s_ds, inck=16)
                tap_done[0] = s_ds.n
            wz = [(s_ac, z_drained[7]), (s_ds, tap_done[0])]
            CQ, S01, S23, CNT, GE, LO, HI, MID, MU = 64, 96, 104, 112, 16, 32, 40, 48, 56
            first = True
            for t in range(8):
                for q in range(4):
                    c = CQ + 4 * t + q
                    if (4 * t + q) % 2 == 0 and not first:
                        P.dr("vector")
                    P.emit("vector", lambda e, t=t, q=q, c=c: e.tensor_scalar(
                        shalf(4 * t + q), zq(t, q), 0.0, None, op0=op.add,
                        op1=op.add, accum_out=stats[:, c:c + 1]),
                        waits=(wz if first else []))
                    first = False
            P.dr("vector")
            for t in range(8):
                P.emit("vector", lambda e, t=t: e.tensor_tensor(
                    stats[:, S01 + t:S01 + t + 1], stats[:, CQ + 4 * t:CQ + 4 * t + 1],
                    stats[:, CQ + 4 * t + 1:CQ + 4 * t + 2], op=op.add))
            for t in range(8):
                P.emit("vector", lambda e, t=t: e.tensor_tensor(
                    stats[:, S23 + t:S23 + t + 1],
                    stats[:, CQ + 4 * t + 2:CQ + 4 * t + 3],
                    stats[:, CQ + 4 * t + 3:CQ + 4 * t + 4], op=op.add))
            P.dr("vector")
            for t in range(8):
                P.emit("vector", lambda e, t=t: e.tensor_tensor(
                    stats[:, CNT + t:CNT + t + 1], stats[:, S01 + t:S01 + t + 1],
                    stats[:, S23 + t:S23 + t + 1], op=op.add))
            P.dr("vector")
            for t in range(8):
                P.emit("vector", lambda e, t=t: e.tensor_scalar(
                    stats[:, MU + t:MU + t + 1], stats[:, CNT + t:CNT + t + 1],
                    1.0 / NH, None, op0=op.mult),
                    inc=(s_ve if t == 7 else None))
            P.dr("vector")
            P.dr("vector")
            for q in range(4):
                if q == 2:
                    P.dr("scalar")
                P.emit("scalar", lambda e, q=q: e.activation(
                    shalf(q), zq(0, q), AF.Square,
                    accum_out=stats[:, CQ + q:CQ + q + 1]),
                    waits=([(s_ve, s_ve.n)] if q == 0 else []),
                    inc=(s_ac if q == 3 else None))
            P.emit("vector", lambda e: e.tensor_tensor(
                stats[:, 0:1], stats[:, CQ:CQ + 1], stats[:, CQ + 1:CQ + 2],
                op=op.add), waits=[(s_ac, s_ac.n)])
            P.emit("vector", lambda e: e.tensor_tensor(
                stats[:, 1:2], stats[:, CQ + 2:CQ + 3], stats[:, CQ + 3:CQ + 4],
                op=op.add))
            P.emit("vector", lambda e: e.tensor_tensor(
                stats[:, 3:4], stats[:, MU:MU + 1], stats[:, MU:MU + 1],
                op=op.mult))
            P.dr("vector")
            P.emit("vector", lambda e: e.tensor_tensor(
                stats[:, 2:3], stats[:, 0:1], stats[:, 1:2], op=op.add))
            P.dr("vector")
            P.emit("vector", lambda e: e.tensor_scalar(
                stats[:, 4:5], stats[:, 2:3], 1.0 / NH, None, op0=op.mult))
            P.dr("vector")
            P.emit("vector", lambda e: e.tensor_tensor(
                stats[:, 5:6], stats[:, 4:5], stats[:, 3:4], op=op.subtract),
                inc=s_ve)
            P.emit("tensor", lambda e: e.matmul(
                pm[:, 2:3], stats[:, 10:11], stats[:, 5:6],
                start=True, stop=True), waits=[(s_ve, s_ve.n)], inc=s_mm)
            P.emit("vector", lambda e: e.tensor_scalar(
                scal[0:1, 0:1], pm[:, 2:3], 1.0 / 128, None, op0=op.mult),
                waits=[(s_mm, s_mm.n)], inc=s_ve)
            P.emit("scalar", lambda e: e.activation(
                scal[0:1, 1:2], scal[0:1, 0:1], AF.Sqrt),
                waits=[(s_ve, s_ve.n)], inc=s_ac)
            P.emit("tensor", lambda e: e.matmul(
                pm2[:, 0:2], onesf[:, :], scal[0:1, 0:2], start=True, stop=True),
                waits=[(s_ac, s_ac.n)], inc=s_mm)
            P.emit("vector", lambda e: e.tensor_copy(scal[:, 0:2], pm2[:, 0:2]),
                   waits=[(s_mm, s_mm.n)], inc=s_ve)
            P.emit("gpsimd", lambda e: e.memset(stats[:, 30:31], 0.0),
                   waits=[(s_ve, s_ve.n)], inc=s_gp)
            sig_ready = s_gp.n
            for t in range(8):
                P.emit("vector", lambda e, t=t: e.scalar_tensor_tensor(
                    stats[:, LO + t:LO + t + 1], scal[:, 1:2], 1.0,
                    stats[:, MU + t:MU + t + 1], op0=op.mult, op1=op.add),
                    waits=([(s_gp, sig_ready)] if t == 0 else []))
            for t in range(8):
                P.emit("vector", lambda e, t=t: e.scalar_tensor_tensor(
                    stats[:, HI + t:HI + t + 1], scal[:, 1:2], 1.6,
                    stats[:, MU + t:MU + t + 1], op0=op.mult, op1=op.add))
            for it in range(NSEL + 1):
                last = (it == NSEL)
                P.dr("vector")
                if not last:
                    for t in range(8):
                        P.emit("vector", lambda e, t=t: e.tensor_tensor(
                            stats[:, MID + t:MID + t + 1],
                            stats[:, LO + t:LO + t + 1],
                            stats[:, HI + t:HI + t + 1], op=op.add))
                    P.dr("vector")
                    for t in range(8):
                        P.emit("vector", lambda e, t=t: e.tensor_scalar(
                            stats[:, MID + t:MID + t + 1],
                            stats[:, MID + t:MID + t + 1], 0.5, None,
                            op0=op.mult))
                    P.dr("vector")
                pc = MID if not last else LO
                for t in range(8):
                    for q in range(4):
                        c = CQ + 4 * t + q
                        if (4 * t + q) % 2 == 0:
                            P.dr("vector")
                        P.emit("vector", lambda e, t=t, q=q, c=c, pc=pc:
                               e.tensor_scalar(
                                   shalf(4 * t + q), zq(t, q),
                                   stats[:, pc + t:pc + t + 1], None,
                                   op0=op.is_ge, op1=op.add,
                                   accum_out=stats[:, c:c + 1]))
                P.dr("vector")
                for t in range(8):
                    P.emit("vector", lambda e, t=t: e.tensor_tensor(
                        stats[:, S01 + t:S01 + t + 1],
                        stats[:, CQ + 4 * t:CQ + 4 * t + 1],
                        stats[:, CQ + 4 * t + 1:CQ + 4 * t + 2], op=op.add))
                for t in range(8):
                    P.emit("vector", lambda e, t=t: e.tensor_tensor(
                        stats[:, S23 + t:S23 + t + 1],
                        stats[:, CQ + 4 * t + 2:CQ + 4 * t + 3],
                        stats[:, CQ + 4 * t + 3:CQ + 4 * t + 4], op=op.add))
                P.dr("vector")
                for t in range(8):
                    P.emit("vector", lambda e, t=t: e.tensor_tensor(
                        stats[:, CNT + t:CNT + t + 1],
                        stats[:, S01 + t:S01 + t + 1],
                        stats[:, S23 + t:S23 + t + 1], op=op.add),
                        inc=(s_ve if (last and t == 7) else None))
                P.dr("vector")
                if last and "seldbg2" in taps:
                    P.emit("sync", lambda e: e.dma_start(
                        taps["seldbg2"][:, :], stats[:, :]),
                        waits=[(s_ve, s_ve.n)], inc=s_ds, inck=16)
                    tap2 = s_ds.n
                    P.emit("vector", lambda e: e.tensor_copy(
                        stats[:, 30:31], stats[:, 10:11]),
                        waits=[(s_ds, tap2)])
                if not last:
                    for t in range(8):
                        P.emit("vector", lambda e, t=t: e.tensor_scalar(
                            stats[:, GE + t:GE + t + 1],
                            stats[:, CNT + t:CNT + t + 1], 818.5, None,
                            op0=op.is_gt))
                    for t in range(8):
                        P.emit("vector", lambda e, t=t: e.tensor_scalar(
                            stats[:, MU + t:MU + t + 1],
                            stats[:, CNT + t:CNT + t + 1], 818.5, None,
                            op0=op.is_le))
                    P.dr("vector")
                    for t in range(8):
                        P.emit("vector", lambda e, t=t: e.tensor_tensor(
                            stats[:, S01 + t:S01 + t + 1],
                            stats[:, MID + t:MID + t + 1],
                            stats[:, LO + t:LO + t + 1], op=op.subtract))
                    for t in range(8):
                        P.emit("vector", lambda e, t=t: e.tensor_tensor(
                            stats[:, S23 + t:S23 + t + 1],
                            stats[:, MID + t:MID + t + 1],
                            stats[:, HI + t:HI + t + 1], op=op.subtract))
                    P.dr("vector")
                    for t in range(8):
                        P.emit("vector", lambda e, t=t: e.tensor_tensor(
                            stats[:, S01 + t:S01 + t + 1],
                            stats[:, S01 + t:S01 + t + 1],
                            stats[:, GE + t:GE + t + 1], op=op.mult))
                    for t in range(8):
                        P.emit("vector", lambda e, t=t: e.tensor_tensor(
                            stats[:, S23 + t:S23 + t + 1],
                            stats[:, S23 + t:S23 + t + 1],
                            stats[:, MU + t:MU + t + 1], op=op.mult))
                    P.dr("vector")
                    for t in range(8):
                        P.emit("vector", lambda e, t=t: e.tensor_tensor(
                            stats[:, LO + t:LO + t + 1],
                            stats[:, LO + t:LO + t + 1],
                            stats[:, S01 + t:S01 + t + 1], op=op.add))
                    for t in range(8):
                        P.emit("vector", lambda e, t=t: e.tensor_tensor(
                            stats[:, HI + t:HI + t + 1],
                            stats[:, HI + t:HI + t + 1],
                            stats[:, S23 + t:S23 + t + 1], op=op.add))
            for t in range(8):
                for qp in range(2):
                    q0, q1 = 2 * qp, 2 * qp + 1
                    P.dr("vector")
                    P.emit("vector", lambda e, t=t, q=q0: e.tensor_scalar(
                        shalf(q), zq(t, q), stats[:, LO + t:LO + t + 1], None,
                        op0=op.is_lt))
                    P.emit("vector", lambda e, t=t, q=q1: e.tensor_scalar(
                        shalf(q), zq(t, q), stats[:, LO + t:LO + t + 1], None,
                        op0=op.is_lt))
                    P.dr("vector")
                    P.emit("vector", lambda e, t=t, q=q0: e.scalar_tensor_tensor(
                        shalf(q), shalf(q), BIGH, zq(t, q),
                        op0=op.mult, op1=op.add))
                    P.emit("vector", lambda e, t=t, q=q1: e.scalar_tensor_tensor(
                        shalf(q), shalf(q), BIGH, zq(t, q),
                        op0=op.mult, op1=op.add))
                    P.dr("vector")
                    P.emit("vector", lambda e, t=t, q=q0: e.tensor_scalar(
                        shalf(q), shalf(q), 0.0, None, op0=op.add, op1=op.min,
                        accum_out=stats[:, CQ + 4 * t + q0:CQ + 4 * t + q0 + 1]))
                    P.emit("vector", lambda e, t=t, q=q1: e.tensor_scalar(
                        shalf(q), shalf(q), 0.0, None, op0=op.add, op1=op.min,
                        accum_out=stats[:, CQ + 4 * t + q1:CQ + 4 * t + q1 + 1]))
            P.dr("vector")
            for t in range(8):
                P.emit("vector", lambda e, t=t: e.tensor_tensor(
                    stats[:, S01 + t:S01 + t + 1],
                    stats[:, CQ + 4 * t:CQ + 4 * t + 1],
                    stats[:, CQ + 4 * t + 1:CQ + 4 * t + 2], op=op.min))
            for t in range(8):
                P.emit("vector", lambda e, t=t: e.tensor_tensor(
                    stats[:, S23 + t:S23 + t + 1],
                    stats[:, CQ + 4 * t + 2:CQ + 4 * t + 3],
                    stats[:, CQ + 4 * t + 3:CQ + 4 * t + 4], op=op.min))
            P.dr("vector")
            for t in range(8):
                P.emit("vector", lambda e, t=t: e.tensor_tensor(
                    stats[:, GE + t:GE + t + 1], stats[:, S01 + t:S01 + t + 1],
                    stats[:, S23 + t:S23 + t + 1], op=op.min))
            P.dr("vector")
            for t in range(8):
                P.emit("vector", lambda e, t=t: e.tensor_scalar(
                    stats[:, S23 + t:S23 + t + 1],
                    stats[:, CNT + t:CNT + t + 1], 819.5, None, op0=op.is_gt))
            for t in range(8):
                P.emit("vector", lambda e, t=t: e.tensor_scalar(
                    stats[:, MID + t:MID + t + 1],
                    stats[:, LO + t:LO + t + 1], -1e-12, None, op0=op.add))
            P.dr("vector")
            for t in range(8):
                P.emit("vector", lambda e, t=t: e.tensor_tensor(
                    stats[:, S01 + t:S01 + t + 1],
                    stats[:, GE + t:GE + t + 1],
                    stats[:, MID + t:MID + t + 1], op=op.subtract))
            P.dr("vector")
            for t in range(8):
                P.emit("vector", lambda e, t=t: e.tensor_tensor(
                    stats[:, S01 + t:S01 + t + 1],
                    stats[:, S01 + t:S01 + t + 1],
                    stats[:, S23 + t:S23 + t + 1], op=op.mult))
            P.dr("vector")
            for t in range(8):
                P.emit("vector", lambda e, t=t: e.tensor_tensor(
                    stats[:, CNT + t:CNT + t + 1],
                    stats[:, MID + t:MID + t + 1],
                    stats[:, S01 + t:S01 + t + 1], op=op.add))
            P.dr("vector")
            for t in range(8):
                for qp in range(2):
                    q0, q1 = 2 * qp, 2 * qp + 1
                    P.emit("vector", lambda e, t=t, q=q0: e.tensor_scalar(
                        shalf(q), zq(t, q), stats[:, CNT + t:CNT + t + 1],
                        None, op0=op.is_gt))
                    P.emit("vector", lambda e, t=t, q=q1: e.tensor_scalar(
                        shalf(q), zq(t, q), stats[:, CNT + t:CNT + t + 1],
                        None, op0=op.is_gt))
                    P.dr("vector")
                    P.emit("vector", lambda e, t=t, q=q0: e.tensor_tensor(
                        zq(t, q), zq(t, q), shalf(q), op=op.mult))
                    P.emit("vector", lambda e, t=t, q=q1: e.tensor_tensor(
                        zq(t, q), zq(t, q), shalf(q), op=op.mult),
                        inc=(s_ve if (t, qp) == (7, 1) else None))
                    P.dr("vector")
            for t in range(8):
                sel_done[t] = s_ve.n
            if "seldbg" in taps:
                P.emit("sync", lambda e: e.dma_start(
                    taps["seldbg"][0:128, :], stats[:, :]),
                    waits=[(s_ve, sel_done[7])], inc=s_ds, inck=16)
                P.emit("sync", lambda e: e.dma_start(
                    taps["seldbg"][128:129, 0:8], scal[0:1, :]),
                    inc=s_ds, inck=16)
                tap_done[0] = s_ds.n
            if "zdbg" in taps:
                for t in range(8):
                    P.emit("sync", lambda e, t=t: e.dma_start(
                        taps["zdbg"][t, :, :],
                        arena_t[:, t * NH:(t + 1) * NH]),
                        waits=[(s_ve, sel_done[t])], inc=s_ds, inck=16)
                tap_done[0] = s_ds.n

        for ch in range(32):
            tn = tern[ch % 2]
            for p2i in range(2):
                pi = ch * 2 + p2i
                sg = stg[pi % 2]
                w = [stg_used[pi - 2]] if (pi - 2) in stg_used else []
                P.emit("sync", lambda e, sg=sg, pi=pi: e.dma_start(
                    sg[:, :], wsT[pi]),
                    waits=w, inc=s_sg[pi % 2], inck=16)
                wq = [(s_sg[pi % 2], s_sg[pi % 2].n), (s_ve, mean_ready)]
                if p2i == 0:
                    wq.append((s_mm, q_done.get(ch - 2, 0)))

                def tslice(tn=tn, p2i=p2i):
                    return tn[:, :].rearrange("p (k o) -> p k o", o=256)[
                        :, :, p2i * 128:(p2i + 1) * 128]
                P.emit("vector", lambda e, sg=sg: e.tensor_scalar(
                    gbuf[:, :], sg[:, :], scal[:, 4:5], None, op0=op.is_gt),
                    waits=wq, inc=s_ve)
                P.dr("vector")
                P.emit("vector", lambda e, sg=sg, tn=tn, p2i=p2i:
                       e.scalar_tensor_tensor(
                           tslice(tn, p2i), sg[:, :], stats[:, 24:25],
                           gbuf[:, :], op0=op.is_lt, op1=op.subtract),
                       inc=s_ve)
                P.dr("vector")
                stg_used[pi] = (s_ve, s_ve.n)
                prev_stt[0] = s_ve.n
            tern_ready = s_ve.n
            P.emit("gpsimd", lambda e, ch=ch: e.dma_start(
                bsb[:, :], bsynt[:, ch * 256:(ch + 1) * 256]),
                waits=[(s_ve, bias_prev[0])], inc=s_bs, inck=16)
            P.emit("vector", lambda e: e.tensor_scalar(
                bsb2[:, :], bsb[:, :], stats[0:1, 25:26], None, op0=op.mult),
                waits=[(s_bs, s_bs.n), (s_ve, mean_ready),
                       (s_mm, q_done.get(ch - 1, 0))], inc=s_ve)
            bias_ready = s_ve.n
            bias_prev[0] = s_ve.n
            for tile in range(8):
                pzi = (ch * 8 + tile) % 4
                wfirst = [(s_ve, tern_ready), (s_ve, bias_ready),
                          (s_ac, pz_drain.get(pzi, 0))]
                for k in range(16):
                    P.emit("tensor", lambda e, pzi=pzi, tn=tn, k=k, tile=tile:
                           e.matmul(pz[pzi][:, 0:256],
                                    xnTc[:, k * NTOK + tile * 128:
                                         k * NTOK + (tile + 1) * 128],
                                    tn[:, k * 256:(k + 1) * 256],
                                    start=(k == 0), stop=False),
                           waits=(wfirst if k == 0 else []))
                P.emit("tensor", lambda e, pzi=pzi: e.matmul(
                    pz[pzi][:, 0:256], onesb[0:1, 0:128], bsb2[:, :],
                    start=False, stop=True), inc=s_mm)
                P.emit("scalar", lambda e, pzi=pzi, tile=tile, ch=ch:
                       e.activation(
                           arena_t[:, tile * NH + ch * 256:
                                   tile * NH + (ch + 1) * 256],
                           pz[pzi][:, 0:256], AF.Identity, scale=scal[:, 6:7]),
                       waits=[(s_mm, s_mm.n)], inc=s_ac)
                pz_drain[pzi] = s_ac.n
                if ch == 31:
                    z_drained[tile] = s_ac.n
            q_done[ch] = s_mm.n
        emit_sel_all()
        hh_drain = {}
        for tile in range(8):
            for g in range(8):
                ph = phh[g % 2]
                wfirst = [(s_ve, sel_done[tile]),
                          (s_ve, hh_drain.get(g % 2, 0)),
                          (s_ds, tap_done[0])]
                for kk in range(8):
                    k = g * 8 + kk
                    P.emit("tensor", lambda e, ph=ph, tile=tile, k=k, kk=kk:
                           e.transpose(
                               ph[:, kk * 128:(kk + 1) * 128],
                               arena_t[:, tile * NH + k * 128:
                                       tile * NH + (k + 1) * 128],
                               identh[:, :]),
                           waits=(wfirst if kk == 0 else []),
                           inc=(s_tt if kk == 7 else None))
                P.emit("vector", lambda e, ph=ph, tile=tile, g=g: e.tensor_copy(
                    arena_t[:, tile * NH + g * 1024:tile * NH + (g + 1) * 1024],
                    ph[:, :]), waits=[(s_tt, s_tt.n)], inc=s_ve)
                hh_drain[g % 2] = s_ve.n
        quiesce()
        P.run_block(nc)
    pX.close()
    if upto <= 4:
        pArena.close(); es.close()
        return nc

    if True:
        with contextlib.ExitStack() as p5:
            def p5sb(name, shape, dt):
                return p5.enter_context(nc.sbuf_tensor(name, shape, dt))
            stg5 = [p5sb(f"stg5{i}", [128, 2048], F32) for i in range(2)]
            onesh = p5sb("onesh", [1, 512], FP16)
            gb5 = p5sb("gb5", [128, 2048], FP16)
            ternD = [p5sb(f"ternD{i}", [128, 8192], FP16) for i in range(2)]
            bob = p5sb("bob", [1, 128], FP16)
            bob2 = p5sb("bob2", [1, 128], FP16)
            obuf = [p5sb(f"obuf{i}", [128, 1024], BF16) for i in range(2)]
            po5 = [p5.enter_context(nc.psum_tensor(f"po5{i}", [128, 512], F32))
                   for i in range(4)]

            P.emit("gpsimd", lambda e: e.memset(onesh[:, :], 1.0), inc=s_gp)
            onesh_ready = s_gp.n
            stg_used5 = {}
            tern_used5 = {}
            po_drain5 = {}
            ob_used = {}
            prev_stt5 = [0]
            b5_prev = [0]
            for d in range(16):
                tn = ternD[d % 2]
                wtn = [(s_mm, tern_used5.get(d - 2, 0))]
                for q4 in range(4):
                    pi = d * 4 + q4
                    sg = stg5[pi % 2]
                    w = [stg_used5[pi - 2]] if (pi - 2) in stg_used5 else []
                    P.emit("sync", lambda e, sg=sg, d=d, q4=q4: e.dma_start(
                        sg[:, :], woutT[d * 4 + q4]),
                        waits=w, inc=s_s5[pi % 2], inck=16)
                    P.emit("vector", lambda e, sg=sg: e.tensor_scalar(
                        gb5[:, :], sg[:, :], scal[:, 5:6], None, op0=op.is_gt),
                        waits=[(s_s5[pi % 2], s_s5[pi % 2].n)]
                        + (wtn if q4 == 0 else []), inc=s_ve)
                    P.dr("vector")
                    P.emit("vector", lambda e, sg=sg, tn=tn, q4=q4:
                           e.scalar_tensor_tensor(
                               tn[:, q4 * 2048:(q4 + 1) * 2048], sg[:, :],
                               stats[:, 26:27], gb5[:, :],
                               op0=op.is_lt, op1=op.subtract),
                           inc=s_ve)
                    P.dr("vector")
                    stg_used5[pi] = (s_ve, s_ve.n)
                t5_ready = s_ve.n
                P.emit("gpsimd", lambda e, d=d: e.dma_start(
                    bob[:, :], boutt[:, d * 128:(d + 1) * 128]),
                    waits=[(s_ve, b5_prev[0])], inc=s_bo, inck=16)
                P.emit("vector", lambda e: e.tensor_scalar(
                    bob2[:, :], bob[:, :], stats[0:1, 27:28], None, op0=op.mult),
                    waits=[(s_bo, s_bo.n), (s_mm, tern_used5.get(d - 1, 0))],
                    inc=s_ve)
                b5_ready = s_ve.n
                b5_prev[0] = s_ve.n
                ob = obuf[d % 2]
                for n in range(2):
                    pzi = (d * 2 + n) % 4
                    wfirst = [(s_ve, t5_ready), (s_ve, b5_ready),
                              (s_ve, po_drain5.get(pzi, 0))]
                    if d == 0 and n == 0:
                        wfirst.append((s_gp, onesh_ready))
                    for j in range(64):
                        def rhs(e, j=j, n=n):
                            return arena_t[:, :].rearrange(
                                "p (i f) -> p i f", f=NH)[
                                :, 4 * n:4 * n + 4, j * 128:(j + 1) * 128]
                        P.emit("tensor", lambda e, pzi=pzi, tn=tn, j=j, n=n:
                               e.matmul(po5[pzi][:, :],
                                        tn[:, j * 128:(j + 1) * 128],
                                        arena_t[:, :].rearrange(
                                            "p (i f) -> p i f", f=NH)[
                                            :, 4 * n:4 * n + 4,
                                            j * 128:(j + 1) * 128],
                                        start=(j == 0), stop=False),
                               waits=(wfirst if j == 0 else []))
                    P.emit("tensor", lambda e, pzi=pzi: e.matmul(
                        po5[pzi][:, :], bob2[:, :], onesh[0:1, 0:512],
                        start=False, stop=True), inc=s_mm)
                    P.emit("vector", lambda e, pzi=pzi, ob=ob, n=n:
                           e.tensor_scalar(
                               ob[:, n * 512:(n + 1) * 512], po5[pzi][:, :],
                               scal[:, 7:8], None, op0=op.mult),
                           waits=[(s_mm, s_mm.n)]
                           + ([ob_used[d - 2]]
                              if (n == 0 and (d - 2) in ob_used) else []),
                           inc=s_ve)
                    po_drain5[pzi] = s_ve.n
                tern_used5[d] = s_mm.n
                P.emit("sync", lambda e, ob=ob, d=d: e.dma_start(
                    osp[d * 128:(d + 1) * 128, :], ob[:, :]),
                    waits=[(s_ve, s_ve.n)], inc=s_os[d % 2], inck=16)
                ob_used[d] = (s_os[d % 2], s_os[d % 2].n)
            quiesce()
        P.run_block(nc)
    pArena.close()

    with contextlib.ExitStack() as p6:
        def p6sb(name, shape, dt):
            return p6.enter_context(nc.sbuf_tensor(name, shape, dt))
        wob = p6sb("wob", [128, 16 * 2048], BF16)
        gbb = p6sb("gbb", [128, 2048], F32)
        beb = p6sb("beb", [128, 2048], F32)
        xnl = [p6sb(f"xnl{i}", [128, 2048], BF16) for i in range(2)]
        onl = [p6sb(f"onl{i}", [128, 2048], BF16) for i in range(2)]
        anl = [p6sb(f"anl{i}", [128, 2048], BF16) for i in range(2)]
        xl = [p6sb(f"xl{i}", [128, 2048], F32) for i in range(2)]
        ybuf = p6sb("ybuf", [128, 2048], BF16)
        lnf = p6sb("lnf", [128, 2048], F32)
        t2buf = p6sb("t2buf", [128, 2048], F32)
        fout = [p6sb(f"fout{i}", [128, 2048], F32) for i in range(2)]
        trash6 = p6sb("trash6", [128, 4096], FP16)
        pa6 = [p6.enter_context(nc.psum_tensor(f"pa6{i}", [128, 512], F32))
               for i in range(4)]

        P.emit("sync", lambda e: e.dma_start(gbb[0:1, :], gammat[:, :]),
               inc=s_gb2, inck=16)
        P.emit("sync", lambda e: e.dma_start(beb[0:1, :], betat[:, :]),
               inc=s_gb2, inck=16)
        for n in range(4):
            P.emit("tensor", lambda e, n=n: e.matmul(
                pa6[n][:, :], onesf[:, :], gbb[0:1, n * 512:(n + 1) * 512],
                start=True, stop=True),
                waits=([(s_gb2, s_gb2.n)] if n == 0 else []),
                inc=(s_mm if n == 3 else None))
        for n in range(4):
            P.emit("vector", lambda e, n=n: e.tensor_copy(
                gbb[:, n * 512:(n + 1) * 512], pa6[n][:, :]),
                waits=([(s_mm, s_mm.n)] if n == 0 else []),
                inc=(s_ve if n == 3 else None))
        gbb_ve = s_ve.n
        for n in range(4):
            P.emit("tensor", lambda e, n=n: e.matmul(
                pa6[n][:, :], onesf[:, :], beb[0:1, n * 512:(n + 1) * 512],
                start=True, stop=True),
                waits=([(s_ve, gbb_ve)] if n == 0 else []),
                inc=(s_mm if n == 3 else None))
        for n in range(4):
            P.emit("vector", lambda e, n=n: e.tensor_copy(
                beb[:, n * 512:(n + 1) * 512], pa6[n][:, :]),
                waits=([(s_mm, s_mm.n)] if n == 0 else []),
                inc=(s_ve if n == 3 else None))
        P.emit("gpsimd", lambda e: e.memset(stats[:, 15:16], 0.0),
               waits=[(s_ve, s_ve.n)], inc=s_gp)
        gb_ready = s_gp.n
        for k in range(16):
            P.emit("gpsimd", lambda e, k=k: e.dma_start(
                wob[:, k * 2048:(k + 1) * 2048], woT[k * 128:(k + 1) * 128, :]),
                inc=s_wo, inck=16)
        wo_ready = s_wo.n

        buf_free = {}
        out_done = {}
        for i in range(8):
            xnb6, onb, anb, xb6, fo = (xnl[i % 2], onl[i % 2], anl[i % 2],
                                       xl[i % 2], fout[i % 2])
            w = [(s_ve, buf_free.get(i - 2, 0))]
            if i == 0:
                w += [(s_sp[0], s_sp[0].n), (s_sp[1], s_sp[1].n),
                      (s_os[0], s_os[0].n), (s_os[1], s_os[1].n),
                      (s_at, s_at.n)]
            P.emit("sync", lambda e, xnb6=xnb6, i=i: e.dma_start(
                xnb6[:, :], xnsp[i, :, :]), waits=w, inc=s_l[i % 2], inck=16)
            P.emit("sync", lambda e, onb=onb, i=i: e.dma_start(
                onb[:, :], osp[:, i * 128:(i + 1) * 128], transpose=True),
                inc=s_l[i % 2], inck=16)
            P.emit("sync", lambda e, anb=anb, i=i: e.dma_start(
                anb[:, :].rearrange("p (m t) -> p m t", t=128),
                asp[:, :, i * 128:(i + 1) * 128]
                .rearrange("m p t -> p m t")), inc=s_l[i % 2], inck=16)
            P.emit("scalar", lambda e, xb6=xb6, i=i: e.dma_start(
                xb6[:, :], xs[HALO + i * 128:HALO + (i + 1) * 128, :]),
                inc=s_l[i % 2], inck=16)
            dr6 = (s_l[i % 2], s_l[i % 2].n)
            for n in range(4):
                pzi = n
                wfirst = [dr6, (s_wo, wo_ready),
                          (s_ve, buf_free.get(i - 1, 0))]
                for k in range(16):
                    P.emit("tensor", lambda e, pzi=pzi, anb=anb, k=k, n=n:
                           e.matmul(pa6[pzi][:, :],
                                    anb[:, k * 128:(k + 1) * 128],
                                    wob[:, k * 2048 + n * 512:
                                        k * 2048 + (n + 1) * 512],
                                    start=(k == 0), stop=(k == 15)),
                           waits=(wfirst if (k == 0 and n == 0) else []),
                           inc=(s_mm if (k == 15 and n == 3) else None))
            mm_here = s_mm.n
            HF = C // 2
            P.emit("vector", lambda e, xnb6=xnb6, onb=onb: e.tensor_tensor(
                ybuf[:, 0:HF], xnb6[:, 0:HF], onb[:, 0:HF], op=op.add),
                waits=[dr6])
            P.emit("vector", lambda e, xnb6=xnb6, onb=onb: e.tensor_tensor(
                ybuf[:, HF:C], xnb6[:, HF:C], onb[:, HF:C], op=op.add))
            P.dr("vector")
            P.emit("vector", lambda e: e.tensor_scalar(
                trash6[:, 0:HF], ybuf[:, 0:HF], 0.0, None, op0=op.add,
                op1=op.add, accum_out=stats[:, 0:1]))
            P.emit("vector", lambda e: e.tensor_scalar(
                trash6[:, HF:C], ybuf[:, HF:C], 0.0, None, op0=op.add,
                op1=op.add, accum_out=stats[:, 10:11]))
            P.dr("vector")
            P.emit("vector", lambda e: e.tensor_copy(
                stats[:, 30:31], stats[:, 10:11]), inc=s_ve)
            P.emit("scalar", lambda e: e.activation(
                trash6[:, C:C + HF], ybuf[:, 0:HF], AF.Square,
                accum_out=stats[:, 1:2]), waits=[(s_ve, s_ve.n)])
            P.emit("scalar", lambda e: e.activation(
                trash6[:, C + HF:2 * C], ybuf[:, HF:C], AF.Square,
                accum_out=stats[:, 11:12]), inc=s_ac)
            P.dr("vector")
            P.emit("vector", lambda e: e.tensor_tensor(
                stats[:, 12:13], stats[:, 0:1], stats[:, 10:11], op=op.add),
                waits=[(s_ac, s_ac.n)])
            P.emit("vector", lambda e: e.tensor_tensor(
                stats[:, 13:14], stats[:, 1:2], stats[:, 11:12], op=op.add))
            P.dr("vector")
            P.emit("vector", lambda e: e.tensor_scalar(
                stats[:, 2:3], stats[:, 12:13], 1.0 / C, None, op0=op.mult))
            P.emit("vector", lambda e: e.tensor_scalar(
                stats[:, 3:4], stats[:, 13:14], 1.0 / C, None, op0=op.mult))
            P.dr("vector")
            P.emit("vector", lambda e: e.tensor_scalar(
                stats[:, 28:29], stats[:, 2:3], -1.0, None, op0=op.mult))
            P.emit("vector", lambda e: e.tensor_tensor(
                stats[:, 4:5], stats[:, 2:3], stats[:, 2:3], op=op.mult))
            P.dr("vector")
            P.emit("vector", lambda e: e.tensor_tensor(
                stats[:, 5:6], stats[:, 3:4], stats[:, 4:5], op=op.subtract),
                inc=s_ve)
            P.emit("scalar", lambda e: e.activation(
                stats[:, 6:7], stats[:, 5:6], AF.Sqrt, bias=cstb[:, 1:2]),
                waits=[(s_ve, s_ve.n)], inc=s_ac)
            P.emit("vector", lambda e: e.reciprocal(stats[:, 7:8], stats[:, 6:7]),
                   waits=[(s_ac, s_ac.n)])
            P.dr("vector")
            P.emit("vector", lambda e: e.tensor_tensor(
                stats[:, 9:10], stats[:, 28:29], stats[:, 7:8], op=op.mult),
                inc=s_ve)
            P.emit("scalar", lambda e: e.activation(
                lnf[:, :], ybuf[:, :], AF.Identity, bias=stats[:, 9:10],
                scale=stats[:, 7:8]), waits=[(s_ve, s_ve.n)], inc=s_ac)
            P.emit("vector", lambda e: e.tensor_tensor(
                lnf[:, :], lnf[:, :], gbb[:, :], op=op.mult),
                waits=[(s_ac, s_ac.n), (s_gp, gb_ready)])
            P.emit("vector", lambda e, xb6=xb6: e.tensor_tensor(
                t2buf[:, :], xb6[:, :], beb[:, :], op=op.add))
            P.dr("vector")
            for n in range(4):
                P.emit("vector", lambda e, fo=fo, n=n: e.tensor_tensor(
                    fo[:, n * 512:(n + 1) * 512], lnf[:, n * 512:(n + 1) * 512],
                    pa6[n][:, :], op=op.add),
                    waits=([(s_mm, mm_here)]
                           + ([out_done[i - 2]]
                              if (i - 2) in out_done else [])
                           if n == 0 else []))
            P.dr("vector")
            for n in range(4):
                P.emit("vector", lambda e, fo=fo, n=n: e.tensor_tensor(
                    fo[:, n * 512:(n + 1) * 512], fo[:, n * 512:(n + 1) * 512],
                    t2buf[:, n * 512:(n + 1) * 512], op=op.add),
                    inc=(s_ve if n == 3 else None))
            buf_free[i] = s_ve.n
            P.emit("sync", lambda e, fo=fo, i=i: e.dma_start(
                outt[i * 128:(i + 1) * 128, :], fo[:, :]),
                waits=[(s_ve, s_ve.n)], inc=s_do[i % 2], inck=16)
            out_done[i] = (s_do[i % 2], s_do[i % 2].n)
        P.emit("sync", lambda e: e.nop(),
               waits=[(s_do[0], s_do[0].n), (s_do[1], s_do[1].n)])
        quiesce()
        P.run_block(nc)

    es.close()
    return nc


def _prep_weights(inputs):
    Wq, Wk, Wv, Wo = (inputs["L0_attn_Wq"], inputs["L0_attn_Wk"],
                      inputs["L0_attn_Wv"], inputs["L0_attn_Wo"])
    h = np.arange(C) // 128
    j = np.arange(C) % 128
    perm = h * 128 + 2 * (j % 64) + (j // 64)

    def tile4(WT):
        return np.ascontiguousarray(
            WT.reshape(16, 128, 4, 512).transpose(2, 1, 0, 3).reshape(
                4, 128, 16 * 512))
    wqT = tile4(np.ascontiguousarray(Wq.T)[:, perm])
    wkT = tile4(np.ascontiguousarray(Wk.T)[:, perm])
    wvT = tile4(np.ascontiguousarray(Wv.T))
    woT = np.ascontiguousarray(Wo.T)
    wsT = np.ascontiguousarray(
        inputs["L0_ffn_W_syn"].T.reshape(16, 128, 64, 128)
        .transpose(2, 1, 0, 3).reshape(64, 128, C))
    woutT = np.ascontiguousarray(
        inputs["L0_ffn_W_out"].T.reshape(4, 16, 128, 16, 128)
        .transpose(3, 0, 2, 1, 4).reshape(64, 128, C))
    return dict(wqT=wqT, wkT=wkT, wvT=wvT, woT=woT, wsT=wsT, woutT=woutT,
                bsynt=inputs["L0_ffn_b_syn"].reshape(1, NH),
                boutt=inputs["L0_ffn_b_out"].reshape(1, C),
                gammat=inputs["L0_ffn_gamma"].reshape(1, C),
                betat=inputs["L0_ffn_beta"].reshape(1, C))


def kernel(**inputs):
    inputs = {k: np.asarray(v, dtype=np.float32) for k, v in inputs.items()}
    x = inputs["L0_attn_x"]
    wd = _prep_weights(inputs)
    nc = build_nc()
    in_maps = []
    for c in range(8):
        b, hh = c // 2, c % 2
        if hh == 0:
            xsh = np.concatenate([x[b, :HALO], x[b, :NTOK]], axis=0)
            mf = np.full((128, 1), NEG, dtype=np.float32)
        else:
            xsh = x[b, NTOK - HALO:2 * NTOK]
            mf = np.zeros((128, 1), dtype=np.float32)
        m = {"xs": np.ascontiguousarray(xsh), "maskft": mf}
        m.update(wd)
        in_maps.append(m)
    res = run_bass_kernel_spmd(nc, in_maps, core_ids=list(range(8)))
    outp = np.empty((Bsz, T, C), dtype=np.float32)
    for c in range(8):
        b, hh = c // 2, c % 2
        outp[b, hh * NTOK:(hh + 1) * NTOK] = res.results[c]["out"]
    return outp


# version 13
# speedup vs baseline: 1.5513x; 1.3646x over previous
import contextlib
import numpy as np
import concourse.bass as bass
import concourse.mybir as mybir
from concourse.alu_op_type import AluOpType as op
from concourse.bass_utils import run_bass_kernel_spmd

F32 = mybir.dt.float32
BF16 = mybir.dt.bfloat16
FP16 = mybir.dt.float16
AF = mybir.ActivationFunctionType

Bsz, T, C = 4, 2048, 2048
H, DH, W = 16, 128, 256
NH = 8192
NTOK = 1024
HALO = 256
NTOKH = NTOK + HALO
NSEL = 22
SCALE = float(DH) ** -0.5
LN1E4 = float(np.log(10000.0))
PI = float(np.pi)
NEG = -2.0e9
BIGH = 30000.0

ENGS = ("sync", "scalar", "vector", "tensor", "gpsimd")


class Sem:
    def __init__(self, h):
        self.h = h
        self.n = 0


class Prog:
    def __init__(self):
        self.streams = {e: [] for e in ENGS}

    def emit(self, eng, fn, waits=(), inc=None, inck=1):
        if inc is not None:
            inc.n += inck
        self.streams[eng].append((fn, tuple(waits), inc, inck))

    def dr(self, eng):
        self.streams[eng].append((lambda e: e.drain(), (), None, 1))

    def run_block(self, nc):
        with nc.Block() as blk:
            for eng in ENGS:
                items = self.streams[eng]
                if not items:
                    continue

                def mk(items):
                    def body(e):
                        for fn, waits, inc, inck in items:
                            for (s, v) in waits:
                                if v > 0:
                                    e.wait_ge(s.h, v)
                            instr = fn(e)
                            if inc is not None:
                                instr.then_inc(inc.h, inck)
                    return body
                getattr(blk, eng)(mk(items))
        self.streams = {e: [] for e in ENGS}


def build_nc(debug_taps=(), upto=99, p0mask=7):
    nc = bass.Bass("TRN2", target_bir_lowering=False)
    es = contextlib.ExitStack()

    def sb(name, shape, dt):
        return es.enter_context(nc.sbuf_tensor(name, shape, dt))

    def mksem(name):
        return Sem(es.enter_context(nc.semaphore(name)))

    xs = nc.dram_tensor("xs", [NTOKH, C], F32, kind="ExternalInput")
    wqT = nc.dram_tensor("wqT", [4, 128, 16 * 512], F32, kind="ExternalInput")
    wkT = nc.dram_tensor("wkT", [4, 128, 16 * 512], F32, kind="ExternalInput")
    wvT = nc.dram_tensor("wvT", [4, 128, 16 * 512], F32, kind="ExternalInput")
    woT = nc.dram_tensor("woT", [C, C], F32, kind="ExternalInput")
    wsT = nc.dram_tensor("wsT", [64, 128, C], F32, kind="ExternalInput")
    woutT = nc.dram_tensor("woutT", [64, 128, C], F32, kind="ExternalInput")
    bsynt = nc.dram_tensor("bsynt", [1, NH], F32, kind="ExternalInput")
    boutt = nc.dram_tensor("boutt", [1, C], F32, kind="ExternalInput")
    gammat = nc.dram_tensor("gammat", [1, C], F32, kind="ExternalInput")
    betat = nc.dram_tensor("betat", [1, C], F32, kind="ExternalInput")
    maskft = nc.dram_tensor("maskft", [128, 1], F32, kind="ExternalInput")
    outt = nc.dram_tensor("out", [NTOK, C], F32, kind="ExternalOutput")

    xnsp = nc.dram_tensor("xnsp", [8, 128, C], BF16, kind="Internal")
    qsp = nc.dram_tensor("qsp", [16, 128, NTOK], BF16, kind="Internal")
    ksp = nc.dram_tensor("ksp", [16, 128, NTOKH], BF16, kind="Internal")
    vsp = nc.dram_tensor("vsp", [NTOKH, C], BF16, kind="Internal")
    asp = nc.dram_tensor("asp", [16, 128, NTOK], BF16, kind="Internal")
    osp = nc.dram_tensor("osp", [C, NTOK], BF16, kind="Internal")

    taps = {}
    for name, shape, dt in debug_taps:
        taps[name] = nc.dram_tensor(name, shape, dt, kind="ExternalOutput")

    ident = sb("ident", [128, 128], BF16)
    identh = sb("identh", [128, 128], FP16)
    onesb = sb("onesb", [1, 128], BF16)
    onesf = sb("onesf", [1, 128], F32)
    stats = sb("stats", [128, 120], F32)
    scal = sb("scal", [128, 8], F32)
    maskfb = sb("maskfb", [128, 1], F32)
    mcols = sb("mcols", [128, 64], F32)
    cstb = sb("cstb", [128, 2], F32)
    pX = contextlib.ExitStack()
    xnTc = pX.enter_context(nc.sbuf_tensor("xnTc", [128, 16 * NTOK], BF16))
    pA = contextlib.ExitStack()
    xnTh = pA.enter_context(nc.sbuf_tensor("xnTh", [128, 16 * HALO], BF16))
    cosT = pA.enter_context(nc.sbuf_tensor("cosT", [128, NTOKH], BF16))
    sinsT = pA.enter_context(nc.sbuf_tensor("sinsT", [128, NTOKH], BF16))

    s_dx = mksem("s_dx")
    s_dw = mksem("s_dw")
    s_dm = mksem("s_dm")
    s_ds = mksem("s_ds")
    s_dr = mksem("s_dr")
    s_do = [mksem("s_do0"), mksem("s_do1")]
    s_mm = mksem("s_mm")
    s_ve = mksem("s_ve")
    s_ac = mksem("s_ac")
    s_gp = mksem("s_gp")
    s_tt = mksem("s_tt")
    s_mf = mksem("s_mf")
    s_x = [mksem("s_x0"), mksem("s_x1")]
    s_sp = [mksem("s_sp0"), mksem("s_sp1")]
    s_wb = [mksem("s_wb0"), mksem("s_wb1")]
    s_ms = [mksem("s_ms0"), mksem("s_ms1")]
    s_qsp = mksem("s_qsp")
    s_vb = [mksem("s_vb0"), mksem("s_vb1")]
    s_wd = [mksem("s_wd0"), mksem("s_wd1")]
    s_at = mksem("s_at")
    s_sg = [mksem("s_sg0"), mksem("s_sg1")]
    s_bs = mksem("s_bs")
    s_s5 = [mksem("s_s50"), mksem("s_s51")]
    s_bo = mksem("s_bo")
    s_os = [mksem("s_os0"), mksem("s_os1")]
    s_l = [mksem("s_l0"), mksem("s_l1")]
    s_wo = mksem("s_wo")
    s_gb2 = mksem("s_gb2")
    s_sw = mksem("s_sw")

    DMASEMS = [s_ds, s_dr, s_do[0], s_do[1], s_mf, s_x[0], s_x[1], s_sp[0], s_sp[1],
               s_wb[0], s_wb[1], s_ms[0], s_ms[1], s_qsp, s_vb[0], s_vb[1],
               s_wd[0], s_wd[1], s_at, s_sg[0], s_sg[1], s_bs, s_s5[0],
               s_s5[1], s_bo, s_os[0], s_os[1], s_l[0], s_l[1], s_wo, s_gb2, s_sw,
               s_dx, s_dw, s_dm]

    def quiesce():
        ws = [(sm, sm.n) for sm in DMASEMS if sm.n > 0]
        if ws:
            P.emit("sync", lambda e: e.nop(), waits=ws)

    P = Prog()

    with contextlib.ExitStack() as p0:
        def p0sb(name, shape, dt):
            return p0.enter_context(nc.sbuf_tensor(name, shape, dt))
        ct = p0sb("ct", [128, NTOKH], F32)
        st = p0sb("st", [128, NTOKH], F32)
        ang = p0sb("ang", [128, 24], F32)
        tmpl = p0sb("tmpl", [128, 2048], F32)
        xb = [p0sb(f"xb{i}", [128, C], F32) for i in range(2)]
        xnb = [p0sb(f"xnb{i}", [128, C], BF16) for i in range(2)]
        trash = p0sb("trash", [128, 2 * C], FP16)
        ptp = [p0.enter_context(nc.psum_tensor(f"ptp{i}", [128, 1024], BF16))
               for i in range(2)]

        P.emit("gpsimd", lambda e: e.memset(stats[:, :], 0.0))
        P.emit("gpsimd", lambda e: e.memset(scal[:, :], 0.0))
        P.emit("gpsimd", lambda e: e.memset(ident[:, :], 0.0), inc=s_gp)
        P.emit("gpsimd", lambda e: e.affine_select(
            out=ident[:, :], in_=ident[:, :], compare_op=op.not_equal,
            fill=1.0, base=0, pattern=[[-1, 128]], channel_multiplier=1),
            waits=[(s_gp, s_gp.n)])
        P.emit("gpsimd", lambda e: e.memset(identh[:, :], 0.0), inc=s_gp)
        P.emit("gpsimd", lambda e: e.affine_select(
            out=identh[:, :], in_=identh[:, :], compare_op=op.not_equal,
            fill=1.0, base=0, pattern=[[-1, 128]], channel_multiplier=1),
            waits=[(s_gp, s_gp.n)])
        P.emit("gpsimd", lambda e: e.memset(onesb[:, :], 1.0))
        P.emit("gpsimd", lambda e: e.memset(onesf[:, :], 1.0))
        P.emit("gpsimd", lambda e: e.memset(stats[:, 10:11], 1.0))
        P.emit("gpsimd", lambda e: e.memset(cstb[:, 0:1], PI / 2))
        P.emit("gpsimd", lambda e: e.memset(cstb[:, 1:2], 1e-5))
        P.emit("gpsimd", lambda e: e.iota(
            stats[:, 14:15], [[1, 1]], channel_multiplier=1,
            allow_small_or_imprecise_dtypes=True), inc=s_gp)

        P.emit("sync", lambda e: e.dma_start(maskfb[:, :], maskft[:, :]),
               inc=s_mf, inck=16)

        P.emit("vector", lambda e: e.tensor_scalar(
            stats[:, 15:16], stats[:, 14:15], 64.0, None, op0=op.is_ge),
            waits=[(s_gp, s_gp.n)])
        P.dr("vector")
        P.emit("vector", lambda e: e.scalar_tensor_tensor(
            stats[:, 13:14], stats[:, 15:16], -64.0, stats[:, 14:15],
            op0=op.mult, op1=op.add))
        P.emit("vector", lambda e: e.tensor_scalar(
            stats[:, 12:13], stats[:, 15:16], 2.0, -1.0, op0=op.mult, op1=op.add),
            inc=s_ve)
        P.dr("vector")

        P.emit("scalar", lambda e: e.activation(
            stats[:, 11:12], stats[:, 13:14], AF.Exp, scale=-LN1E4 / 64.0),
            waits=[(s_ve, s_ve.n)])
        P.dr("scalar")
        P.emit("scalar", lambda e: e.activation(
            ct[:, 1:2], stats[:, 11:12], AF.Sin, bias=cstb[:, 0:1]))
        P.emit("scalar", lambda e: e.activation(
            st[:, 1:2], stats[:, 11:12], AF.Sin), inc=s_ac)

        def ladder(e):
            e.memset(ct[:, 0:1], 1.0)
            e.memset(st[:, 0:1], 0.0)
            e.drain()
            wl = 2
            while wl < NTOKH:
                L = min(wl, NTOKH - wl)
                c1, s1 = ct[:, 1:2], st[:, 1:2]
                cp, sp = ct[:, wl - 1:wl], st[:, wl - 1:wl]
                e.tensor_tensor(tmpl[:, 0:1], cp, c1, op=op.mult)
                e.tensor_tensor(tmpl[:, 1:2], sp, s1, op=op.mult)
                e.tensor_tensor(tmpl[:, 2:3], cp, s1, op=op.mult)
                e.tensor_tensor(tmpl[:, 3:4], sp, c1, op=op.mult)
                e.drain()
                e.tensor_tensor(ang[:, 0:1], tmpl[:, 0:1], tmpl[:, 1:2],
                                op=op.subtract)
                e.tensor_tensor(ang[:, 1:2], tmpl[:, 2:3], tmpl[:, 3:4],
                                op=op.add)
                e.drain()
                cr, sr = ang[:, 0:1], ang[:, 1:2]
                ta, tb = tmpl[:, 0:L], tmpl[:, 1024:1024 + L]
                e.tensor_scalar(ta, st[:, 0:L], sr, None, op0=op.mult)
                e.tensor_scalar(tb, st[:, 0:L], cr, None, op0=op.mult)
                e.drain()
                e.scalar_tensor_tensor(ct[:, wl:wl + L], ct[:, 0:L], cr,
                                       ta, op0=op.mult, op1=op.subtract)
                e.scalar_tensor_tensor(st[:, wl:wl + L], ct[:, 0:L], sr,
                                       tb, op0=op.mult, op1=op.add)
                e.drain()
                wl *= 2
            e.tensor_copy(cosT[:, :], ct[:, :])
            e.drain()
            return e.tensor_scalar(sinsT[:, :], st[:, :], stats[:, 12:13], None,
                                   op0=op.mult)
        if p0mask & 1:
            P.emit("vector", ladder, waits=[(s_ac, s_ac.n)], inc=s_ve)
        else:
            P.emit("vector", lambda e: e.tensor_copy(cosT[:, :], ct[:, :]),
                   waits=[(s_ac, s_ac.n)])
            P.emit("vector", lambda e: e.tensor_copy(sinsT[:, :], st[:, :]),
                   inc=s_ve)

        cast_done = {}
        drain_done = {}
        tp_done = {}
        spill_xn = {}
        for i in range(10):
            b, nb = xb[i % 2], xnb[i % 2]
            w = [(s_ac, cast_done[i - 2])] if i >= 2 else []
            P.emit("sync", lambda e, b=b, i=i: e.dma_start(
                b[:, :], xs[128 * i:128 * (i + 1), :]), waits=w,
                inc=s_x[i % 2], inck=16)
            P.emit("vector", lambda e, b=b: e.tensor_scalar(
                trash[:, 0:C], b[:, :], 0.0, None, op0=op.add, op1=op.add,
                accum_out=stats[:, 0:1]), waits=[(s_x[i % 2], s_x[i % 2].n)])
            P.emit("scalar", lambda e, b=b: e.activation(
                trash[:, C:2 * C], b[:, :], AF.Square,
                accum_out=stats[:, 1:2]),
                waits=[(s_x[i % 2], s_x[i % 2].n)], inc=s_ac)
            P.dr("vector")
            P.emit("vector", lambda e: e.tensor_scalar(
                stats[:, 2:3], stats[:, 0:1], 1.0 / C, None, op0=op.mult),
                waits=[(s_ac, s_ac.n)])
            P.emit("vector", lambda e: e.tensor_scalar(
                stats[:, 3:4], stats[:, 1:2], 1.0 / C, None, op0=op.mult))
            P.dr("vector")
            P.emit("vector", lambda e: e.tensor_scalar(
                stats[:, 28:29], stats[:, 2:3], -1.0, None, op0=op.mult))
            P.emit("vector", lambda e: e.tensor_tensor(
                stats[:, 4:5], stats[:, 2:3], stats[:, 2:3], op=op.mult))
            P.dr("vector")
            P.emit("vector", lambda e: e.tensor_tensor(
                stats[:, 5:6], stats[:, 3:4], stats[:, 4:5], op=op.subtract),
                inc=s_ve)
            P.emit("scalar", lambda e: e.activation(
                stats[:, 6:7], stats[:, 5:6], AF.Sqrt, bias=cstb[:, 1:2]),
                waits=[(s_ve, s_ve.n)], inc=s_ac)
            P.emit("vector", lambda e: e.reciprocal(stats[:, 7:8], stats[:, 6:7]),
                   waits=[(s_ac, s_ac.n)])
            P.dr("vector")
            P.emit("vector", lambda e: e.tensor_tensor(
                stats[:, 9:10], stats[:, 28:29], stats[:, 7:8], op=op.mult),
                inc=s_ve)
            wcast = [(s_ve, s_ve.n)]
            if i >= 2:
                wcast.append((s_tt, tp_done[i - 2]))
                if (i - 2) in spill_xn:
                    wcast.append(spill_xn[i - 2])
            P.emit("scalar", lambda e, b=b, nb=nb: e.activation(
                nb[:, :], b[:, :], AF.Identity, bias=stats[:, 9:10],
                scale=stats[:, 7:8]), waits=wcast, inc=s_ac)
            cast_done[i] = s_ac.n
            if i >= 2:
                P.emit("sync", lambda e, nb=nb, i=i: e.dma_start(
                    xnsp[i - 2, :, :], nb[:, :]), waits=[(s_ac, s_ac.n)],
                    inc=s_sp[i % 2], inck=16)
                spill_xn[i] = (s_sp[i % 2], s_sp[i % 2].n)
            dstf = xnTc if i >= 2 else xnTh
            ncols = NTOK if i >= 2 else HALO
            toff = (i - 2) * 128 if i >= 2 else i * 128
            for g in range(2):
                pt = ptp[g]
                w = [(s_ac, cast_done[i])]
                if (i, g) != (0, 0) and (i, g) != (0, 1):
                    w.append((s_ve, drain_done[(i - 1, g)]))
                for kk in range(8):
                    k = 8 * g + kk
                    P.emit("tensor", lambda e, pt=pt, nb=nb, k=k, kk=kk: e.transpose(
                        pt[:, 128 * kk:128 * (kk + 1)],
                        nb[:, 128 * k:128 * (k + 1)], ident[:, :]),
                        waits=(w if kk == 0 else []), inc=s_tt)

                def drain(e, pt=pt, dstf=dstf, ncols=ncols, toff=toff, g=g):
                    dst = dstf[:, :].rearrange("p (k t) -> p k t", t=ncols)[
                        :, 8 * g:8 * g + 8, toff:toff + 128]
                    return e.tensor_copy(dst, pt[:, :])
                P.emit("vector", drain, waits=[(s_tt, s_tt.n)], inc=s_ve)
                drain_done[(i, g)] = s_ve.n
            tp_done[i] = s_tt.n
        quiesce()
        P.run_block(nc)
    if upto <= 0:
        pA.close(); pX.close(); es.close()
        return nc

    with contextlib.ExitStack() as p2:
        def p2sb(name, shape, dt):
            return p2.enter_context(nc.sbuf_tensor(name, shape, dt))
        wbuf = [p2sb(f"wbuf{i}", [128, 16 * 512], BF16) for i in range(2)]
        qkb = p2sb("qkb", [128, NTOKH], BF16)
        rsw = p2sb("rsw", [128, NTOKH], BF16)
        rt1 = p2sb("rt1", [128, NTOKH], BF16)
        rt2 = p2sb("rt2", [128, NTOKH], BF16)
        vb = [p2sb(f"vb{i}", [128, 512], BF16) for i in range(2)]
        mst = [p2sb(f"mst{i}", [128, 2 * C], F32) for i in range(2)]
        mtr = p2sb("mtr", [128, 2 * C], FP16)
        pj = [p2.enter_context(nc.psum_tensor(f"pj{i}", [128, 512], F32))
              for i in range(4)]

        mean_ms = {}
        for i in range(65):
            if i < 64:
                ms = mst[i % 2]
                src = (wsT[2 * i:2 * i + 2] if i < 32
                       else woutT[2 * (i - 32):2 * (i - 32) + 2])
                P.emit("scalar", lambda e, ms=ms, src=src: e.dma_start(
                    ms[:, :].rearrange("p (j o) -> p j o", o=C),
                    src.rearrange("j p o -> p j o")),
                    inc=s_ms[i % 2], inck=16)
                mean_ms[i] = s_ms[i % 2].n
            if i >= 1:
                j = i - 1
                msr = mst[j % 2]
                P.emit("scalar", lambda e, msr=msr, j=j: e.activation(
                    mtr[:, :], msr[:, :], AF.Abs, accum_out=mcols[:, j:j + 1]),
                    waits=[(s_ms[j % 2], mean_ms[j])], inc=s_ac)
        mean_done_ac = s_ac.n

        wts = {"q": wqT, "k": wkT, "v": wvT}
        NGR = {"q": [(256, 512), (768, 512)],
               "k": [(0, 256), (256, 512), (768, 512)]}
        chunk_use = {}
        pj_drain = {}
        qkb_spill = [0]
        vb_spill = {}

        def xnT_slice(k, t0, tl):
            if t0 < HALO:
                assert t0 + tl <= HALO
                return xnTh[:, k * HALO + t0: k * HALO + t0 + tl]
            return xnTc[:, k * NTOK + (t0 - HALO): k * NTOK + (t0 - HALO) + tl]

        cidx = 0
        for wi, wn in enumerate("qkv"):
            for c in range(4):
                wb = wbuf[cidx % 2]
                w = [(s_mm, chunk_use.get(cidx - 2, 0))]
                P.emit("gpsimd", lambda e, wb=wb, wn=wn, c=c: e.dma_start(
                    wb[:, :], wts[wn][c]),
                    waits=w, inc=s_wb[cidx % 2], inck=16)
                wb_ready = (s_wb[cidx % 2], s_wb[cidx % 2].n)
                if wn in "qk":
                    ntok = NTOK if wn == "q" else NTOKH
                    spt = qsp if wn == "q" else ksp
                    for m in range(4):
                        mg = 4 * c + m
                        pjs_used = []
                        for gi, (t0, tl) in enumerate(NGR[wn]):
                            pji = (mg * 3 + gi) % 4
                            pjs_used.append((pji, t0, tl))
                            wfirst = [wb_ready,
                                      (s_ve, pj_drain.get(pji, 0))]
                            for k in range(16):
                                P.emit("tensor",
                                       lambda e, pji=pji, tl=tl, k=k, t0=t0,
                                       wb=wb, m=m: e.matmul(
                                           pj[pji][:, 0:tl],
                                           wb[:, k * 512 + m * 128:
                                              k * 512 + (m + 1) * 128],
                                           xnT_slice(k, t0, tl),
                                           start=(k == 0), stop=(k == 15)),
                                       waits=(wfirst if k == 0 else []),
                                       inc=(s_mm if k == 15 else None))
                        mm_here = s_mm.n
                        first_drain = True
                        for (pji, t0, tl) in pjs_used:
                            off = t0 if wn == "k" else t0 - HALO
                            wd = [(s_mm, mm_here)]
                            if first_drain:
                                wd.append((s_qsp, qkb_spill[0]))
                                first_drain = False
                            lastd = (pji == pjs_used[-1][0])
                            P.emit("vector", lambda e, pji=pji, off=off, tl=tl:
                                   e.tensor_copy(qkb[:, off:off + tl],
                                                 pj[pji][:, 0:tl]),
                                   waits=wd,
                                   inc=(s_ve if (pji, t0, tl) == pjs_used[-1]
                                        else None))
                        for (pji, t0, tl) in pjs_used:
                            pj_drain[pji] = s_ve.n
                        P.emit("sync", lambda e, ntok=ntok: e.dma_start(
                            rsw[0:64, 0:ntok], qkb[64:128, 0:ntok]),
                            waits=[(s_ve, s_ve.n), (s_qsp, qkb_spill[0])],
                            inc=s_sw, inck=16)
                        P.emit("sync", lambda e, ntok=ntok: e.dma_start(
                            rsw[64:128, 0:ntok], qkb[0:64, 0:ntok]),
                            inc=s_sw, inck=16)
                        tsl = (HALO, NTOKH) if wn == "q" else (0, NTOKH)
                        P.emit("vector", lambda e, ntok=ntok, tsl=tsl: e.tensor_tensor(
                            rt1[:, 0:ntok], rsw[:, 0:ntok],
                            sinsT[:, tsl[0]:tsl[1]], op=op.mult),
                            waits=[(s_sw, s_sw.n)])
                        P.emit("vector", lambda e, ntok=ntok, tsl=tsl: e.tensor_tensor(
                            rt2[:, 0:ntok], qkb[:, 0:ntok],
                            cosT[:, tsl[0]:tsl[1]], op=op.mult))
                        P.dr("vector")
                        P.emit("vector", lambda e, ntok=ntok: e.tensor_tensor(
                            qkb[:, 0:ntok], rt2[:, 0:ntok], rt1[:, 0:ntok],
                            op=op.add), inc=s_ve)
                        P.emit("sync", lambda e, spt=spt, mg=mg, ntok=ntok:
                               e.dma_start(spt[mg, :, 0:ntok], qkb[:, 0:ntok]),
                               waits=[(s_ve, s_ve.n)], inc=s_qsp, inck=16)
                        qkb_spill[0] = s_qsp.n
                        chunk_use[cidx] = s_mm.n
                else:
                    for m in range(10):
                        pji = m % 2
                        vbi = vb[m % 2]
                        wfirst = [wb_ready, (s_ve, pj_drain.get(pji, 0))]
                        for k in range(16):
                            t0 = m * 128
                            P.emit("tensor",
                                   lambda e, pji=pji, k=k, t0=t0, wb=wb:
                                   e.matmul(pj[pji][:, :],
                                            xnT_slice(k, t0, 128),
                                            wb[:, k * 512:(k + 1) * 512],
                                            start=(k == 0), stop=(k == 15)),
                                   waits=(wfirst if k == 0 else []),
                                   inc=(s_mm if k == 15 else None))
                        P.emit("vector", lambda e, pji=pji, vbi=vbi: e.tensor_copy(
                            vbi[:, :], pj[pji][:, :]),
                            waits=[(s_mm, s_mm.n)]
                            + ([vb_spill[m % 2]] if (m % 2) in vb_spill else []),
                            inc=s_ve)
                        pj_drain[pji] = s_ve.n
                        P.emit("sync", lambda e, vbi=vbi, m=m, c=c: e.dma_start(
                            vsp[m * 128:(m + 1) * 128, c * 512:(c + 1) * 512],
                            vbi[:, :]), waits=[(s_ve, s_ve.n)],
                            inc=s_vb[m % 2], inck=16)
                        vb_spill[m % 2] = (s_vb[m % 2], s_vb[m % 2].n)
                    chunk_use[cidx] = s_mm.n
                cidx += 1
        quiesce()
        P.run_block(nc)
    pA.close()
    if upto <= 2:
        pX.close(); es.close()
        return nc

    with contextlib.ExitStack() as p3:
        def p3sb(name, shape, dt):
            return p3.enter_context(nc.sbuf_tensor(name, shape, dt))
        qwin = [p3sb(f"qwin{i}", [128, 16 * 256], BF16) for i in range(2)]
        kwin = [p3sb(f"kwin{i}", [128, 16 * 512], BF16) for i in range(2)]
        vwin = [p3sb(f"vwin{i}", [128, 4 * 2048], BF16) for i in range(2)]
        probs = [p3sb(f"probs{i}", [128, 1024], BF16) for i in range(3)]
        pTs = [p3sb(f"pTs{i}", [128, 1024], BF16) for i in range(2)]
        atile = p3sb("atile", [128, 16 * 256], BF16)
        mt = [p3sb(f"mt{i}", [128, 256], F32) for i in range(2)]
        psc = [p3.enter_context(nc.psum_tensor(f"psc{i}", [128, 512], F32))
               for i in range(4)]
        ppt = [p3.enter_context(nc.psum_tensor(f"ppt{i}", [128, 1024], BF16))
               for i in range(2)]
        pao = [p3.enter_context(nc.psum_tensor(f"pao{i}", [128, 512], F32))
               for i in range(2)]

        for j in range(2):
            P.emit("gpsimd", lambda e, j=j: e.memset(mt[j][:, :], 0.0), inc=s_gp)
            P.emit("gpsimd", lambda e, j=j: e.affine_select(
                out=mt[j][:, :], in_=mt[j][:, :], compare_op=op.is_ge,
                fill=NEG, base=j * 128, pattern=[[-1, 256]],
                channel_multiplier=1), waits=[(s_gp, s_gp.n)], inc=s_gp)
        mask_ready = s_gp.n

        win_used = {}
        sc_read = {}
        pt_drain = {}
        ao_drain = {}
        probs_tt = {0: 0, 1: 0, 2: 0}
        pts_mm = {0: 0, 1: 0}
        asp_prev = [0]
        spills_done = [(s_qsp, s_qsp.n), vb_spill[0], vb_spill[1]]
        for ci in range(4):
            qw, kw, vw = qwin[ci % 2], kwin[ci % 2], vwin[ci % 2]
            wreuse = [(s_mm, win_used.get(ci - 2, 0))]
            if ci == 0:
                wreuse += spills_done
            P.emit("sync", lambda e, qw=qw, ci=ci: e.dma_start(
                qw[:, :].rearrange("p (m t) -> p m t", t=256),
                qsp[:, :, ci * 256:(ci + 1) * 256]
                .rearrange("m p t -> p m t")), waits=wreuse,
                inc=s_wd[ci % 2], inck=16)
            P.emit("sync", lambda e, kw=kw, ci=ci: e.dma_start(
                kw[:, :].rearrange("p (m t) -> p m t", t=512),
                ksp[:, :, ci * 256:ci * 256 + 512]
                .rearrange("m p t -> p m t")), inc=s_wd[ci % 2], inck=16)
            P.emit("sync", lambda e, vw=vw, ci=ci: e.dma_start(
                vw[:, :].rearrange("p (m t) -> p m t", t=2048),
                vsp[ci * 256:ci * 256 + 512, :]
                .rearrange("(m p) t -> p m t", p=128)),
                inc=s_wd[ci % 2], inck=16)
            dr_here = (s_wd[ci % 2], s_wd[ci % 2].n)
            norm_mark = {}

            def emitAB(h, qw=qw, kw=kw, ci=ci, dr_here=dr_here):
                b0, b1 = (2 * (h % 2)), (2 * (h % 2) + 1)
                p = h % 2
                pi = h % 3
                prb = probs[pi]
                for j, bj in enumerate((b0, b1)):
                    wfirst = [dr_here, (s_ac, sc_read.get(bj, 0))]
                    P.emit("tensor", lambda e, bj=bj, qw=qw, kw=kw, h=h, j=j:
                           e.matmul(psc[bj][:, :],
                                    qw[:, h * 256 + j * 128:
                                       h * 256 + (j + 1) * 128],
                                    kw[:, h * 512:(h + 1) * 512],
                                    start=True, stop=True), waits=wfirst,
                           inc=(s_mm if j == 1 else None))
                mm_here = s_mm.n
                nmask = 4 if ci == 0 else 2
                cnt_m = [0]
                for j, bj in enumerate((b0, b1)):
                    cnt_m[0] += 1
                    P.emit("vector", lambda e, bj=bj, j=j: e.tensor_tensor(
                        psc[bj][:, 256:512], psc[bj][:, 256:512], mt[j][:, :],
                        op=op.add), waits=([(s_mm, mm_here), (s_gp, mask_ready),
                                            (s_mf, 16)]
                                           if j == 0 else []),
                        inc=(s_ve if cnt_m[0] == nmask else None))
                    if ci == 0:
                        cnt_m[0] += 1
                        P.emit("vector", lambda e, bj=bj: e.tensor_scalar(
                            psc[bj][:, 0:256], psc[bj][:, 0:256], maskfb[:, :],
                            None, op0=op.add),
                            inc=(s_ve if cnt_m[0] == nmask else None))
                for j, bj in enumerate((b0, b1)):
                    P.emit("scalar", lambda e, bj=bj, j=j, prb=prb, p=p:
                           e.activation(
                               prb[:, j * 512:(j + 1) * 512], psc[bj][:, :],
                               AF.Exp, scale=SCALE,
                               accum_out=stats[:, 18 + 2 * p + j:19 + 2 * p + j]),
                           waits=([(s_ve, s_ve.n), (s_tt, probs_tt[pi])]
                                  if j == 0 else []),
                           inc=(s_ac if j == 1 else None))
                for j, bj in enumerate((b0, b1)):
                    sc_read[bj] = s_ac.n
                P.dr("vector")
                P.emit("vector", lambda e, p=p: e.reciprocal(
                    stats[:, 22 + 2 * p:24 + 2 * p],
                    stats[:, 18 + 2 * p:20 + 2 * p]), waits=[(s_ac, s_ac.n)])
                P.dr("vector")
                for j in range(2):
                    P.emit("vector", lambda e, j=j, prb=prb, p=p: e.tensor_scalar(
                        prb[:, j * 512:(j + 1) * 512],
                        prb[:, j * 512:(j + 1) * 512],
                        stats[:, 22 + 2 * p + j:23 + 2 * p + j], None,
                        op0=op.mult),
                        inc=(s_ve if j == 1 else None))
                norm_mark[h] = s_ve.n

            def emitCD(h, vw=vw):
                p = h % 2
                pi = h % 3
                prb = probs[pi]
                pb = ppt[h % 2]
                wfirst = [(s_ve, norm_mark[h]), (s_ve, pt_drain.get(h % 2, 0))]
                for j in range(2):
                    for bb in range(4):
                        P.emit("tensor", lambda e, pb=pb, j=j, bb=bb, prb=prb:
                               e.transpose(
                                   pb[:, bb * 256 + j * 128:
                                      bb * 256 + (j + 1) * 128],
                                   prb[:, j * 512 + bb * 128:
                                       j * 512 + (bb + 1) * 128],
                                   ident[:, :]),
                               waits=(wfirst if (j, bb) == (0, 0) else []),
                               inc=(s_tt if (j, bb) == (1, 3) else None))
                probs_tt[pi] = s_tt.n
                P.emit("vector", lambda e, pb=pb, p=p: e.tensor_copy(
                    pTs[p][:, :], pb[:, :]),
                    waits=[(s_tt, s_tt.n), (s_mm, pts_mm[p])], inc=s_ve)
                pt_drain[h % 2] = s_ve.n
                po = pao[h % 2]
                wfirst = [(s_ve, s_ve.n), (s_ve, ao_drain.get(h % 2, 0))]
                for bb in range(4):
                    P.emit("tensor", lambda e, po=po, vw=vw, h=h, bb=bb, p=p:
                           e.matmul(
                               po[:, 0:256], vw[:, bb * 2048 + h * 128:
                                                bb * 2048 + (h + 1) * 128],
                               pTs[p][:, bb * 256:(bb + 1) * 256],
                               start=(bb == 0), stop=(bb == 3)),
                           waits=(wfirst if bb == 0 else []),
                           inc=(s_mm if bb == 3 else None))
                pts_mm[p] = s_mm.n
                wad = [(s_mm, s_mm.n)]
                if h == 0:
                    wad.append((s_at, asp_prev[0]))
                P.emit("vector", lambda e, po=po, h=h: e.tensor_copy(
                    atile[:, h * 256:(h + 1) * 256], po[:, 0:256]),
                    waits=wad, inc=s_ve)
                ao_drain[h % 2] = s_ve.n

            emitAB(0)
            emitAB(1)
            for h in range(16):
                if h + 2 < 16:
                    emitAB(h + 2)
                emitCD(h)
            win_used[ci] = s_mm.n
            P.emit("sync", lambda e, ci=ci: e.dma_start(
                asp[:, :, ci * 256:(ci + 1) * 256]
                .rearrange("m p t -> p m t"),
                atile[:, :].rearrange("p (m t) -> p m t", t=256)),
                waits=[(s_ve, s_ve.n)], inc=s_at, inck=16)
            asp_prev[0] = s_at.n
        quiesce()
        P.run_block(nc)
    if upto <= 3:
        pX.close(); es.close()
        return nc

    pArena = contextlib.ExitStack()
    arena_t = pArena.enter_context(nc.sbuf_tensor("arena", [128, 8 * NH], FP16, side="right"))
    p4 = contextlib.ExitStack()
    with p4:
        def p4sb(name, shape, dt):
            return p4.enter_context(nc.sbuf_tensor(name, shape, dt))
        selscr = p4sb("selscr", [128, 4096], FP16)
        gbuf = p4sb("gbuf", [128, 2048], BF16)
        stg = [p4sb(f"stg{i}", [128, 2048], F32) for i in range(2)]
        tern = [p4sb(f"tern{i}", [128, 16 * 256], BF16) for i in range(2)]
        bsb = p4sb("bsb", [1, 256], BF16)
        bsb2 = p4sb("bsb2", [1, 256], BF16)
        pz = [p4.enter_context(nc.psum_tensor(f"pz{i}", [128, 512], F32))
              for i in range(4)]
        phh = [p4.enter_context(nc.psum_tensor(f"phh{i}", [128, 1024], FP16))
               for i in range(2)]
        pm = p4.enter_context(nc.psum_tensor("pm", [1, 128], F32))
        pm2 = p4.enter_context(nc.psum_tensor("pm2", [128, 64], F32))

        P.emit("tensor", lambda e: e.matmul(pm[:, 0:64], stats[:, 10:11],
                                            mcols[:, :], start=True, stop=True),
               waits=[(s_ac, mean_done_ac)], inc=s_mm)
        P.emit("vector", lambda e: e.tensor_scalar(
            pm2[0:1, 0:32], pm[0:1, 0:32], 0.0, None, op0=op.add,
            op1=op.add, accum_out=scal[0:1, 0:1]), waits=[(s_mm, s_mm.n)])
        P.dr("vector")
        P.emit("vector", lambda e: e.tensor_scalar(
            pm2[0:1, 0:32], pm[0:1, 32:64], 0.0, None, op0=op.add,
            op1=op.add, accum_out=scal[0:1, 1:2]))
        P.dr("vector")
        P.emit("vector", lambda e: e.tensor_scalar(
            scal[0:1, 2:4], scal[0:1, 0:2], 1.0 / (NH * C), None, op0=op.mult))
        P.dr("vector")
        P.emit("vector", lambda e: e.tensor_scalar(
            scal[0:1, 4:6], scal[0:1, 2:4], 1e-8, 0.5, op0=op.add, op1=op.mult))
        P.emit("vector", lambda e: e.tensor_scalar(
            scal[0:1, 6:8], scal[0:1, 2:4], -1.0, None, op0=op.mult), inc=s_ve)
        P.dr("vector")
        P.emit("tensor", lambda e: e.matmul(
            pm2[:, 0:6], onesf[:, :], scal[0:1, 2:8], start=True, stop=True),
            waits=[(s_ve, s_ve.n)], inc=s_mm)
        P.emit("vector", lambda e: e.tensor_copy(scal[:, 2:8], pm2[:, 0:6]),
               waits=[(s_mm, s_mm.n)], inc=s_ve)
        P.emit("gpsimd", lambda e: e.memset(stats[:, 31:32], 0.0),
               waits=[(s_ve, s_ve.n)], inc=s_gp)
        P.emit("vector", lambda e: e.tensor_scalar(
            stats[:, 24:25], scal[:, 4:5], -1.0, None, op0=op.mult),
            waits=[(s_gp, s_gp.n)])
        P.emit("vector", lambda e: e.tensor_scalar(
            stats[:, 26:27], scal[:, 5:6], -1.0, None, op0=op.mult))
        P.emit("vector", lambda e: e.reciprocal(stats[:, 25:26], scal[:, 6:7]))
        P.emit("vector", lambda e: e.reciprocal(stats[:, 27:28], scal[:, 7:8]),
               inc=s_ve)
        mean_ready = s_ve.n

        q_done = {}
        stg_used = {}
        prev_stt = [0]
        bias_prev = [0]
        tap_done = [0]
        z_drained = {}
        sel_done = {}
        pz_drain = {}
        sig_ready = 0

        def zq(t, q):
            return arena_t[:, t * NH + q * 2048: t * NH + (q + 1) * 2048]

        def shalf(i):
            return selscr[:, (i % 2) * 2048:(i % 2) * 2048 + 2048]

        def emit_sel_all():
            if "zraw" in taps:
                for t in range(8):
                    P.emit("sync", lambda e, t=t: e.dma_start(
                        taps["zraw"][t, :, :],
                        arena_t[:, t * NH:(t + 1) * NH]),
                        waits=[(s_ac, z_drained[7])], inc=s_ds, inck=16)
                tap_done[0] = s_ds.n
            wz = [(s_ac, z_drained[7]), (s_ds, tap_done[0])]
            GE, LO, HI, MID, MU, CNT = 16, 32, 40, 48, 56, 112
            GENA, NMA, GEND = 0, 56, 60

            def ACC(q, t):
                return 64 + q * 8 + t

            def sc(c, w=1):
                return stats[:, c:c + w]
            first = True
            for t in range(4):
                for q in range(4):
                    if (4 * t + q) % 2 == 0 and not first:
                        P.dr("vector")
                    P.emit("vector", lambda e, t=t, q=q: e.tensor_scalar(
                        shalf(4 * t + q), zq(t, q), 0.0, None, op0=op.add,
                        op1=op.add, accum_out=sc(ACC(q, t))),
                        waits=(wz if first else []))
                    first = False
            P.dr("vector")
            for q in range(4):
                P.emit("scalar", lambda e, q=q: e.activation(
                    gbuf[:, :], zq(0, q), AF.Square,
                    accum_out=sc(4 + q)),
                    inc=(s_ac if q == 3 else None))
            sig_ac = s_ac.n
            for t in range(4, 8):
                for q in range(4):
                    P.emit("scalar", lambda e, t=t, q=q: e.activation(
                        gbuf[:, :], zq(t, q), AF.Identity,
                        accum_out=sc(ACC(q, t))),
                        inc=(s_ac if (t, q) == (7, 3) else None))
            act_mu = s_ac.n
            P.emit("vector", lambda e: e.tensor_tensor(
                sc(96, 8), sc(64, 8), sc(72, 8), op=op.add),
                waits=[(s_ac, act_mu)])
            P.emit("vector", lambda e: e.tensor_tensor(
                sc(104, 8), sc(80, 8), sc(88, 8), op=op.add))
            P.dr("vector")
            P.emit("vector", lambda e: e.tensor_tensor(
                sc(112, 8), sc(96, 8), sc(104, 8), op=op.add))
            P.dr("vector")
            P.emit("vector", lambda e: e.tensor_scalar(
                sc(MU, 8), sc(112, 8), 1.0 / NH, None, op0=op.mult))
            P.dr("vector")
            P.emit("vector", lambda e: e.tensor_tensor(
                sc(8), sc(4), sc(5), op=op.add))
            P.emit("vector", lambda e: e.tensor_tensor(
                sc(9), sc(6), sc(7), op=op.add))
            P.emit("vector", lambda e: e.tensor_tensor(
                sc(11), sc(MU), sc(MU), op=op.mult))
            P.dr("vector")
            P.emit("vector", lambda e: e.tensor_tensor(
                sc(12), sc(8), sc(9), op=op.add))
            P.dr("vector")
            P.emit("vector", lambda e: e.tensor_scalar(
                sc(13), sc(12), 1.0 / NH, None, op0=op.mult))
            P.dr("vector")
            P.emit("vector", lambda e: e.tensor_tensor(
                sc(14), sc(13), sc(11), op=op.subtract),
                inc=s_ve)
            P.emit("tensor", lambda e: e.matmul(
                pm[:, 2:3], stats[:, 10:11], sc(14),
                start=True, stop=True), waits=[(s_ve, s_ve.n)], inc=s_mm)
            P.emit("vector", lambda e: e.tensor_scalar(
                scal[0:1, 0:1], pm[:, 2:3], 1.0 / 128, None, op0=op.mult),
                waits=[(s_mm, s_mm.n)], inc=s_ve)
            P.emit("scalar", lambda e: e.activation(
                scal[0:1, 1:2], scal[0:1, 0:1], AF.Sqrt),
                waits=[(s_ve, s_ve.n)], inc=s_ac)
            P.emit("tensor", lambda e: e.matmul(
                pm2[:, 0:2], onesf[:, :], scal[0:1, 0:2], start=True, stop=True),
                waits=[(s_ac, s_ac.n)], inc=s_mm)
            P.emit("vector", lambda e: e.tensor_copy(scal[:, 0:2], pm2[:, 0:2]),
                   waits=[(s_mm, s_mm.n)], inc=s_ve)
            P.emit("gpsimd", lambda e: e.memset(stats[:, 30:31], 0.0),
                   waits=[(s_ve, s_ve.n)], inc=s_gp)
            sig_ready = s_gp.n
            for t in range(8):
                P.emit("vector", lambda e, t=t: e.scalar_tensor_tensor(
                    stats[:, LO + t:LO + t + 1], scal[:, 1:2], 1.0,
                    stats[:, MU + t:MU + t + 1], op0=op.mult, op1=op.add),
                    waits=([(s_gp, sig_ready)] if t == 0 else []))
            for t in range(8):
                P.emit("vector", lambda e, t=t: e.scalar_tensor_tensor(
                    stats[:, HI + t:HI + t + 1], scal[:, 1:2], 1.6,
                    stats[:, MU + t:MU + t + 1], op0=op.mult, op1=op.add))
            P.dr("vector")
            P.emit("vector", lambda e: e.tensor_tensor(
                sc(100, 4), sc(LO + 4, 4), sc(HI + 4, 4), op=op.add))
            P.dr("vector")
            P.emit("vector", lambda e: e.tensor_scalar(
                sc(MID + 4, 4), sc(100, 4), 0.5, None, op0=op.mult))
            P.emit("vector", lambda e: e.tensor_scalar(
                sc(NMA, 4), sc(100, 4), -0.5, None, op0=op.mult), inc=s_ve)
            a_state = s_ve.n
            P.dr("vector")
            for it in range(NSEL):
                P.emit("vector", lambda e: e.tensor_tensor(
                    sc(96, 4), sc(LO, 4), sc(HI, 4), op=op.add))
                P.dr("vector")
                P.emit("vector", lambda e: e.tensor_scalar(
                    sc(MID, 4), sc(96, 4), 0.5, None, op0=op.mult))
                P.dr("vector")
                for t in range(4):
                    for q in range(4):
                        if (4 * t + q) % 2 == 0:
                            P.dr("vector")
                        P.emit("vector", lambda e, t=t, q=q:
                               e.tensor_scalar(
                                   shalf(4 * t + q), zq(t, q),
                                   stats[:, MID + t:MID + t + 1], None,
                                   op0=op.is_ge, op1=op.add,
                                   accum_out=sc(ACC(q, t))))
                P.dr("vector")
                for t in range(4, 8):
                    for q in range(4):
                        P.emit("scalar", lambda e, t=t, q=q: e.activation(
                            gbuf[:, :], zq(t, q), AF.Sign,
                            bias=stats[:, NMA + t - 4:NMA + t - 3],
                            accum_out=sc(ACC(q, t))),
                            waits=([(s_ve, a_state)] if (t, q) == (4, 0)
                                   else []),
                            inc=(s_ac if (t, q) == (7, 3) else None))
                act_cnt = s_ac.n
                P.emit("vector", lambda e: e.tensor_tensor(
                    sc(100, 4), sc(ACC(0, 4), 4), sc(ACC(1, 4), 4),
                    op=op.add), waits=[(s_ac, act_cnt)])
                P.emit("vector", lambda e: e.tensor_tensor(
                    sc(108, 4), sc(ACC(2, 4), 4), sc(ACC(3, 4), 4),
                    op=op.add))
                P.dr("vector")
                P.emit("vector", lambda e: e.tensor_tensor(
                    sc(116, 4), sc(100, 4), sc(108, 4), op=op.add))
                P.dr("vector")
                P.emit("vector", lambda e: e.tensor_scalar(
                    sc(GE + 4, 4), sc(116, 4), -6555.0, None, op0=op.is_gt))
                P.emit("vector", lambda e: e.tensor_scalar(
                    sc(GENA, 4), sc(116, 4), -6555.0, None, op0=op.is_le))
                P.dr("vector")
                P.emit("vector", lambda e: e.tensor_tensor(
                    sc(100, 4), sc(MID + 4, 4), sc(LO + 4, 4),
                    op=op.subtract))
                P.emit("vector", lambda e: e.tensor_tensor(
                    sc(108, 4), sc(MID + 4, 4), sc(HI + 4, 4),
                    op=op.subtract))
                P.dr("vector")
                P.emit("vector", lambda e: e.tensor_tensor(
                    sc(100, 4), sc(100, 4), sc(GE + 4, 4), op=op.mult))
                P.emit("vector", lambda e: e.tensor_tensor(
                    sc(108, 4), sc(108, 4), sc(GENA, 4), op=op.mult))
                P.dr("vector")
                P.emit("vector", lambda e: e.tensor_tensor(
                    sc(LO + 4, 4), sc(LO + 4, 4), sc(100, 4), op=op.add))
                P.emit("vector", lambda e: e.tensor_tensor(
                    sc(HI + 4, 4), sc(HI + 4, 4), sc(108, 4), op=op.add))
                P.dr("vector")
                if it < NSEL - 1:
                    P.emit("vector", lambda e: e.tensor_tensor(
                        sc(100, 4), sc(LO + 4, 4), sc(HI + 4, 4), op=op.add))
                    P.dr("vector")
                    P.emit("vector", lambda e: e.tensor_scalar(
                        sc(MID + 4, 4), sc(100, 4), 0.5, None, op0=op.mult))
                    P.emit("vector", lambda e: e.tensor_scalar(
                        sc(NMA, 4), sc(100, 4), -0.5, None, op0=op.mult),
                        inc=s_ve)
                    a_state = s_ve.n
                    P.dr("vector")
                P.emit("vector", lambda e: e.tensor_tensor(
                    sc(96, 4), sc(ACC(0, 0), 4), sc(ACC(1, 0), 4), op=op.add))
                P.emit("vector", lambda e: e.tensor_tensor(
                    sc(104, 4), sc(ACC(2, 0), 4), sc(ACC(3, 0), 4),
                    op=op.add))
                P.dr("vector")
                P.emit("vector", lambda e: e.tensor_tensor(
                    sc(112, 4), sc(96, 4), sc(104, 4), op=op.add))
                P.dr("vector")
                P.emit("vector", lambda e: e.tensor_scalar(
                    sc(GE, 4), sc(112, 4), 818.5, None, op0=op.is_gt))
                P.emit("vector", lambda e: e.tensor_scalar(
                    sc(GEND, 4), sc(112, 4), 818.5, None, op0=op.is_le))
                P.dr("vector")
                P.emit("vector", lambda e: e.tensor_tensor(
                    sc(96, 4), sc(MID, 4), sc(LO, 4), op=op.subtract))
                P.emit("vector", lambda e: e.tensor_tensor(
                    sc(104, 4), sc(MID, 4), sc(HI, 4), op=op.subtract))
                P.dr("vector")
                P.emit("vector", lambda e: e.tensor_tensor(
                    sc(96, 4), sc(96, 4), sc(GE, 4), op=op.mult))
                P.emit("vector", lambda e: e.tensor_tensor(
                    sc(104, 4), sc(104, 4), sc(GEND, 4), op=op.mult))
                P.dr("vector")
                P.emit("vector", lambda e: e.tensor_tensor(
                    sc(LO, 4), sc(LO, 4), sc(96, 4), op=op.add))
                P.emit("vector", lambda e: e.tensor_tensor(
                    sc(HI, 4), sc(HI, 4), sc(104, 4), op=op.add))
                P.dr("vector")
            for t in range(8):
                for q in range(4):
                    if (4 * t + q) % 2 == 0:
                        P.dr("vector")
                    P.emit("vector", lambda e, t=t, q=q: e.tensor_scalar(
                        shalf(4 * t + q), zq(t, q),
                        stats[:, LO + t:LO + t + 1], None,
                        op0=op.is_ge, op1=op.add,
                        accum_out=sc(ACC(q, t))))
            P.dr("vector")
            P.emit("vector", lambda e: e.tensor_tensor(
                sc(96, 8), sc(64, 8), sc(72, 8), op=op.add))
            P.emit("vector", lambda e: e.tensor_tensor(
                sc(104, 8), sc(80, 8), sc(88, 8), op=op.add))
            P.dr("vector")
            P.emit("vector", lambda e: e.tensor_tensor(
                sc(112, 8), sc(96, 8), sc(104, 8), op=op.add), inc=s_ve)
            P.dr("vector")
            for t in range(8):
                for qp in range(2):
                    q0, q1 = 2 * qp, 2 * qp + 1
                    P.dr("vector")
                    P.emit("vector", lambda e, t=t, q=q0: e.tensor_scalar(
                        shalf(q), zq(t, q), stats[:, LO + t:LO + t + 1], BIGH,
                        op0=op.is_lt, op1=op.mult))
                    P.emit("vector", lambda e, t=t, q=q1: e.tensor_scalar(
                        shalf(q), zq(t, q), stats[:, LO + t:LO + t + 1], BIGH,
                        op0=op.is_lt, op1=op.mult))
                    P.dr("vector")
                    P.emit("vector", lambda e, t=t, q=q0: e.tensor_tensor(
                        shalf(q), shalf(q), zq(t, q), op=op.add))
                    P.emit("vector", lambda e, t=t, q=q1: e.tensor_tensor(
                        shalf(q), shalf(q), zq(t, q), op=op.add))
                    P.dr("vector")
                    P.emit("vector", lambda e, t=t, q=q0: e.tensor_scalar(
                        shalf(q), shalf(q), 0.0, None, op0=op.add, op1=op.min,
                        accum_out=sc(ACC(q0, t))))
                    P.emit("vector", lambda e, t=t, q=q1: e.tensor_scalar(
                        shalf(q), shalf(q), 0.0, None, op0=op.add, op1=op.min,
                        accum_out=sc(ACC(q1, t))))
            P.dr("vector")
            P.emit("vector", lambda e: e.tensor_tensor(
                sc(96, 8), sc(64, 8), sc(72, 8), op=op.min))
            P.emit("vector", lambda e: e.tensor_tensor(
                sc(104, 8), sc(80, 8), sc(88, 8), op=op.min))
            P.dr("vector")
            P.emit("vector", lambda e: e.tensor_tensor(
                sc(GE, 8), sc(96, 8), sc(104, 8), op=op.min))
            P.dr("vector")
            P.emit("vector", lambda e: e.tensor_scalar(
                sc(104, 8), sc(112, 8), 819.5, None, op0=op.is_gt))
            P.emit("vector", lambda e: e.tensor_scalar(
                sc(MID, 8), sc(LO, 8), -1e-12, None, op0=op.add))
            P.dr("vector")
            P.emit("vector", lambda e: e.tensor_tensor(
                sc(96, 8), sc(GE, 8), sc(MID, 8), op=op.subtract))
            P.dr("vector")
            P.emit("vector", lambda e: e.tensor_tensor(
                sc(96, 8), sc(96, 8), sc(104, 8), op=op.mult))
            P.dr("vector")
            P.emit("vector", lambda e: e.tensor_tensor(
                sc(112, 8), sc(MID, 8), sc(96, 8), op=op.add))
            P.dr("vector")
            for t in range(8):
                for qp in range(2):
                    q0, q1 = 2 * qp, 2 * qp + 1
                    P.emit("vector", lambda e, t=t, q=q0: e.tensor_scalar(
                        shalf(q), zq(t, q), stats[:, CNT + t:CNT + t + 1],
                        None, op0=op.is_gt))
                    P.emit("vector", lambda e, t=t, q=q1: e.tensor_scalar(
                        shalf(q), zq(t, q), stats[:, CNT + t:CNT + t + 1],
                        None, op0=op.is_gt))
                    P.dr("vector")
                    P.emit("vector", lambda e, t=t, q=q0: e.tensor_tensor(
                        zq(t, q), zq(t, q), shalf(q), op=op.mult))
                    P.emit("vector", lambda e, t=t, q=q1: e.tensor_tensor(
                        zq(t, q), zq(t, q), shalf(q), op=op.mult),
                        inc=(s_ve if (t, qp) == (7, 1) else None))
                    P.dr("vector")
            for t in range(8):
                sel_done[t] = s_ve.n
            if "seldbg" in taps:
                P.emit("sync", lambda e: e.dma_start(
                    taps["seldbg"][0:128, :], stats[:, :]),
                    waits=[(s_ve, sel_done[7])], inc=s_ds, inck=16)
                P.emit("sync", lambda e: e.dma_start(
                    taps["seldbg"][128:129, 0:8], scal[0:1, :]),
                    inc=s_ds, inck=16)
                tap_done[0] = s_ds.n
            if "zdbg" in taps:
                for t in range(8):
                    P.emit("sync", lambda e, t=t: e.dma_start(
                        taps["zdbg"][t, :, :],
                        arena_t[:, t * NH:(t + 1) * NH]),
                        waits=[(s_ve, sel_done[t])], inc=s_ds, inck=16)
                tap_done[0] = s_ds.n

        for ch in range(32):
            tn = tern[ch % 2]
            for p2i in range(2):
                pi = ch * 2 + p2i
                sg = stg[pi % 2]
                w = [stg_used[pi - 2]] if (pi - 2) in stg_used else []
                P.emit("sync", lambda e, sg=sg, pi=pi: e.dma_start(
                    sg[:, :], wsT[pi]),
                    waits=w, inc=s_sg[pi % 2], inck=16)
                wq = [(s_sg[pi % 2], s_sg[pi % 2].n), (s_ve, mean_ready)]
                if p2i == 0:
                    wq.append((s_mm, q_done.get(ch - 2, 0)))

                def tslice(tn=tn, p2i=p2i):
                    return tn[:, :].rearrange("p (k o) -> p k o", o=256)[
                        :, :, p2i * 128:(p2i + 1) * 128]
                P.emit("vector", lambda e, sg=sg: e.tensor_scalar(
                    gbuf[:, :], sg[:, :], scal[:, 4:5], None, op0=op.is_gt),
                    waits=wq, inc=s_ve)
                P.dr("vector")
                P.emit("vector", lambda e, sg=sg, tn=tn, p2i=p2i:
                       e.scalar_tensor_tensor(
                           tslice(tn, p2i), sg[:, :], stats[:, 24:25],
                           gbuf[:, :], op0=op.is_lt, op1=op.subtract),
                       inc=s_ve)
                P.dr("vector")
                stg_used[pi] = (s_ve, s_ve.n)
                prev_stt[0] = s_ve.n
            tern_ready = s_ve.n
            P.emit("gpsimd", lambda e, ch=ch: e.dma_start(
                bsb[:, :], bsynt[:, ch * 256:(ch + 1) * 256]),
                waits=[(s_ve, bias_prev[0])], inc=s_bs, inck=16)
            P.emit("vector", lambda e: e.tensor_scalar(
                bsb2[:, :], bsb[:, :], stats[0:1, 25:26], None, op0=op.mult),
                waits=[(s_bs, s_bs.n), (s_ve, mean_ready),
                       (s_mm, q_done.get(ch - 1, 0))], inc=s_ve)
            bias_ready = s_ve.n
            bias_prev[0] = s_ve.n
            for tile in range(8):
                pzi = (ch * 8 + tile) % 4
                wfirst = [(s_ve, tern_ready), (s_ve, bias_ready),
                          (s_ac, pz_drain.get(pzi, 0))]
                for k in range(16):
                    P.emit("tensor", lambda e, pzi=pzi, tn=tn, k=k, tile=tile:
                           e.matmul(pz[pzi][:, 0:256],
                                    xnTc[:, k * NTOK + tile * 128:
                                         k * NTOK + (tile + 1) * 128],
                                    tn[:, k * 256:(k + 1) * 256],
                                    start=(k == 0), stop=False),
                           waits=(wfirst if k == 0 else []))
                P.emit("tensor", lambda e, pzi=pzi: e.matmul(
                    pz[pzi][:, 0:256], onesb[0:1, 0:128], bsb2[:, :],
                    start=False, stop=True), inc=s_mm)
                P.emit("scalar", lambda e, pzi=pzi, tile=tile, ch=ch:
                       e.activation(
                           arena_t[:, tile * NH + ch * 256:
                                   tile * NH + (ch + 1) * 256],
                           pz[pzi][:, 0:256], AF.Identity, scale=scal[:, 6:7]),
                       waits=[(s_mm, s_mm.n)], inc=s_ac)
                pz_drain[pzi] = s_ac.n
                if ch == 31:
                    z_drained[tile] = s_ac.n
            q_done[ch] = s_mm.n
        emit_sel_all()
        hh_drain = {}
        for tile in range(8):
            for g in range(8):
                ph = phh[g % 2]
                wfirst = [(s_ve, sel_done[tile]),
                          (s_ve, hh_drain.get(g % 2, 0)),
                          (s_ds, tap_done[0])]
                for kk in range(8):
                    k = g * 8 + kk
                    P.emit("tensor", lambda e, ph=ph, tile=tile, k=k, kk=kk:
                           e.transpose(
                               ph[:, kk * 128:(kk + 1) * 128],
                               arena_t[:, tile * NH + k * 128:
                                       tile * NH + (k + 1) * 128],
                               identh[:, :]),
                           waits=(wfirst if kk == 0 else []),
                           inc=(s_tt if kk == 7 else None))
                P.emit("vector", lambda e, ph=ph, tile=tile, g=g: e.tensor_copy(
                    arena_t[:, tile * NH + g * 1024:tile * NH + (g + 1) * 1024],
                    ph[:, :]), waits=[(s_tt, s_tt.n)], inc=s_ve)
                hh_drain[g % 2] = s_ve.n
        quiesce()
        P.run_block(nc)
    pX.close()
    if upto <= 4:
        pArena.close(); es.close()
        return nc

    if True:
        with contextlib.ExitStack() as p5:
            def p5sb(name, shape, dt):
                return p5.enter_context(nc.sbuf_tensor(name, shape, dt))
            stg5 = [p5sb(f"stg5{i}", [128, 2048], F32) for i in range(2)]
            onesh = p5sb("onesh", [1, 512], FP16)
            gb5 = p5sb("gb5", [128, 2048], FP16)
            ternD = [p5sb(f"ternD{i}", [128, 8192], FP16) for i in range(2)]
            bob = p5sb("bob", [1, 128], FP16)
            bob2 = [p5sb(f"bob2{i}", [1, 128], FP16) for i in range(2)]
            obuf = [p5sb(f"obuf{i}", [128, 1024], BF16) for i in range(2)]
            po5 = [p5.enter_context(nc.psum_tensor(f"po5{i}", [128, 512], F32))
                   for i in range(4)]

            P.emit("gpsimd", lambda e: e.memset(onesh[:, :], 1.0), inc=s_gp)
            onesh_ready = s_gp.n
            stg_used5 = {}
            tern_used5 = {}
            po_drain5 = {}
            ob_used = {}
            prev_stt5 = [0]
            b5_prev = [0]
            t5r, b5r = {}, {}

            def emit_quant5(d):
                tn = ternD[d % 2]
                wtn = [(s_mm, tern_used5.get(d - 2, 0))]
                for q4 in range(4):
                    pi = d * 4 + q4
                    sg = stg5[pi % 2]
                    w = [stg_used5[pi - 2]] if (pi - 2) in stg_used5 else []
                    P.emit("sync", lambda e, sg=sg, d=d, q4=q4: e.dma_start(
                        sg[:, :], woutT[d * 4 + q4]),
                        waits=w, inc=s_s5[pi % 2], inck=16)
                    P.emit("vector", lambda e, sg=sg: e.tensor_scalar(
                        gb5[:, :], sg[:, :], scal[:, 5:6], None, op0=op.is_gt),
                        waits=[(s_s5[pi % 2], s_s5[pi % 2].n)]
                        + (wtn if q4 == 0 else []), inc=s_ve)
                    P.dr("vector")
                    P.emit("vector", lambda e, sg=sg, tn=tn, q4=q4:
                           e.scalar_tensor_tensor(
                               tn[:, q4 * 2048:(q4 + 1) * 2048], sg[:, :],
                               stats[:, 26:27], gb5[:, :],
                               op0=op.is_lt, op1=op.subtract),
                           inc=s_ve)
                    P.dr("vector")
                    stg_used5[pi] = (s_ve, s_ve.n)
                t5r[d] = s_ve.n
                P.emit("gpsimd", lambda e, d=d: e.dma_start(
                    bob[:, :], boutt[:, d * 128:(d + 1) * 128]),
                    waits=[(s_ve, b5_prev[0])], inc=s_bo, inck=16)
                P.emit("vector", lambda e, d=d: e.tensor_scalar(
                    bob2[d % 2][:, :], bob[:, :], stats[0:1, 27:28], None,
                    op0=op.mult),
                    waits=[(s_bo, s_bo.n), (s_mm, tern_used5.get(d - 2, 0))],
                    inc=s_ve)
                b5r[d] = s_ve.n
                b5_prev[0] = s_ve.n

            emit_quant5(0)
            for d in range(16):
                tn = ternD[d % 2]
                if d + 1 < 16:
                    emit_quant5(d + 1)
                ob = obuf[d % 2]
                for n in range(2):
                    pzi = (d * 2 + n) % 4
                    wfirst = [(s_ve, t5r[d]), (s_ve, b5r[d]),
                              (s_ve, po_drain5.get(pzi, 0))]
                    if d == 0 and n == 0:
                        wfirst.append((s_gp, onesh_ready))
                    for j in range(64):
                        def rhs(e, j=j, n=n):
                            return arena_t[:, :].rearrange(
                                "p (i f) -> p i f", f=NH)[
                                :, 4 * n:4 * n + 4, j * 128:(j + 1) * 128]
                        P.emit("tensor", lambda e, pzi=pzi, tn=tn, j=j, n=n:
                               e.matmul(po5[pzi][:, :],
                                        tn[:, j * 128:(j + 1) * 128],
                                        arena_t[:, :].rearrange(
                                            "p (i f) -> p i f", f=NH)[
                                            :, 4 * n:4 * n + 4,
                                            j * 128:(j + 1) * 128],
                                        start=(j == 0), stop=False),
                               waits=(wfirst if j == 0 else []))
                    P.emit("tensor", lambda e, pzi=pzi, d=d: e.matmul(
                        po5[pzi][:, :], bob2[d % 2][:, :], onesh[0:1, 0:512],
                        start=False, stop=True), inc=s_mm)
                    P.emit("vector", lambda e, pzi=pzi, ob=ob, n=n:
                           e.tensor_scalar(
                               ob[:, n * 512:(n + 1) * 512], po5[pzi][:, :],
                               scal[:, 7:8], None, op0=op.mult),
                           waits=[(s_mm, s_mm.n)]
                           + ([ob_used[d - 2]]
                              if (n == 0 and (d - 2) in ob_used) else []),
                           inc=s_ve)
                    po_drain5[pzi] = s_ve.n
                tern_used5[d] = s_mm.n
                P.emit("sync", lambda e, ob=ob, d=d: e.dma_start(
                    osp[d * 128:(d + 1) * 128, :], ob[:, :]),
                    waits=[(s_ve, s_ve.n)], inc=s_os[d % 2], inck=16)
                ob_used[d] = (s_os[d % 2], s_os[d % 2].n)
            quiesce()
        P.run_block(nc)
    pArena.close()

    with contextlib.ExitStack() as p6:
        def p6sb(name, shape, dt):
            return p6.enter_context(nc.sbuf_tensor(name, shape, dt))
        wob = p6sb("wob", [128, 16 * 2048], BF16)
        gbb = p6sb("gbb", [128, 2048], F32)
        beb = p6sb("beb", [128, 2048], F32)
        xnl = [p6sb(f"xnl{i}", [128, 2048], BF16) for i in range(2)]
        onl = [p6sb(f"onl{i}", [128, 2048], BF16) for i in range(2)]
        anl = [p6sb(f"anl{i}", [128, 2048], BF16) for i in range(2)]
        xl = [p6sb(f"xl{i}", [128, 2048], F32) for i in range(2)]
        ybuf = p6sb("ybuf", [128, 2048], BF16)
        lnf = p6sb("lnf", [128, 2048], F32)
        t2buf = p6sb("t2buf", [128, 2048], F32)
        fout = [p6sb(f"fout{i}", [128, 2048], F32) for i in range(2)]
        trash6 = p6sb("trash6", [128, 4096], FP16)
        pa6 = [p6.enter_context(nc.psum_tensor(f"pa6{i}", [128, 512], F32))
               for i in range(4)]

        P.emit("sync", lambda e: e.dma_start(gbb[0:1, :], gammat[:, :]),
               inc=s_gb2, inck=16)
        P.emit("sync", lambda e: e.dma_start(beb[0:1, :], betat[:, :]),
               inc=s_gb2, inck=16)
        for n in range(4):
            P.emit("tensor", lambda e, n=n: e.matmul(
                pa6[n][:, :], onesf[:, :], gbb[0:1, n * 512:(n + 1) * 512],
                start=True, stop=True),
                waits=([(s_gb2, s_gb2.n)] if n == 0 else []),
                inc=(s_mm if n == 3 else None))
        for n in range(4):
            P.emit("vector", lambda e, n=n: e.tensor_copy(
                gbb[:, n * 512:(n + 1) * 512], pa6[n][:, :]),
                waits=([(s_mm, s_mm.n)] if n == 0 else []),
                inc=(s_ve if n == 3 else None))
        gbb_ve = s_ve.n
        for n in range(4):
            P.emit("tensor", lambda e, n=n: e.matmul(
                pa6[n][:, :], onesf[:, :], beb[0:1, n * 512:(n + 1) * 512],
                start=True, stop=True),
                waits=([(s_ve, gbb_ve)] if n == 0 else []),
                inc=(s_mm if n == 3 else None))
        for n in range(4):
            P.emit("vector", lambda e, n=n: e.tensor_copy(
                beb[:, n * 512:(n + 1) * 512], pa6[n][:, :]),
                waits=([(s_mm, s_mm.n)] if n == 0 else []),
                inc=(s_ve if n == 3 else None))
        P.emit("gpsimd", lambda e: e.memset(stats[:, 15:16], 0.0),
               waits=[(s_ve, s_ve.n)], inc=s_gp)
        gb_ready = s_gp.n
        for k in range(16):
            P.emit("gpsimd", lambda e, k=k: e.dma_start(
                wob[:, k * 2048:(k + 1) * 2048], woT[k * 128:(k + 1) * 128, :]),
                inc=s_wo, inck=16)
        wo_ready = s_wo.n

        buf_free = {}
        out_done = {}
        for i in range(8):
            xnb6, onb, anb, xb6, fo = (xnl[i % 2], onl[i % 2], anl[i % 2],
                                       xl[i % 2], fout[i % 2])
            w = [(s_ve, buf_free.get(i - 2, 0))]
            if i == 0:
                w += [(s_sp[0], s_sp[0].n), (s_sp[1], s_sp[1].n),
                      (s_os[0], s_os[0].n), (s_os[1], s_os[1].n),
                      (s_at, s_at.n)]
            P.emit("sync", lambda e, xnb6=xnb6, i=i: e.dma_start(
                xnb6[:, :], xnsp[i, :, :]), waits=w, inc=s_l[i % 2], inck=16)
            P.emit("sync", lambda e, onb=onb, i=i: e.dma_start(
                onb[:, :], osp[:, i * 128:(i + 1) * 128], transpose=True),
                inc=s_l[i % 2], inck=16)
            P.emit("sync", lambda e, anb=anb, i=i: e.dma_start(
                anb[:, :].rearrange("p (m t) -> p m t", t=128),
                asp[:, :, i * 128:(i + 1) * 128]
                .rearrange("m p t -> p m t")), inc=s_l[i % 2], inck=16)
            P.emit("scalar", lambda e, xb6=xb6, i=i: e.dma_start(
                xb6[:, :], xs[HALO + i * 128:HALO + (i + 1) * 128, :]),
                inc=s_l[i % 2], inck=16)
            dr6 = (s_l[i % 2], s_l[i % 2].n)
            for n in range(4):
                pzi = n
                wfirst = [dr6, (s_wo, wo_ready),
                          (s_ve, buf_free.get(i - 1, 0))]
                for k in range(16):
                    P.emit("tensor", lambda e, pzi=pzi, anb=anb, k=k, n=n:
                           e.matmul(pa6[pzi][:, :],
                                    anb[:, k * 128:(k + 1) * 128],
                                    wob[:, k * 2048 + n * 512:
                                        k * 2048 + (n + 1) * 512],
                                    start=(k == 0), stop=(k == 15)),
                           waits=(wfirst if (k == 0 and n == 0) else []),
                           inc=(s_mm if (k == 15 and n == 3) else None))
            mm_here = s_mm.n
            HF = C // 2
            P.emit("vector", lambda e, xnb6=xnb6, onb=onb: e.tensor_tensor(
                ybuf[:, 0:HF], xnb6[:, 0:HF], onb[:, 0:HF], op=op.add),
                waits=[dr6])
            P.emit("vector", lambda e, xnb6=xnb6, onb=onb: e.tensor_tensor(
                ybuf[:, HF:C], xnb6[:, HF:C], onb[:, HF:C], op=op.add))
            P.dr("vector")
            P.emit("vector", lambda e: e.tensor_scalar(
                trash6[:, 0:HF], ybuf[:, 0:HF], 0.0, None, op0=op.add,
                op1=op.add, accum_out=stats[:, 0:1]))
            P.emit("vector", lambda e: e.tensor_scalar(
                trash6[:, HF:C], ybuf[:, HF:C], 0.0, None, op0=op.add,
                op1=op.add, accum_out=stats[:, 10:11]))
            P.dr("vector")
            P.emit("vector", lambda e: e.tensor_copy(
                stats[:, 30:31], stats[:, 10:11]), inc=s_ve)
            P.emit("scalar", lambda e: e.activation(
                trash6[:, C:C + HF], ybuf[:, 0:HF], AF.Square,
                accum_out=stats[:, 1:2]), waits=[(s_ve, s_ve.n)])
            P.emit("scalar", lambda e: e.activation(
                trash6[:, C + HF:2 * C], ybuf[:, HF:C], AF.Square,
                accum_out=stats[:, 11:12]), inc=s_ac)
            P.dr("vector")
            P.emit("vector", lambda e: e.tensor_tensor(
                stats[:, 12:13], stats[:, 0:1], stats[:, 10:11], op=op.add),
                waits=[(s_ac, s_ac.n)])
            P.emit("vector", lambda e: e.tensor_tensor(
                stats[:, 13:14], stats[:, 1:2], stats[:, 11:12], op=op.add))
            P.dr("vector")
            P.emit("vector", lambda e: e.tensor_scalar(
                stats[:, 2:3], stats[:, 12:13], 1.0 / C, None, op0=op.mult))
            P.emit("vector", lambda e: e.tensor_scalar(
                stats[:, 3:4], stats[:, 13:14], 1.0 / C, None, op0=op.mult))
            P.dr("vector")
            P.emit("vector", lambda e: e.tensor_scalar(
                stats[:, 28:29], stats[:, 2:3], -1.0, None, op0=op.mult))
            P.emit("vector", lambda e: e.tensor_tensor(
                stats[:, 4:5], stats[:, 2:3], stats[:, 2:3], op=op.mult))
            P.dr("vector")
            P.emit("vector", lambda e: e.tensor_tensor(
                stats[:, 5:6], stats[:, 3:4], stats[:, 4:5], op=op.subtract),
                inc=s_ve)
            P.emit("scalar", lambda e: e.activation(
                stats[:, 6:7], stats[:, 5:6], AF.Sqrt, bias=cstb[:, 1:2]),
                waits=[(s_ve, s_ve.n)], inc=s_ac)
            P.emit("vector", lambda e: e.reciprocal(stats[:, 7:8], stats[:, 6:7]),
                   waits=[(s_ac, s_ac.n)])
            P.dr("vector")
            P.emit("vector", lambda e: e.tensor_tensor(
                stats[:, 9:10], stats[:, 28:29], stats[:, 7:8], op=op.mult),
                inc=s_ve)
            P.emit("scalar", lambda e: e.activation(
                lnf[:, :], ybuf[:, :], AF.Identity, bias=stats[:, 9:10],
                scale=stats[:, 7:8]), waits=[(s_ve, s_ve.n)], inc=s_ac)
            P.emit("vector", lambda e: e.tensor_tensor(
                lnf[:, :], lnf[:, :], gbb[:, :], op=op.mult),
                waits=[(s_ac, s_ac.n), (s_gp, gb_ready)])
            P.emit("vector", lambda e, xb6=xb6: e.tensor_tensor(
                t2buf[:, :], xb6[:, :], beb[:, :], op=op.add))
            P.dr("vector")
            for n in range(4):
                P.emit("vector", lambda e, fo=fo, n=n: e.tensor_tensor(
                    fo[:, n * 512:(n + 1) * 512], lnf[:, n * 512:(n + 1) * 512],
                    pa6[n][:, :], op=op.add),
                    waits=([(s_mm, mm_here)]
                           + ([out_done[i - 2]]
                              if (i - 2) in out_done else [])
                           if n == 0 else []))
            P.dr("vector")
            for n in range(4):
                P.emit("vector", lambda e, fo=fo, n=n: e.tensor_tensor(
                    fo[:, n * 512:(n + 1) * 512], fo[:, n * 512:(n + 1) * 512],
                    t2buf[:, n * 512:(n + 1) * 512], op=op.add),
                    inc=(s_ve if n == 3 else None))
            buf_free[i] = s_ve.n
            P.emit("sync", lambda e, fo=fo, i=i: e.dma_start(
                outt[i * 128:(i + 1) * 128, :], fo[:, :]),
                waits=[(s_ve, s_ve.n)], inc=s_do[i % 2], inck=16)
            out_done[i] = (s_do[i % 2], s_do[i % 2].n)
        P.emit("sync", lambda e: e.nop(),
               waits=[(s_do[0], s_do[0].n), (s_do[1], s_do[1].n)])
        quiesce()
        P.run_block(nc)

    es.close()
    return nc


def _prep_weights(inputs):
    Wq, Wk, Wv, Wo = (inputs["L0_attn_Wq"], inputs["L0_attn_Wk"],
                      inputs["L0_attn_Wv"], inputs["L0_attn_Wo"])
    h = np.arange(C) // 128
    j = np.arange(C) % 128
    perm = h * 128 + 2 * (j % 64) + (j // 64)

    def tile4(WT):
        return np.ascontiguousarray(
            WT.reshape(16, 128, 4, 512).transpose(2, 1, 0, 3).reshape(
                4, 128, 16 * 512))
    wqT = tile4(np.ascontiguousarray(Wq.T)[:, perm])
    wkT = tile4(np.ascontiguousarray(Wk.T)[:, perm])
    wvT = tile4(np.ascontiguousarray(Wv.T))
    woT = np.ascontiguousarray(Wo.T)
    wsT = np.ascontiguousarray(
        inputs["L0_ffn_W_syn"].T.reshape(16, 128, 64, 128)
        .transpose(2, 1, 0, 3).reshape(64, 128, C))
    woutT = np.ascontiguousarray(
        inputs["L0_ffn_W_out"].T.reshape(4, 16, 128, 16, 128)
        .transpose(3, 0, 2, 1, 4).reshape(64, 128, C))
    return dict(wqT=wqT, wkT=wkT, wvT=wvT, woT=woT, wsT=wsT, woutT=woutT,
                bsynt=inputs["L0_ffn_b_syn"].reshape(1, NH),
                boutt=inputs["L0_ffn_b_out"].reshape(1, C),
                gammat=inputs["L0_ffn_gamma"].reshape(1, C),
                betat=inputs["L0_ffn_beta"].reshape(1, C))


def kernel(**inputs):
    inputs = {k: np.asarray(v, dtype=np.float32) for k, v in inputs.items()}
    x = inputs["L0_attn_x"]
    wd = _prep_weights(inputs)
    nc = build_nc()
    in_maps = []
    for c in range(8):
        b, hh = c // 2, c % 2
        if hh == 0:
            xsh = np.concatenate([x[b, :HALO], x[b, :NTOK]], axis=0)
            mf = np.full((128, 1), NEG, dtype=np.float32)
        else:
            xsh = x[b, NTOK - HALO:2 * NTOK]
            mf = np.zeros((128, 1), dtype=np.float32)
        m = {"xs": np.ascontiguousarray(xsh), "maskft": mf}
        m.update(wd)
        in_maps.append(m)
    res = run_bass_kernel_spmd(nc, in_maps, core_ids=list(range(8)))
    outp = np.empty((Bsz, T, C), dtype=np.float32)
    for c in range(8):
        b, hh = c // 2, c % 2
        outp[b, hh * NTOK:(hh + 1) * NTOK] = res.results[c]["out"]
    return outp


# version 15
# speedup vs baseline: 1.7060x; 1.0997x over previous
import contextlib
import numpy as np
import concourse.bass as bass
import concourse.mybir as mybir
from concourse.alu_op_type import AluOpType as op
from concourse.bass_utils import run_bass_kernel_spmd

F32 = mybir.dt.float32
BF16 = mybir.dt.bfloat16
FP16 = mybir.dt.float16
AF = mybir.ActivationFunctionType

Bsz, T, C = 4, 2048, 2048
H, DH, W = 16, 128, 256
NH = 8192
NTOK = 1024
HALO = 256
NTOKH = NTOK + HALO
NSEL = 14
SCALE = float(DH) ** -0.5
LN1E4 = float(np.log(10000.0))
PI = float(np.pi)
NEG = -2.0e9
BIGH = 30000.0

ENGS = ("sync", "scalar", "vector", "tensor", "gpsimd")


class Sem:
    def __init__(self, h):
        self.h = h
        self.n = 0


class Prog:
    def __init__(self):
        self.streams = {e: [] for e in ENGS}

    def emit(self, eng, fn, waits=(), inc=None, inck=1):
        if inc is not None:
            inc.n += inck
        self.streams[eng].append((fn, tuple(waits), inc, inck))

    def dr(self, eng):
        self.streams[eng].append((lambda e: e.drain(), (), None, 1))

    def run_block(self, nc):
        with nc.Block() as blk:
            for eng in ENGS:
                items = self.streams[eng]
                if not items:
                    continue

                def mk(items):
                    def body(e):
                        for fn, waits, inc, inck in items:
                            for (s, v) in waits:
                                if v > 0:
                                    e.wait_ge(s.h, v)
                            instr = fn(e)
                            if inc is not None:
                                instr.then_inc(inc.h, inck)
                    return body
                getattr(blk, eng)(mk(items))
        self.streams = {e: [] for e in ENGS}


def build_nc(debug_taps=(), upto=99, p0mask=7):
    nc = bass.Bass("TRN2", target_bir_lowering=False)
    es = contextlib.ExitStack()

    def sb(name, shape, dt):
        return es.enter_context(nc.sbuf_tensor(name, shape, dt))

    def mksem(name):
        return Sem(es.enter_context(nc.semaphore(name)))

    xs = nc.dram_tensor("xs", [NTOKH, C], F32, kind="ExternalInput")
    wqT = nc.dram_tensor("wqT", [4, 128, 16 * 512], F32, kind="ExternalInput")
    wkT = nc.dram_tensor("wkT", [4, 128, 16 * 512], F32, kind="ExternalInput")
    wvT = nc.dram_tensor("wvT", [4, 128, 16 * 512], F32, kind="ExternalInput")
    woT = nc.dram_tensor("woT", [C, C], F32, kind="ExternalInput")
    wsT = nc.dram_tensor("wsT", [64, 128, C], F32, kind="ExternalInput")
    woutT = nc.dram_tensor("woutT", [64, 128, C], F32, kind="ExternalInput")
    bsynt = nc.dram_tensor("bsynt", [1, NH], F32, kind="ExternalInput")
    boutt = nc.dram_tensor("boutt", [1, C], F32, kind="ExternalInput")
    gammat = nc.dram_tensor("gammat", [1, C], F32, kind="ExternalInput")
    betat = nc.dram_tensor("betat", [1, C], F32, kind="ExternalInput")
    maskft = nc.dram_tensor("maskft", [128, 1], F32, kind="ExternalInput")
    outt = nc.dram_tensor("out", [NTOK, C], F32, kind="ExternalOutput")

    xnsp = nc.dram_tensor("xnsp", [8, 128, C], BF16, kind="Internal")
    qsp = nc.dram_tensor("qsp", [16, 128, NTOK], BF16, kind="Internal")
    ksp = nc.dram_tensor("ksp", [16, 128, NTOKH], BF16, kind="Internal")
    vsp = nc.dram_tensor("vsp", [NTOKH, C], BF16, kind="Internal")
    asp = nc.dram_tensor("asp", [16, 128, NTOK], BF16, kind="Internal")
    osp = nc.dram_tensor("osp", [C, NTOK], BF16, kind="Internal")

    taps = {}
    for name, shape, dt in debug_taps:
        taps[name] = nc.dram_tensor(name, shape, dt, kind="ExternalOutput")

    ident = sb("ident", [128, 128], BF16)
    identh = sb("identh", [128, 128], FP16)
    onesb = sb("onesb", [1, 128], BF16)
    onesf = sb("onesf", [1, 128], F32)
    stats = sb("stats", [128, 120], F32)
    scal = sb("scal", [128, 8], F32)
    maskfb = sb("maskfb", [128, 1], F32)
    mcols = sb("mcols", [128, 64], F32)
    cstb = sb("cstb", [128, 2], F32)
    pX = contextlib.ExitStack()
    xnTc = pX.enter_context(nc.sbuf_tensor("xnTc", [128, 16 * NTOK], BF16))
    pA = contextlib.ExitStack()
    xnTh = pA.enter_context(nc.sbuf_tensor("xnTh", [128, 16 * HALO], BF16))
    cosT = pA.enter_context(nc.sbuf_tensor("cosT", [128, NTOKH], BF16))
    sinsT = pA.enter_context(nc.sbuf_tensor("sinsT", [128, NTOKH], BF16))

    s_dx = mksem("s_dx")
    s_dw = mksem("s_dw")
    s_dm = mksem("s_dm")
    s_ds = mksem("s_ds")
    s_dr = mksem("s_dr")
    s_do = [mksem("s_do0"), mksem("s_do1")]
    s_mm = mksem("s_mm")
    s_ve = mksem("s_ve")
    s_ac = mksem("s_ac")
    s_gp = mksem("s_gp")
    s_tt = mksem("s_tt")
    s_mf = mksem("s_mf")
    s_x = [mksem("s_x0"), mksem("s_x1")]
    s_sp = [mksem("s_sp0"), mksem("s_sp1")]
    s_wb = [mksem("s_wb0"), mksem("s_wb1")]
    s_ms = [mksem("s_ms0"), mksem("s_ms1")]
    s_qsp = mksem("s_qsp")
    s_vb = [mksem("s_vb0"), mksem("s_vb1")]
    s_wd = [mksem("s_wd0"), mksem("s_wd1")]
    s_at = mksem("s_at")
    s_sg = [mksem("s_sg0"), mksem("s_sg1")]
    s_bs = mksem("s_bs")
    s_s5 = [mksem("s_s50"), mksem("s_s51")]
    s_bo = mksem("s_bo")
    s_os = [mksem("s_os0"), mksem("s_os1")]
    s_l = [mksem("s_l0"), mksem("s_l1")]
    s_wo = mksem("s_wo")
    s_gb2 = mksem("s_gb2")
    s_sw = mksem("s_sw")

    DMASEMS = [s_ds, s_dr, s_do[0], s_do[1], s_mf, s_x[0], s_x[1], s_sp[0], s_sp[1],
               s_wb[0], s_wb[1], s_ms[0], s_ms[1], s_qsp, s_vb[0], s_vb[1],
               s_wd[0], s_wd[1], s_at, s_sg[0], s_sg[1], s_bs, s_s5[0],
               s_s5[1], s_bo, s_os[0], s_os[1], s_l[0], s_l[1], s_wo, s_gb2, s_sw,
               s_dx, s_dw, s_dm]

    def quiesce():
        ws = [(sm, sm.n) for sm in DMASEMS if sm.n > 0]
        if ws:
            P.emit("sync", lambda e: e.nop(), waits=ws)

    P = Prog()

    with contextlib.ExitStack() as p0:
        def p0sb(name, shape, dt):
            return p0.enter_context(nc.sbuf_tensor(name, shape, dt))
        ct = p0sb("ct", [128, NTOKH], F32)
        st = p0sb("st", [128, NTOKH], F32)
        ang = p0sb("ang", [128, 24], F32)
        tmpl = p0sb("tmpl", [128, 2048], F32)
        xb = [p0sb(f"xb{i}", [128, C], F32) for i in range(2)]
        xnb = [p0sb(f"xnb{i}", [128, C], BF16) for i in range(2)]
        trash = p0sb("trash", [128, 2 * C], FP16)
        ptp = [p0.enter_context(nc.psum_tensor(f"ptp{i}", [128, 1024], BF16))
               for i in range(2)]

        P.emit("gpsimd", lambda e: e.memset(stats[:, :], 0.0))
        P.emit("gpsimd", lambda e: e.memset(scal[:, :], 0.0))
        P.emit("gpsimd", lambda e: e.memset(ident[:, :], 0.0), inc=s_gp)
        P.emit("gpsimd", lambda e: e.affine_select(
            out=ident[:, :], in_=ident[:, :], compare_op=op.not_equal,
            fill=1.0, base=0, pattern=[[-1, 128]], channel_multiplier=1),
            waits=[(s_gp, s_gp.n)])
        P.emit("gpsimd", lambda e: e.memset(identh[:, :], 0.0), inc=s_gp)
        P.emit("gpsimd", lambda e: e.affine_select(
            out=identh[:, :], in_=identh[:, :], compare_op=op.not_equal,
            fill=1.0, base=0, pattern=[[-1, 128]], channel_multiplier=1),
            waits=[(s_gp, s_gp.n)])
        P.emit("gpsimd", lambda e: e.memset(onesb[:, :], 1.0))
        P.emit("gpsimd", lambda e: e.memset(onesf[:, :], 1.0))
        P.emit("gpsimd", lambda e: e.memset(stats[:, 10:11], 1.0))
        P.emit("gpsimd", lambda e: e.memset(cstb[:, 0:1], PI / 2))
        P.emit("gpsimd", lambda e: e.memset(cstb[:, 1:2], 1e-5))
        P.emit("gpsimd", lambda e: e.iota(
            stats[:, 14:15], [[1, 1]], channel_multiplier=1,
            allow_small_or_imprecise_dtypes=True), inc=s_gp)

        P.emit("sync", lambda e: e.dma_start(maskfb[:, :], maskft[:, :]),
               inc=s_mf, inck=16)

        P.emit("vector", lambda e: e.tensor_scalar(
            stats[:, 15:16], stats[:, 14:15], 64.0, None, op0=op.is_ge),
            waits=[(s_gp, s_gp.n)])
        P.dr("vector")
        P.emit("vector", lambda e: e.scalar_tensor_tensor(
            stats[:, 13:14], stats[:, 15:16], -64.0, stats[:, 14:15],
            op0=op.mult, op1=op.add))
        P.emit("vector", lambda e: e.tensor_scalar(
            stats[:, 12:13], stats[:, 15:16], 2.0, -1.0, op0=op.mult, op1=op.add),
            inc=s_ve)
        P.dr("vector")

        P.emit("scalar", lambda e: e.activation(
            stats[:, 11:12], stats[:, 13:14], AF.Exp, scale=-LN1E4 / 64.0),
            waits=[(s_ve, s_ve.n)])
        P.dr("scalar")
        P.emit("scalar", lambda e: e.activation(
            ct[:, 1:2], stats[:, 11:12], AF.Sin, bias=cstb[:, 0:1]))
        P.emit("scalar", lambda e: e.activation(
            st[:, 1:2], stats[:, 11:12], AF.Sin), inc=s_ac)

        def ladder(e):
            e.memset(ct[:, 0:1], 1.0)
            e.memset(st[:, 0:1], 0.0)
            e.drain()
            wl = 2
            while wl < NTOKH:
                L = min(wl, NTOKH - wl)
                c1, s1 = ct[:, 1:2], st[:, 1:2]
                cp, sp = ct[:, wl - 1:wl], st[:, wl - 1:wl]
                e.tensor_tensor(tmpl[:, 0:1], cp, c1, op=op.mult)
                e.tensor_tensor(tmpl[:, 1:2], sp, s1, op=op.mult)
                e.tensor_tensor(tmpl[:, 2:3], cp, s1, op=op.mult)
                e.tensor_tensor(tmpl[:, 3:4], sp, c1, op=op.mult)
                e.drain()
                e.tensor_tensor(ang[:, 0:1], tmpl[:, 0:1], tmpl[:, 1:2],
                                op=op.subtract)
                e.tensor_tensor(ang[:, 1:2], tmpl[:, 2:3], tmpl[:, 3:4],
                                op=op.add)
                e.drain()
                cr, sr = ang[:, 0:1], ang[:, 1:2]
                ta, tb = tmpl[:, 0:L], tmpl[:, 1024:1024 + L]
                e.tensor_scalar(ta, st[:, 0:L], sr, None, op0=op.mult)
                e.tensor_scalar(tb, st[:, 0:L], cr, None, op0=op.mult)
                e.drain()
                e.scalar_tensor_tensor(ct[:, wl:wl + L], ct[:, 0:L], cr,
                                       ta, op0=op.mult, op1=op.subtract)
                e.scalar_tensor_tensor(st[:, wl:wl + L], ct[:, 0:L], sr,
                                       tb, op0=op.mult, op1=op.add)
                e.drain()
                wl *= 2
            e.tensor_copy(cosT[:, :], ct[:, :])
            e.drain()
            return e.tensor_scalar(sinsT[:, :], st[:, :], stats[:, 12:13], None,
                                   op0=op.mult)
        if p0mask & 1:
            P.emit("vector", ladder, waits=[(s_ac, s_ac.n)], inc=s_ve)
        else:
            P.emit("vector", lambda e: e.tensor_copy(cosT[:, :], ct[:, :]),
                   waits=[(s_ac, s_ac.n)])
            P.emit("vector", lambda e: e.tensor_copy(sinsT[:, :], st[:, :]),
                   inc=s_ve)

        cast_done = {}
        drain_done = {}
        tp_done = {}
        spill_xn = {}
        for i in range(10):
            b, nb = xb[i % 2], xnb[i % 2]
            w = [(s_ac, cast_done[i - 2])] if i >= 2 else []
            P.emit("sync", lambda e, b=b, i=i: e.dma_start(
                b[:, :], xs[128 * i:128 * (i + 1), :]), waits=w,
                inc=s_x[i % 2], inck=16)
            P.emit("vector", lambda e, b=b: e.tensor_scalar(
                trash[:, 0:C], b[:, :], 0.0, None, op0=op.add, op1=op.add,
                accum_out=stats[:, 0:1]), waits=[(s_x[i % 2], s_x[i % 2].n)])
            P.emit("scalar", lambda e, b=b: e.activation(
                trash[:, C:2 * C], b[:, :], AF.Square,
                accum_out=stats[:, 1:2]),
                waits=[(s_x[i % 2], s_x[i % 2].n)], inc=s_ac)
            P.dr("vector")
            P.emit("vector", lambda e: e.tensor_scalar(
                stats[:, 2:3], stats[:, 0:1], 1.0 / C, None, op0=op.mult),
                waits=[(s_ac, s_ac.n)])
            P.emit("vector", lambda e: e.tensor_scalar(
                stats[:, 3:4], stats[:, 1:2], 1.0 / C, None, op0=op.mult))
            P.dr("vector")
            P.emit("vector", lambda e: e.tensor_scalar(
                stats[:, 28:29], stats[:, 2:3], -1.0, None, op0=op.mult))
            P.emit("vector", lambda e: e.tensor_tensor(
                stats[:, 4:5], stats[:, 2:3], stats[:, 2:3], op=op.mult))
            P.dr("vector")
            P.emit("vector", lambda e: e.tensor_tensor(
                stats[:, 5:6], stats[:, 3:4], stats[:, 4:5], op=op.subtract),
                inc=s_ve)
            P.emit("scalar", lambda e: e.activation(
                stats[:, 6:7], stats[:, 5:6], AF.Sqrt, bias=cstb[:, 1:2]),
                waits=[(s_ve, s_ve.n)], inc=s_ac)
            P.emit("vector", lambda e: e.reciprocal(stats[:, 7:8], stats[:, 6:7]),
                   waits=[(s_ac, s_ac.n)])
            P.dr("vector")
            P.emit("vector", lambda e: e.tensor_tensor(
                stats[:, 9:10], stats[:, 28:29], stats[:, 7:8], op=op.mult),
                inc=s_ve)
            wcast = [(s_ve, s_ve.n)]
            if i >= 2:
                wcast.append((s_tt, tp_done[i - 2]))
                if (i - 2) in spill_xn:
                    wcast.append(spill_xn[i - 2])
            P.emit("scalar", lambda e, b=b, nb=nb: e.activation(
                nb[:, :], b[:, :], AF.Identity, bias=stats[:, 9:10],
                scale=stats[:, 7:8]), waits=wcast, inc=s_ac)
            cast_done[i] = s_ac.n
            if i >= 2:
                P.emit("sync", lambda e, nb=nb, i=i: e.dma_start(
                    xnsp[i - 2, :, :], nb[:, :]), waits=[(s_ac, s_ac.n)],
                    inc=s_sp[i % 2], inck=16)
                spill_xn[i] = (s_sp[i % 2], s_sp[i % 2].n)
            dstf = xnTc if i >= 2 else xnTh
            ncols = NTOK if i >= 2 else HALO
            toff = (i - 2) * 128 if i >= 2 else i * 128
            for g in range(2):
                pt = ptp[g]
                w = [(s_ac, cast_done[i])]
                if (i, g) != (0, 0) and (i, g) != (0, 1):
                    w.append((s_ve, drain_done[(i - 1, g)]))
                for kk in range(8):
                    k = 8 * g + kk
                    P.emit("tensor", lambda e, pt=pt, nb=nb, k=k, kk=kk: e.transpose(
                        pt[:, 128 * kk:128 * (kk + 1)],
                        nb[:, 128 * k:128 * (k + 1)], ident[:, :]),
                        waits=(w if kk == 0 else []), inc=s_tt)

                def drain(e, pt=pt, dstf=dstf, ncols=ncols, toff=toff, g=g):
                    dst = dstf[:, :].rearrange("p (k t) -> p k t", t=ncols)[
                        :, 8 * g:8 * g + 8, toff:toff + 128]
                    return e.tensor_copy(dst, pt[:, :])
                P.emit("vector", drain, waits=[(s_tt, s_tt.n)], inc=s_ve)
                drain_done[(i, g)] = s_ve.n
            tp_done[i] = s_tt.n
        quiesce()
        P.run_block(nc)
    if upto <= 0:
        pA.close(); pX.close(); es.close()
        return nc

    with contextlib.ExitStack() as p2:
        def p2sb(name, shape, dt):
            return p2.enter_context(nc.sbuf_tensor(name, shape, dt))
        wbuf = [p2sb(f"wbuf{i}", [128, 16 * 512], BF16) for i in range(2)]
        qkb = [p2sb(f"qkb{i}", [128, NTOKH], BF16) for i in range(2)]
        rsw = [p2sb(f"rsw{i}", [128, NTOKH], BF16) for i in range(2)]
        rt1 = [p2sb(f"rt1{i}", [128, NTOKH], BF16) for i in range(2)]
        rt2 = [p2sb(f"rt2{i}", [128, NTOKH], BF16) for i in range(2)]
        vb = [p2sb(f"vb{i}", [128, 512], BF16) for i in range(2)]
        mst = [p2sb(f"mst{i}", [128, 2 * C], F32) for i in range(2)]
        mtr = p2sb("mtr", [128, 2 * C], FP16)
        pj = [p2.enter_context(nc.psum_tensor(f"pj{i}", [128, 512], F32))
              for i in range(4)]

        mean_ms = {}
        for i in range(65):
            if i < 64:
                ms = mst[i % 2]
                src = (wsT[2 * i:2 * i + 2] if i < 32
                       else woutT[2 * (i - 32):2 * (i - 32) + 2])
                P.emit("scalar", lambda e, ms=ms, src=src: e.dma_start(
                    ms[:, :].rearrange("p (j o) -> p j o", o=C),
                    src.rearrange("j p o -> p j o")),
                    inc=s_ms[i % 2], inck=16)
                mean_ms[i] = s_ms[i % 2].n
            if i >= 1:
                j = i - 1
                msr = mst[j % 2]
                P.emit("scalar", lambda e, msr=msr, j=j: e.activation(
                    mtr[:, :], msr[:, :], AF.Abs, accum_out=mcols[:, j:j + 1]),
                    waits=[(s_ms[j % 2], mean_ms[j])], inc=s_ac)
        mean_done_ac = s_ac.n

        wts = {"q": wqT, "k": wkT, "v": wvT}
        NGR = {"q": [(256, 512), (768, 512)],
               "k": [(0, 256), (256, 512), (768, 512)]}
        chunk_use = {}
        pj_drain = {}
        qkb_spill = {0: 0, 1: 0}
        grp = [0]
        vb_spill = {}

        def xnT_slice(k, t0, tl):
            if t0 < HALO:
                assert t0 + tl <= HALO
                return xnTh[:, k * HALO + t0: k * HALO + t0 + tl]
            return xnTc[:, k * NTOK + (t0 - HALO): k * NTOK + (t0 - HALO) + tl]

        cidx = 0
        for wi, wn in enumerate("qkv"):
            for c in range(4):
                wb = wbuf[cidx % 2]
                w = [(s_mm, chunk_use.get(cidx - 2, 0))]
                P.emit("gpsimd", lambda e, wb=wb, wn=wn, c=c: e.dma_start(
                    wb[:, :], wts[wn][c]),
                    waits=w, inc=s_wb[cidx % 2], inck=16)
                wb_ready = (s_wb[cidx % 2], s_wb[cidx % 2].n)
                if wn in "qk":
                    ntok = NTOK if wn == "q" else NTOKH
                    spt = qsp if wn == "q" else ksp
                    for m in range(4):
                        mg = 4 * c + m
                        pjs_used = []
                        for gi, (t0, tl) in enumerate(NGR[wn]):
                            pji = (mg * 3 + gi) % 4
                            pjs_used.append((pji, t0, tl))
                            wfirst = [wb_ready,
                                      (s_ve, pj_drain.get(pji, 0))]
                            for k in range(16):
                                P.emit("tensor",
                                       lambda e, pji=pji, tl=tl, k=k, t0=t0,
                                       wb=wb, m=m: e.matmul(
                                           pj[pji][:, 0:tl],
                                           wb[:, k * 512 + m * 128:
                                              k * 512 + (m + 1) * 128],
                                           xnT_slice(k, t0, tl),
                                           start=(k == 0), stop=(k == 15)),
                                       waits=(wfirst if k == 0 else []),
                                       inc=(s_mm if k == 15 else None))
                        mm_here = s_mm.n
                        par = grp[0] % 2
                        grp[0] += 1
                        qk, rw, r1, r2 = qkb[par], rsw[par], rt1[par], rt2[par]
                        first_drain = True
                        for (pji, t0, tl) in pjs_used:
                            off = t0 if wn == "k" else t0 - HALO
                            wd = [(s_mm, mm_here)]
                            if first_drain:
                                wd.append((s_qsp, qkb_spill[par]))
                                first_drain = False
                            P.emit("vector", lambda e, pji=pji, off=off, tl=tl,
                                   qk=qk: e.tensor_copy(qk[:, off:off + tl],
                                                        pj[pji][:, 0:tl]),
                                   waits=wd,
                                   inc=(s_ve if (pji, t0, tl) == pjs_used[-1]
                                        else None))
                        for (pji, t0, tl) in pjs_used:
                            pj_drain[pji] = s_ve.n
                        P.emit("sync", lambda e, ntok=ntok, qk=qk, rw=rw:
                               e.dma_start(
                                   rw[0:64, 0:ntok], qk[64:128, 0:ntok]),
                               waits=[(s_ve, s_ve.n), (s_qsp, qkb_spill[par])],
                               inc=s_sw, inck=16)
                        P.emit("sync", lambda e, ntok=ntok, qk=qk, rw=rw:
                               e.dma_start(
                                   rw[64:128, 0:ntok], qk[0:64, 0:ntok]),
                               inc=s_sw, inck=16)
                        tsl = (HALO, NTOKH) if wn == "q" else (0, NTOKH)
                        P.emit("vector", lambda e, ntok=ntok, tsl=tsl, rw=rw,
                               r1=r1: e.tensor_tensor(
                                   r1[:, 0:ntok], rw[:, 0:ntok],
                                   sinsT[:, tsl[0]:tsl[1]], op=op.mult),
                               waits=[(s_sw, s_sw.n)])
                        P.emit("vector", lambda e, ntok=ntok, tsl=tsl, qk=qk,
                               r2=r2: e.tensor_tensor(
                                   r2[:, 0:ntok], qk[:, 0:ntok],
                                   cosT[:, tsl[0]:tsl[1]], op=op.mult))
                        P.dr("vector")
                        P.emit("vector", lambda e, ntok=ntok, qk=qk, r1=r1,
                               r2=r2: e.tensor_tensor(
                                   qk[:, 0:ntok], r2[:, 0:ntok], r1[:, 0:ntok],
                                   op=op.add), inc=s_ve)
                        P.emit("sync", lambda e, spt=spt, mg=mg, ntok=ntok,
                               qk=qk: e.dma_start(
                                   spt[mg, :, 0:ntok], qk[:, 0:ntok]),
                               waits=[(s_ve, s_ve.n)], inc=s_qsp, inck=16)
                        qkb_spill[par] = s_qsp.n
                        chunk_use[cidx] = s_mm.n
                else:
                    for m in range(10):
                        pji = m % 2
                        vbi = vb[m % 2]
                        wfirst = [wb_ready, (s_ve, pj_drain.get(pji, 0))]
                        for k in range(16):
                            t0 = m * 128
                            P.emit("tensor",
                                   lambda e, pji=pji, k=k, t0=t0, wb=wb:
                                   e.matmul(pj[pji][:, :],
                                            xnT_slice(k, t0, 128),
                                            wb[:, k * 512:(k + 1) * 512],
                                            start=(k == 0), stop=(k == 15)),
                                   waits=(wfirst if k == 0 else []),
                                   inc=(s_mm if k == 15 else None))
                        P.emit("vector", lambda e, pji=pji, vbi=vbi: e.tensor_copy(
                            vbi[:, :], pj[pji][:, :]),
                            waits=[(s_mm, s_mm.n)]
                            + ([vb_spill[m % 2]] if (m % 2) in vb_spill else []),
                            inc=s_ve)
                        pj_drain[pji] = s_ve.n
                        P.emit("sync", lambda e, vbi=vbi, m=m, c=c: e.dma_start(
                            vsp[m * 128:(m + 1) * 128, c * 512:(c + 1) * 512],
                            vbi[:, :]), waits=[(s_ve, s_ve.n)],
                            inc=s_vb[m % 2], inck=16)
                        vb_spill[m % 2] = (s_vb[m % 2], s_vb[m % 2].n)
                    chunk_use[cidx] = s_mm.n
                cidx += 1
        quiesce()
        P.run_block(nc)
    pA.close()
    if upto <= 2:
        pX.close(); es.close()
        return nc

    with contextlib.ExitStack() as p3:
        def p3sb(name, shape, dt):
            return p3.enter_context(nc.sbuf_tensor(name, shape, dt))
        qwin = [p3sb(f"qwin{i}", [128, 16 * 256], BF16) for i in range(2)]
        kwin = [p3sb(f"kwin{i}", [128, 16 * 512], BF16) for i in range(2)]
        vwin = [p3sb(f"vwin{i}", [128, 4 * 2048], BF16) for i in range(2)]
        probs = [p3sb(f"probs{i}", [128, 1024], BF16) for i in range(3)]
        pTs = [p3sb(f"pTs{i}", [128, 1024], BF16) for i in range(2)]
        atile = p3sb("atile", [128, 16 * 256], BF16)
        mt = [p3sb(f"mt{i}", [128, 256], F32) for i in range(2)]
        psc = [p3.enter_context(nc.psum_tensor(f"psc{i}", [128, 512], F32))
               for i in range(4)]
        ppt = [p3.enter_context(nc.psum_tensor(f"ppt{i}", [128, 1024], BF16))
               for i in range(2)]
        pao = [p3.enter_context(nc.psum_tensor(f"pao{i}", [128, 512], F32))
               for i in range(2)]

        for j in range(2):
            P.emit("gpsimd", lambda e, j=j: e.memset(mt[j][:, :], 0.0), inc=s_gp)
            P.emit("gpsimd", lambda e, j=j: e.affine_select(
                out=mt[j][:, :], in_=mt[j][:, :], compare_op=op.is_ge,
                fill=NEG, base=j * 128, pattern=[[-1, 256]],
                channel_multiplier=1), waits=[(s_gp, s_gp.n)], inc=s_gp)
        mask_ready = s_gp.n

        win_used = {}
        sc_read = {}
        pt_drain = {}
        ao_drain = {}
        probs_tt = {0: 0, 1: 0, 2: 0}
        pts_mm = {0: 0, 1: 0}
        asp_prev = [0]
        spills_done = [(s_qsp, s_qsp.n), vb_spill[0], vb_spill[1]]
        for ci in range(4):
            qw, kw, vw = qwin[ci % 2], kwin[ci % 2], vwin[ci % 2]
            wreuse = [(s_mm, win_used.get(ci - 2, 0))]
            if ci == 0:
                wreuse += spills_done
            P.emit("sync", lambda e, qw=qw, ci=ci: e.dma_start(
                qw[:, :].rearrange("p (m t) -> p m t", t=256),
                qsp[:, :, ci * 256:(ci + 1) * 256]
                .rearrange("m p t -> p m t")), waits=wreuse,
                inc=s_wd[ci % 2], inck=16)
            P.emit("sync", lambda e, kw=kw, ci=ci: e.dma_start(
                kw[:, :].rearrange("p (m t) -> p m t", t=512),
                ksp[:, :, ci * 256:ci * 256 + 512]
                .rearrange("m p t -> p m t")), inc=s_wd[ci % 2], inck=16)
            P.emit("sync", lambda e, vw=vw, ci=ci: e.dma_start(
                vw[:, :].rearrange("p (m t) -> p m t", t=2048),
                vsp[ci * 256:ci * 256 + 512, :]
                .rearrange("(m p) t -> p m t", p=128)),
                inc=s_wd[ci % 2], inck=16)
            dr_here = (s_wd[ci % 2], s_wd[ci % 2].n)
            norm_mark = {}

            def emitAB(h, qw=qw, kw=kw, ci=ci, dr_here=dr_here):
                b0, b1 = (2 * (h % 2)), (2 * (h % 2) + 1)
                p = h % 2
                pi = h % 3
                prb = probs[pi]
                for j, bj in enumerate((b0, b1)):
                    wfirst = [dr_here, (s_ac, sc_read.get(bj, 0))]
                    P.emit("tensor", lambda e, bj=bj, qw=qw, kw=kw, h=h, j=j:
                           e.matmul(psc[bj][:, :],
                                    qw[:, h * 256 + j * 128:
                                       h * 256 + (j + 1) * 128],
                                    kw[:, h * 512:(h + 1) * 512],
                                    start=True, stop=True), waits=wfirst,
                           inc=(s_mm if j == 1 else None))
                mm_here = s_mm.n
                nmask = 4 if ci == 0 else 2
                cnt_m = [0]
                for j, bj in enumerate((b0, b1)):
                    cnt_m[0] += 1
                    P.emit("vector", lambda e, bj=bj, j=j: e.tensor_tensor(
                        psc[bj][:, 256:512], psc[bj][:, 256:512], mt[j][:, :],
                        op=op.add), waits=([(s_mm, mm_here), (s_gp, mask_ready),
                                            (s_mf, 16)]
                                           if j == 0 else []),
                        inc=(s_ve if cnt_m[0] == nmask else None))
                    if ci == 0:
                        cnt_m[0] += 1
                        P.emit("vector", lambda e, bj=bj: e.tensor_scalar(
                            psc[bj][:, 0:256], psc[bj][:, 0:256], maskfb[:, :],
                            None, op0=op.add),
                            inc=(s_ve if cnt_m[0] == nmask else None))
                for j, bj in enumerate((b0, b1)):
                    P.emit("scalar", lambda e, bj=bj, j=j, prb=prb, p=p:
                           e.activation(
                               prb[:, j * 512:(j + 1) * 512], psc[bj][:, :],
                               AF.Exp, scale=SCALE,
                               accum_out=stats[:, 18 + 2 * p + j:19 + 2 * p + j]),
                           waits=([(s_ve, s_ve.n), (s_tt, probs_tt[pi])]
                                  if j == 0 else []),
                           inc=(s_ac if j == 1 else None))
                for j, bj in enumerate((b0, b1)):
                    sc_read[bj] = s_ac.n
                P.dr("vector")
                P.emit("vector", lambda e, p=p: e.reciprocal(
                    stats[:, 22 + 2 * p:24 + 2 * p],
                    stats[:, 18 + 2 * p:20 + 2 * p]), waits=[(s_ac, s_ac.n)])
                P.dr("vector")
                for j in range(2):
                    P.emit("vector", lambda e, j=j, prb=prb, p=p: e.tensor_scalar(
                        prb[:, j * 512:(j + 1) * 512],
                        prb[:, j * 512:(j + 1) * 512],
                        stats[:, 22 + 2 * p + j:23 + 2 * p + j], None,
                        op0=op.mult),
                        inc=(s_ve if j == 1 else None))
                norm_mark[h] = s_ve.n

            def emitCD(h, vw=vw):
                p = h % 2
                pi = h % 3
                prb = probs[pi]
                pb = ppt[h % 2]
                wfirst = [(s_ve, norm_mark[h]), (s_ve, pt_drain.get(h % 2, 0))]
                for j in range(2):
                    for bb in range(4):
                        P.emit("tensor", lambda e, pb=pb, j=j, bb=bb, prb=prb:
                               e.transpose(
                                   pb[:, bb * 256 + j * 128:
                                      bb * 256 + (j + 1) * 128],
                                   prb[:, j * 512 + bb * 128:
                                       j * 512 + (bb + 1) * 128],
                                   ident[:, :]),
                               waits=(wfirst if (j, bb) == (0, 0) else []),
                               inc=(s_tt if (j, bb) == (1, 3) else None))
                probs_tt[pi] = s_tt.n
                P.emit("vector", lambda e, pb=pb, p=p: e.tensor_copy(
                    pTs[p][:, :], pb[:, :]),
                    waits=[(s_tt, s_tt.n), (s_mm, pts_mm[p])], inc=s_ve)
                pt_drain[h % 2] = s_ve.n
                po = pao[h % 2]
                wfirst = [(s_ve, s_ve.n), (s_ve, ao_drain.get(h % 2, 0))]
                for bb in range(4):
                    P.emit("tensor", lambda e, po=po, vw=vw, h=h, bb=bb, p=p:
                           e.matmul(
                               po[:, 0:256], vw[:, bb * 2048 + h * 128:
                                                bb * 2048 + (h + 1) * 128],
                               pTs[p][:, bb * 256:(bb + 1) * 256],
                               start=(bb == 0), stop=(bb == 3)),
                           waits=(wfirst if bb == 0 else []),
                           inc=(s_mm if bb == 3 else None))
                pts_mm[p] = s_mm.n
                wad = [(s_mm, s_mm.n)]
                if h == 0:
                    wad.append((s_at, asp_prev[0]))
                P.emit("vector", lambda e, po=po, h=h: e.tensor_copy(
                    atile[:, h * 256:(h + 1) * 256], po[:, 0:256]),
                    waits=wad, inc=s_ve)
                ao_drain[h % 2] = s_ve.n

            emitAB(0)
            emitAB(1)
            for h in range(16):
                if h + 2 < 16:
                    emitAB(h + 2)
                emitCD(h)
            win_used[ci] = s_mm.n
            P.emit("sync", lambda e, ci=ci: e.dma_start(
                asp[:, :, ci * 256:(ci + 1) * 256]
                .rearrange("m p t -> p m t"),
                atile[:, :].rearrange("p (m t) -> p m t", t=256)),
                waits=[(s_ve, s_ve.n)], inc=s_at, inck=16)
            asp_prev[0] = s_at.n
        quiesce()
        P.run_block(nc)
    if upto <= 3:
        pX.close(); es.close()
        return nc

    pArena = contextlib.ExitStack()
    arena_t = pArena.enter_context(nc.sbuf_tensor("arena", [128, 8 * NH], FP16, side="right"))
    p4 = contextlib.ExitStack()
    with p4:
        def p4sb(name, shape, dt):
            return p4.enter_context(nc.sbuf_tensor(name, shape, dt))
        selscr = p4sb("selscr", [128, 4096], FP16)
        gbuf = p4sb("gbuf", [128, 2048], BF16)
        stg = [p4sb(f"stg{i}", [128, 2048], F32) for i in range(2)]
        tern = [p4sb(f"tern{i}", [128, 16 * 256], BF16) for i in range(2)]
        bsb = p4sb("bsb", [1, 256], BF16)
        bsb2 = p4sb("bsb2", [1, 256], BF16)
        pz = [p4.enter_context(nc.psum_tensor(f"pz{i}", [128, 512], F32))
              for i in range(4)]
        phh = [p4.enter_context(nc.psum_tensor(f"phh{i}", [128, 1024], FP16))
               for i in range(2)]
        pm = p4.enter_context(nc.psum_tensor("pm", [1, 128], F32))
        pm2 = p4.enter_context(nc.psum_tensor("pm2", [128, 64], F32))

        P.emit("tensor", lambda e: e.matmul(pm[:, 0:64], stats[:, 10:11],
                                            mcols[:, :], start=True, stop=True),
               waits=[(s_ac, mean_done_ac)], inc=s_mm)
        P.emit("vector", lambda e: e.tensor_scalar(
            pm2[0:1, 0:32], pm[0:1, 0:32], 0.0, None, op0=op.add,
            op1=op.add, accum_out=scal[0:1, 0:1]), waits=[(s_mm, s_mm.n)])
        P.dr("vector")
        P.emit("vector", lambda e: e.tensor_scalar(
            pm2[0:1, 0:32], pm[0:1, 32:64], 0.0, None, op0=op.add,
            op1=op.add, accum_out=scal[0:1, 1:2]))
        P.dr("vector")
        P.emit("vector", lambda e: e.tensor_scalar(
            scal[0:1, 2:4], scal[0:1, 0:2], 1.0 / (NH * C), None, op0=op.mult))
        P.dr("vector")
        P.emit("vector", lambda e: e.tensor_scalar(
            scal[0:1, 4:6], scal[0:1, 2:4], 1e-8, 0.5, op0=op.add, op1=op.mult))
        P.emit("vector", lambda e: e.tensor_scalar(
            scal[0:1, 6:8], scal[0:1, 2:4], -1.0, None, op0=op.mult), inc=s_ve)
        P.dr("vector")
        P.emit("tensor", lambda e: e.matmul(
            pm2[:, 0:6], onesf[:, :], scal[0:1, 2:8], start=True, stop=True),
            waits=[(s_ve, s_ve.n)], inc=s_mm)
        P.emit("vector", lambda e: e.tensor_copy(scal[:, 2:8], pm2[:, 0:6]),
               waits=[(s_mm, s_mm.n)], inc=s_ve)
        P.emit("gpsimd", lambda e: e.memset(stats[:, 31:32], 0.0),
               waits=[(s_ve, s_ve.n)], inc=s_gp)
        P.emit("vector", lambda e: e.tensor_scalar(
            stats[:, 24:25], scal[:, 4:5], -1.0, None, op0=op.mult),
            waits=[(s_gp, s_gp.n)])
        P.emit("vector", lambda e: e.tensor_scalar(
            stats[:, 26:27], scal[:, 5:6], -1.0, None, op0=op.mult))
        P.emit("vector", lambda e: e.reciprocal(stats[:, 25:26], scal[:, 6:7]))
        P.emit("vector", lambda e: e.reciprocal(stats[:, 27:28], scal[:, 7:8]),
               inc=s_ve)
        mean_ready = s_ve.n

        q_done = {}
        stg_used = {}
        prev_stt = [0]
        bias_prev = [0]
        tap_done = [0]
        z_drained = {}
        sel_done = {}
        pz_drain = {}
        sig_ready = 0

        def zq(t, q):
            return arena_t[:, t * NH + q * 2048: t * NH + (q + 1) * 2048]

        def shalf(i):
            return selscr[:, (i % 2) * 2048:(i % 2) * 2048 + 2048]

        def emit_sel_all():
            if "zraw" in taps:
                for t in range(8):
                    P.emit("sync", lambda e, t=t: e.dma_start(
                        taps["zraw"][t, :, :],
                        arena_t[:, t * NH:(t + 1) * NH]),
                        waits=[(s_ac, z_drained[7])], inc=s_ds, inck=16)
                tap_done[0] = s_ds.n
            wz = [(s_ac, z_drained[7]), (s_ds, tap_done[0])]
            GE, LO, HI, MID, MU, CNT = 16, 32, 40, 48, 56, 112
            GENA, NMA, GEND = 0, 56, 60

            def ACC(q, t):
                return 64 + q * 8 + t

            def sc(c, w=1):
                return stats[:, c:c + w]
            first = True
            for t in range(4):
                for q in range(4):
                    if (4 * t + q) % 2 == 0 and not first:
                        P.dr("vector")
                    P.emit("vector", lambda e, t=t, q=q: e.tensor_scalar(
                        shalf(4 * t + q), zq(t, q), 0.0, None, op0=op.add,
                        op1=op.add, accum_out=sc(ACC(q, t))),
                        waits=(wz if first else []))
                    first = False
            P.dr("vector")
            for q in range(4):
                P.emit("scalar", lambda e, q=q: e.activation(
                    gbuf[:, :], zq(0, q), AF.Square,
                    accum_out=sc(4 + q)),
                    inc=(s_ac if q == 3 else None))
            sig_ac = s_ac.n
            for t in range(4, 8):
                for q in range(4):
                    P.emit("scalar", lambda e, t=t, q=q: e.activation(
                        gbuf[:, :], zq(t, q), AF.Identity,
                        accum_out=sc(ACC(q, t))),
                        inc=(s_ac if (t, q) == (7, 3) else None))
            act_mu = s_ac.n
            P.emit("vector", lambda e: e.tensor_tensor(
                sc(96, 8), sc(64, 8), sc(72, 8), op=op.add),
                waits=[(s_ac, act_mu)])
            P.emit("vector", lambda e: e.tensor_tensor(
                sc(104, 8), sc(80, 8), sc(88, 8), op=op.add))
            P.dr("vector")
            P.emit("vector", lambda e: e.tensor_tensor(
                sc(112, 8), sc(96, 8), sc(104, 8), op=op.add))
            P.dr("vector")
            P.emit("vector", lambda e: e.tensor_scalar(
                sc(MU, 8), sc(112, 8), 1.0 / NH, None, op0=op.mult))
            P.dr("vector")
            P.emit("vector", lambda e: e.tensor_tensor(
                sc(8), sc(4), sc(5), op=op.add))
            P.emit("vector", lambda e: e.tensor_tensor(
                sc(9), sc(6), sc(7), op=op.add))
            P.emit("vector", lambda e: e.tensor_tensor(
                sc(11), sc(MU), sc(MU), op=op.mult))
            P.dr("vector")
            P.emit("vector", lambda e: e.tensor_tensor(
                sc(12), sc(8), sc(9), op=op.add))
            P.dr("vector")
            P.emit("vector", lambda e: e.tensor_scalar(
                sc(13), sc(12), 1.0 / NH, None, op0=op.mult))
            P.dr("vector")
            P.emit("vector", lambda e: e.tensor_tensor(
                sc(14), sc(13), sc(11), op=op.subtract),
                inc=s_ve)
            P.emit("tensor", lambda e: e.matmul(
                pm[:, 2:3], stats[:, 10:11], sc(14),
                start=True, stop=True), waits=[(s_ve, s_ve.n)], inc=s_mm)
            P.emit("vector", lambda e: e.tensor_scalar(
                scal[0:1, 0:1], pm[:, 2:3], 1.0 / 128, None, op0=op.mult),
                waits=[(s_mm, s_mm.n)], inc=s_ve)
            P.emit("scalar", lambda e: e.activation(
                scal[0:1, 1:2], scal[0:1, 0:1], AF.Sqrt),
                waits=[(s_ve, s_ve.n)], inc=s_ac)
            P.emit("tensor", lambda e: e.matmul(
                pm2[:, 0:2], onesf[:, :], scal[0:1, 0:2], start=True, stop=True),
                waits=[(s_ac, s_ac.n)], inc=s_mm)
            P.emit("vector", lambda e: e.tensor_copy(scal[:, 0:2], pm2[:, 0:2]),
                   waits=[(s_mm, s_mm.n)], inc=s_ve)
            P.emit("gpsimd", lambda e: e.memset(stats[:, 30:31], 0.0),
                   waits=[(s_ve, s_ve.n)], inc=s_gp)
            sig_ready = s_gp.n
            for t in range(8):
                P.emit("vector", lambda e, t=t: e.scalar_tensor_tensor(
                    stats[:, LO + t:LO + t + 1], scal[:, 1:2], 1.0,
                    stats[:, MU + t:MU + t + 1], op0=op.mult, op1=op.add),
                    waits=([(s_gp, sig_ready)] if t == 0 else []))
            for t in range(8):
                P.emit("vector", lambda e, t=t: e.scalar_tensor_tensor(
                    stats[:, HI + t:HI + t + 1], scal[:, 1:2], 1.6,
                    stats[:, MU + t:MU + t + 1], op0=op.mult, op1=op.add))
            P.dr("vector")
            P.emit("vector", lambda e: e.tensor_tensor(
                sc(96, 8), sc(LO, 8), sc(HI, 8), op=op.add))
            P.dr("vector")
            P.emit("vector", lambda e: e.tensor_scalar(
                sc(MID, 8), sc(96, 8), 0.5, None, op0=op.mult))
            P.emit("vector", lambda e: e.tensor_scalar(
                sc(NMA, 8), sc(96, 8), -0.5, None, op0=op.mult), inc=s_ve)
            a_state = s_ve.n
            P.dr("vector")
            P.emit("gpsimd", lambda e: e.memset(stats[:, 8:12], 818.5),
                   waits=[(s_mm, s_mm.n)])
            P.emit("gpsimd", lambda e: e.memset(stats[:, 12:16], -6555.0),
                   inc=s_gp)
            thr_ready = s_gp.n
            for it in range(NSEL):
                for t in range(4):
                    for q in range(4):
                        if (4 * t + q) % 2 == 0:
                            P.dr("vector")
                        P.emit("vector", lambda e, t=t, q=q:
                               e.tensor_scalar(
                                   shalf(4 * t + q), zq(t, q),
                                   stats[:, MID + t:MID + t + 1], None,
                                   op0=op.is_ge, op1=op.add,
                                   accum_out=sc(ACC(q, t))))
                P.dr("vector")
                for t in range(4, 8):
                    for q in range(4):
                        P.emit("scalar", lambda e, t=t, q=q: e.activation(
                            gbuf[:, :], zq(t, q), AF.Sign,
                            bias=stats[:, NMA + t:NMA + t + 1],
                            accum_out=sc(ACC(q, t))),
                            waits=([(s_ve, a_state)] if (t, q) == (4, 0)
                                   else []),
                            inc=(s_ac if (t, q) == (7, 3) else None))
                act_cnt = s_ac.n
                P.emit("vector", lambda e: e.tensor_tensor(
                    sc(96, 8), sc(ACC(0, 0), 8), sc(ACC(1, 0), 8),
                    op=op.add), waits=[(s_ac, act_cnt)]
                    + ([(s_gp, thr_ready)] if it == 0 else []))
                P.emit("vector", lambda e: e.tensor_tensor(
                    sc(104, 8), sc(ACC(2, 0), 8), sc(ACC(3, 0), 8),
                    op=op.add))
                P.dr("vector")
                P.emit("vector", lambda e: e.tensor_tensor(
                    sc(112, 8), sc(96, 8), sc(104, 8), op=op.add))
                P.dr("vector")
                P.emit("vector", lambda e: e.tensor_tensor(
                    sc(GE, 8), sc(112, 8), sc(8, 8), op=op.is_gt))
                P.emit("vector", lambda e: e.tensor_tensor(
                    sc(96, 8), sc(112, 8), sc(8, 8), op=op.is_le))
                P.dr("vector")
                P.emit("vector", lambda e: e.tensor_tensor(
                    sc(104, 8), sc(MID, 8), sc(LO, 8), op=op.subtract))
                P.emit("vector", lambda e: e.tensor_tensor(
                    sc(112, 8), sc(MID, 8), sc(HI, 8), op=op.subtract))
                P.dr("vector")
                P.emit("vector", lambda e: e.tensor_tensor(
                    sc(104, 8), sc(104, 8), sc(GE, 8), op=op.mult))
                P.emit("vector", lambda e: e.tensor_tensor(
                    sc(112, 8), sc(112, 8), sc(96, 8), op=op.mult))
                P.dr("vector")
                P.emit("vector", lambda e: e.tensor_tensor(
                    sc(LO, 8), sc(LO, 8), sc(104, 8), op=op.add))
                P.emit("vector", lambda e: e.tensor_tensor(
                    sc(HI, 8), sc(HI, 8), sc(112, 8), op=op.add))
                P.dr("vector")
                if it < NSEL - 1:
                    P.emit("vector", lambda e: e.tensor_tensor(
                        sc(96, 8), sc(LO, 8), sc(HI, 8), op=op.add))
                    P.dr("vector")
                    P.emit("vector", lambda e: e.tensor_scalar(
                        sc(MID, 8), sc(96, 8), 0.5, None, op0=op.mult))
                    P.emit("vector", lambda e: e.tensor_scalar(
                        sc(NMA, 8), sc(96, 8), -0.5, None, op0=op.mult),
                        inc=s_ve)
                    a_state = s_ve.n
                    P.dr("vector")
            for t in range(8):
                for q in range(4):
                    if (4 * t + q) % 2 == 0:
                        P.dr("vector")
                    P.emit("vector", lambda e, t=t, q=q: e.tensor_scalar(
                        shalf(4 * t + q), zq(t, q),
                        stats[:, LO + t:LO + t + 1], None,
                        op0=op.is_ge, op1=op.add,
                        accum_out=sc(ACC(q, t))))
            P.dr("vector")
            P.emit("vector", lambda e: e.tensor_tensor(
                sc(96, 8), sc(64, 8), sc(72, 8), op=op.add))
            P.emit("vector", lambda e: e.tensor_tensor(
                sc(104, 8), sc(80, 8), sc(88, 8), op=op.add))
            P.dr("vector")
            P.emit("vector", lambda e: e.tensor_tensor(
                sc(112, 8), sc(96, 8), sc(104, 8), op=op.add), inc=s_ve)
            P.dr("vector")
            for t in range(8):
                for qp in range(2):
                    q0, q1 = 2 * qp, 2 * qp + 1
                    P.dr("vector")
                    P.emit("vector", lambda e, t=t, q=q0: e.tensor_scalar(
                        shalf(q), zq(t, q), stats[:, LO + t:LO + t + 1], BIGH,
                        op0=op.is_lt, op1=op.mult))
                    P.emit("vector", lambda e, t=t, q=q1: e.tensor_scalar(
                        shalf(q), zq(t, q), stats[:, LO + t:LO + t + 1], BIGH,
                        op0=op.is_lt, op1=op.mult))
                    P.dr("vector")
                    P.emit("vector", lambda e, t=t, q=q0: e.tensor_tensor(
                        shalf(q), shalf(q), zq(t, q), op=op.add))
                    P.emit("vector", lambda e, t=t, q=q1: e.tensor_tensor(
                        shalf(q), shalf(q), zq(t, q), op=op.add))
                    P.dr("vector")
                    P.emit("vector", lambda e, t=t, q=q0: e.tensor_scalar(
                        shalf(q), shalf(q), 0.0, None, op0=op.add, op1=op.min,
                        accum_out=sc(ACC(q0, t))))
                    P.emit("vector", lambda e, t=t, q=q1: e.tensor_scalar(
                        shalf(q), shalf(q), 0.0, None, op0=op.add, op1=op.min,
                        accum_out=sc(ACC(q1, t))))
            P.dr("vector")
            P.emit("vector", lambda e: e.tensor_tensor(
                sc(96, 8), sc(64, 8), sc(72, 8), op=op.min))
            P.emit("vector", lambda e: e.tensor_tensor(
                sc(104, 8), sc(80, 8), sc(88, 8), op=op.min))
            P.dr("vector")
            P.emit("vector", lambda e: e.tensor_tensor(
                sc(GE, 8), sc(96, 8), sc(104, 8), op=op.min))
            P.dr("vector")
            P.emit("vector", lambda e: e.tensor_scalar(
                sc(104, 8), sc(112, 8), 819.5, None, op0=op.is_gt))
            P.emit("vector", lambda e: e.tensor_scalar(
                sc(MID, 8), sc(LO, 8), -1e-12, None, op0=op.add))
            P.dr("vector")
            P.emit("vector", lambda e: e.tensor_tensor(
                sc(96, 8), sc(GE, 8), sc(MID, 8), op=op.subtract))
            P.dr("vector")
            P.emit("vector", lambda e: e.tensor_tensor(
                sc(96, 8), sc(96, 8), sc(104, 8), op=op.mult))
            P.dr("vector")
            P.emit("vector", lambda e: e.tensor_tensor(
                sc(112, 8), sc(MID, 8), sc(96, 8), op=op.add))
            P.dr("vector")
            for t in range(8):
                for q in range(4):
                    P.emit("vector", lambda e, t=t, q=q:
                           e.scalar_tensor_tensor(
                               zq(t, q), zq(t, q),
                               stats[:, CNT + t:CNT + t + 1], zq(t, q),
                               op0=op.is_gt, op1=op.mult),
                           inc=(s_ve if q == 3 else None))
                sel_done[t] = s_ve.n
                P.dr("vector")
            if "seldbg" in taps:
                P.emit("sync", lambda e: e.dma_start(
                    taps["seldbg"][0:128, :], stats[:, :]),
                    waits=[(s_ve, sel_done[7])], inc=s_ds, inck=16)
                P.emit("sync", lambda e: e.dma_start(
                    taps["seldbg"][128:129, 0:8], scal[0:1, :]),
                    inc=s_ds, inck=16)
                tap_done[0] = s_ds.n
            if "zdbg" in taps:
                for t in range(8):
                    P.emit("sync", lambda e, t=t: e.dma_start(
                        taps["zdbg"][t, :, :],
                        arena_t[:, t * NH:(t + 1) * NH]),
                        waits=[(s_ve, sel_done[t])], inc=s_ds, inck=16)
                tap_done[0] = s_ds.n

        for ch in range(32):
            tn = tern[ch % 2]
            for p2i in range(2):
                pi = ch * 2 + p2i
                sg = stg[pi % 2]
                w = [stg_used[pi - 2]] if (pi - 2) in stg_used else []
                P.emit("sync", lambda e, sg=sg, pi=pi: e.dma_start(
                    sg[:, :], wsT[pi]),
                    waits=w, inc=s_sg[pi % 2], inck=16)
                wq = [(s_sg[pi % 2], s_sg[pi % 2].n), (s_ve, mean_ready)]
                if p2i == 0:
                    wq.append((s_mm, q_done.get(ch - 2, 0)))

                def tslice(tn=tn, p2i=p2i):
                    return tn[:, :].rearrange("p (k o) -> p k o", o=256)[
                        :, :, p2i * 128:(p2i + 1) * 128]
                P.emit("vector", lambda e, sg=sg: e.tensor_scalar(
                    gbuf[:, :], sg[:, :], scal[:, 4:5], None, op0=op.is_gt),
                    waits=wq, inc=s_ve)
                P.dr("vector")
                P.emit("vector", lambda e, sg=sg, tn=tn, p2i=p2i:
                       e.scalar_tensor_tensor(
                           tslice(tn, p2i), sg[:, :], stats[:, 24:25],
                           gbuf[:, :], op0=op.is_lt, op1=op.subtract),
                       inc=s_ve)
                P.dr("vector")
                stg_used[pi] = (s_ve, s_ve.n)
                prev_stt[0] = s_ve.n
            tern_ready = s_ve.n
            P.emit("gpsimd", lambda e, ch=ch: e.dma_start(
                bsb[:, :], bsynt[:, ch * 256:(ch + 1) * 256]),
                waits=[(s_ve, bias_prev[0])], inc=s_bs, inck=16)
            P.emit("vector", lambda e: e.tensor_scalar(
                bsb2[:, :], bsb[:, :], stats[0:1, 25:26], None, op0=op.mult),
                waits=[(s_bs, s_bs.n), (s_ve, mean_ready),
                       (s_mm, q_done.get(ch - 1, 0))], inc=s_ve)
            bias_ready = s_ve.n
            bias_prev[0] = s_ve.n
            for tile in range(8):
                pzi = (ch * 8 + tile) % 4
                wfirst = [(s_ve, tern_ready), (s_ve, bias_ready),
                          (s_ac, pz_drain.get(pzi, 0))]
                for k in range(16):
                    P.emit("tensor", lambda e, pzi=pzi, tn=tn, k=k, tile=tile:
                           e.matmul(pz[pzi][:, 0:256],
                                    xnTc[:, k * NTOK + tile * 128:
                                         k * NTOK + (tile + 1) * 128],
                                    tn[:, k * 256:(k + 1) * 256],
                                    start=(k == 0), stop=False),
                           waits=(wfirst if k == 0 else []))
                P.emit("tensor", lambda e, pzi=pzi: e.matmul(
                    pz[pzi][:, 0:256], onesb[0:1, 0:128], bsb2[:, :],
                    start=False, stop=True), inc=s_mm)
                P.emit("scalar", lambda e, pzi=pzi, tile=tile, ch=ch:
                       e.activation(
                           arena_t[:, tile * NH + ch * 256:
                                   tile * NH + (ch + 1) * 256],
                           pz[pzi][:, 0:256], AF.Identity, scale=scal[:, 6:7]),
                       waits=[(s_mm, s_mm.n)], inc=s_ac)
                pz_drain[pzi] = s_ac.n
                if ch == 31:
                    z_drained[tile] = s_ac.n
            q_done[ch] = s_mm.n
        emit_sel_all()
        hh_drain = {}
        for tile in range(8):
            for g in range(8):
                ph = phh[g % 2]
                wfirst = [(s_ve, sel_done[tile]),
                          (s_ve, hh_drain.get(g % 2, 0)),
                          (s_ds, tap_done[0])]
                for kk in range(8):
                    k = g * 8 + kk
                    P.emit("tensor", lambda e, ph=ph, tile=tile, k=k, kk=kk:
                           e.transpose(
                               ph[:, kk * 128:(kk + 1) * 128],
                               arena_t[:, tile * NH + k * 128:
                                       tile * NH + (k + 1) * 128],
                               identh[:, :]),
                           waits=(wfirst if kk == 0 else []),
                           inc=(s_tt if kk == 7 else None))
                P.emit("vector", lambda e, ph=ph, tile=tile, g=g: e.tensor_copy(
                    arena_t[:, tile * NH + g * 1024:tile * NH + (g + 1) * 1024],
                    ph[:, :]), waits=[(s_tt, s_tt.n)], inc=s_ve)
                hh_drain[g % 2] = s_ve.n
        quiesce()
        P.run_block(nc)
    pX.close()
    if upto <= 4:
        pArena.close(); es.close()
        return nc

    if True:
        with contextlib.ExitStack() as p5:
            def p5sb(name, shape, dt):
                return p5.enter_context(nc.sbuf_tensor(name, shape, dt))
            stg5 = [p5sb(f"stg5{i}", [128, 2048], F32) for i in range(2)]
            onesh = p5sb("onesh", [1, 512], FP16)
            gb5 = p5sb("gb5", [128, 2048], FP16)
            ternD = [p5sb(f"ternD{i}", [128, 8192], FP16) for i in range(2)]
            bob = p5sb("bob", [1, 128], FP16)
            bob2 = [p5sb(f"bob2{i}", [1, 128], FP16) for i in range(2)]
            obuf = [p5sb(f"obuf{i}", [128, 1024], BF16) for i in range(2)]
            po5 = [p5.enter_context(nc.psum_tensor(f"po5{i}", [128, 512], F32))
                   for i in range(4)]

            P.emit("gpsimd", lambda e: e.memset(onesh[:, :], 1.0), inc=s_gp)
            onesh_ready = s_gp.n
            stg_used5 = {}
            tern_used5 = {}
            po_drain5 = {}
            ob_used = {}
            prev_stt5 = [0]
            b5_prev = [0]
            t5r, b5r = {}, {}

            def emit_quant5(d):
                tn = ternD[d % 2]
                wtn = [(s_mm, tern_used5.get(d - 2, 0))]
                for q4 in range(4):
                    pi = d * 4 + q4
                    sg = stg5[pi % 2]
                    w = [stg_used5[pi - 2]] if (pi - 2) in stg_used5 else []
                    P.emit("sync", lambda e, sg=sg, d=d, q4=q4: e.dma_start(
                        sg[:, :], woutT[d * 4 + q4]),
                        waits=w, inc=s_s5[pi % 2], inck=16)
                    P.emit("vector", lambda e, sg=sg: e.tensor_scalar(
                        gb5[:, :], sg[:, :], scal[:, 5:6], None, op0=op.is_gt),
                        waits=[(s_s5[pi % 2], s_s5[pi % 2].n)]
                        + (wtn if q4 == 0 else []), inc=s_ve)
                    P.dr("vector")
                    P.emit("vector", lambda e, sg=sg, tn=tn, q4=q4:
                           e.scalar_tensor_tensor(
                               tn[:, q4 * 2048:(q4 + 1) * 2048], sg[:, :],
                               stats[:, 26:27], gb5[:, :],
                               op0=op.is_lt, op1=op.subtract),
                           inc=s_ve)
                    P.dr("vector")
                    stg_used5[pi] = (s_ve, s_ve.n)
                t5r[d] = s_ve.n
                P.emit("gpsimd", lambda e, d=d: e.dma_start(
                    bob[:, :], boutt[:, d * 128:(d + 1) * 128]),
                    waits=[(s_ve, b5_prev[0])], inc=s_bo, inck=16)
                P.emit("vector", lambda e, d=d: e.tensor_scalar(
                    bob2[d % 2][:, :], bob[:, :], stats[0:1, 27:28], None,
                    op0=op.mult),
                    waits=[(s_bo, s_bo.n), (s_mm, tern_used5.get(d - 2, 0))],
                    inc=s_ve)
                b5r[d] = s_ve.n
                b5_prev[0] = s_ve.n

            emit_quant5(0)
            for d in range(16):
                tn = ternD[d % 2]
                if d + 1 < 16:
                    emit_quant5(d + 1)
                ob = obuf[d % 2]
                for n in range(2):
                    pzi = (d * 2 + n) % 4
                    wfirst = [(s_ve, t5r[d]), (s_ve, b5r[d]),
                              (s_ve, po_drain5.get(pzi, 0))]
                    if d == 0 and n == 0:
                        wfirst.append((s_gp, onesh_ready))
                    for j in range(64):
                        def rhs(e, j=j, n=n):
                            return arena_t[:, :].rearrange(
                                "p (i f) -> p i f", f=NH)[
                                :, 4 * n:4 * n + 4, j * 128:(j + 1) * 128]
                        P.emit("tensor", lambda e, pzi=pzi, tn=tn, j=j, n=n:
                               e.matmul(po5[pzi][:, :],
                                        tn[:, j * 128:(j + 1) * 128],
                                        arena_t[:, :].rearrange(
                                            "p (i f) -> p i f", f=NH)[
                                            :, 4 * n:4 * n + 4,
                                            j * 128:(j + 1) * 128],
                                        start=(j == 0), stop=False),
                               waits=(wfirst if j == 0 else []))
                    P.emit("tensor", lambda e, pzi=pzi, d=d: e.matmul(
                        po5[pzi][:, :], bob2[d % 2][:, :], onesh[0:1, 0:512],
                        start=False, stop=True), inc=s_mm)
                    P.emit("vector", lambda e, pzi=pzi, ob=ob, n=n:
                           e.tensor_scalar(
                               ob[:, n * 512:(n + 1) * 512], po5[pzi][:, :],
                               scal[:, 7:8], None, op0=op.mult),
                           waits=[(s_mm, s_mm.n)]
                           + ([ob_used[d - 2]]
                              if (n == 0 and (d - 2) in ob_used) else []),
                           inc=s_ve)
                    po_drain5[pzi] = s_ve.n
                tern_used5[d] = s_mm.n
                P.emit("sync", lambda e, ob=ob, d=d: e.dma_start(
                    osp[d * 128:(d + 1) * 128, :], ob[:, :]),
                    waits=[(s_ve, s_ve.n)], inc=s_os[d % 2], inck=16)
                ob_used[d] = (s_os[d % 2], s_os[d % 2].n)
            quiesce()
        P.run_block(nc)
    pArena.close()

    with contextlib.ExitStack() as p6:
        def p6sb(name, shape, dt):
            return p6.enter_context(nc.sbuf_tensor(name, shape, dt))
        wob = p6sb("wob", [128, 16 * 2048], BF16)
        gbb = p6sb("gbb", [128, 2048], F32)
        beb = p6sb("beb", [128, 2048], F32)
        xnl = [p6sb(f"xnl{i}", [128, 2048], BF16) for i in range(2)]
        onl = [p6sb(f"onl{i}", [128, 2048], BF16) for i in range(2)]
        anl = [p6sb(f"anl{i}", [128, 2048], BF16) for i in range(2)]
        xl = [p6sb(f"xl{i}", [128, 2048], F32) for i in range(2)]
        ybuf = p6sb("ybuf", [128, 2048], BF16)
        lnf = p6sb("lnf", [128, 2048], F32)
        t2buf = p6sb("t2buf", [128, 2048], F32)
        fout = [p6sb(f"fout{i}", [128, 2048], F32) for i in range(2)]
        trash6 = p6sb("trash6", [128, 4096], FP16)
        pa6 = [p6.enter_context(nc.psum_tensor(f"pa6{i}", [128, 512], F32))
               for i in range(4)]

        P.emit("sync", lambda e: e.dma_start(gbb[0:1, :], gammat[:, :]),
               inc=s_gb2, inck=16)
        P.emit("sync", lambda e: e.dma_start(beb[0:1, :], betat[:, :]),
               inc=s_gb2, inck=16)
        for n in range(4):
            P.emit("tensor", lambda e, n=n: e.matmul(
                pa6[n][:, :], onesf[:, :], gbb[0:1, n * 512:(n + 1) * 512],
                start=True, stop=True),
                waits=([(s_gb2, s_gb2.n)] if n == 0 else []),
                inc=(s_mm if n == 3 else None))
        for n in range(4):
            P.emit("vector", lambda e, n=n: e.tensor_copy(
                gbb[:, n * 512:(n + 1) * 512], pa6[n][:, :]),
                waits=([(s_mm, s_mm.n)] if n == 0 else []),
                inc=(s_ve if n == 3 else None))
        gbb_ve = s_ve.n
        for n in range(4):
            P.emit("tensor", lambda e, n=n: e.matmul(
                pa6[n][:, :], onesf[:, :], beb[0:1, n * 512:(n + 1) * 512],
                start=True, stop=True),
                waits=([(s_ve, gbb_ve)] if n == 0 else []),
                inc=(s_mm if n == 3 else None))
        for n in range(4):
            P.emit("vector", lambda e, n=n: e.tensor_copy(
                beb[:, n * 512:(n + 1) * 512], pa6[n][:, :]),
                waits=([(s_mm, s_mm.n)] if n == 0 else []),
                inc=(s_ve if n == 3 else None))
        P.emit("gpsimd", lambda e: e.memset(stats[:, 15:16], 0.0),
               waits=[(s_ve, s_ve.n)], inc=s_gp)
        gb_ready = s_gp.n
        for k in range(16):
            P.emit("gpsimd", lambda e, k=k: e.dma_start(
                wob[:, k * 2048:(k + 1) * 2048], woT[k * 128:(k + 1) * 128, :]),
                inc=s_wo, inck=16)
        wo_ready = s_wo.n

        buf_free = {}
        out_done = {}
        for i in range(8):
            xnb6, onb, anb, xb6, fo = (xnl[i % 2], onl[i % 2], anl[i % 2],
                                       xl[i % 2], fout[i % 2])
            w = [(s_ve, buf_free.get(i - 2, 0))]
            if i == 0:
                w += [(s_sp[0], s_sp[0].n), (s_sp[1], s_sp[1].n),
                      (s_os[0], s_os[0].n), (s_os[1], s_os[1].n),
                      (s_at, s_at.n)]
            P.emit("sync", lambda e, xnb6=xnb6, i=i: e.dma_start(
                xnb6[:, :], xnsp[i, :, :]), waits=w, inc=s_l[i % 2], inck=16)
            P.emit("sync", lambda e, onb=onb, i=i: e.dma_start(
                onb[:, :], osp[:, i * 128:(i + 1) * 128], transpose=True),
                inc=s_l[i % 2], inck=16)
            P.emit("sync", lambda e, anb=anb, i=i: e.dma_start(
                anb[:, :].rearrange("p (m t) -> p m t", t=128),
                asp[:, :, i * 128:(i + 1) * 128]
                .rearrange("m p t -> p m t")), inc=s_l[i % 2], inck=16)
            P.emit("scalar", lambda e, xb6=xb6, i=i: e.dma_start(
                xb6[:, :], xs[HALO + i * 128:HALO + (i + 1) * 128, :]),
                inc=s_l[i % 2], inck=16)
            dr6 = (s_l[i % 2], s_l[i % 2].n)
            for n in range(4):
                pzi = n
                wfirst = [dr6, (s_wo, wo_ready),
                          (s_ve, buf_free.get(i - 1, 0))]
                for k in range(16):
                    P.emit("tensor", lambda e, pzi=pzi, anb=anb, k=k, n=n:
                           e.matmul(pa6[pzi][:, :],
                                    anb[:, k * 128:(k + 1) * 128],
                                    wob[:, k * 2048 + n * 512:
                                        k * 2048 + (n + 1) * 512],
                                    start=(k == 0), stop=(k == 15)),
                           waits=(wfirst if (k == 0 and n == 0) else []),
                           inc=(s_mm if (k == 15 and n == 3) else None))
            mm_here = s_mm.n
            HF = C // 2
            P.emit("vector", lambda e, xnb6=xnb6, onb=onb: e.tensor_tensor(
                ybuf[:, 0:HF], xnb6[:, 0:HF], onb[:, 0:HF], op=op.add),
                waits=[dr6])
            P.emit("vector", lambda e, xnb6=xnb6, onb=onb: e.tensor_tensor(
                ybuf[:, HF:C], xnb6[:, HF:C], onb[:, HF:C], op=op.add))
            P.dr("vector")
            P.emit("vector", lambda e: e.tensor_scalar(
                trash6[:, 0:HF], ybuf[:, 0:HF], 0.0, None, op0=op.add,
                op1=op.add, accum_out=stats[:, 0:1]))
            P.emit("vector", lambda e: e.tensor_scalar(
                trash6[:, HF:C], ybuf[:, HF:C], 0.0, None, op0=op.add,
                op1=op.add, accum_out=stats[:, 10:11]))
            P.dr("vector")
            P.emit("vector", lambda e: e.tensor_copy(
                stats[:, 30:31], stats[:, 10:11]), inc=s_ve)
            P.emit("scalar", lambda e: e.activation(
                trash6[:, C:C + HF], ybuf[:, 0:HF], AF.Square,
                accum_out=stats[:, 1:2]), waits=[(s_ve, s_ve.n)])
            P.emit("scalar", lambda e: e.activation(
                trash6[:, C + HF:2 * C], ybuf[:, HF:C], AF.Square,
                accum_out=stats[:, 11:12]), inc=s_ac)
            P.dr("vector")
            P.emit("vector", lambda e: e.tensor_tensor(
                stats[:, 12:13], stats[:, 0:1], stats[:, 10:11], op=op.add),
                waits=[(s_ac, s_ac.n)])
            P.emit("vector", lambda e: e.tensor_tensor(
                stats[:, 13:14], stats[:, 1:2], stats[:, 11:12], op=op.add))
            P.dr("vector")
            P.emit("vector", lambda e: e.tensor_scalar(
                stats[:, 2:3], stats[:, 12:13], 1.0 / C, None, op0=op.mult))
            P.emit("vector", lambda e: e.tensor_scalar(
                stats[:, 3:4], stats[:, 13:14], 1.0 / C, None, op0=op.mult))
            P.dr("vector")
            P.emit("vector", lambda e: e.tensor_scalar(
                stats[:, 28:29], stats[:, 2:3], -1.0, None, op0=op.mult))
            P.emit("vector", lambda e: e.tensor_tensor(
                stats[:, 4:5], stats[:, 2:3], stats[:, 2:3], op=op.mult))
            P.dr("vector")
            P.emit("vector", lambda e: e.tensor_tensor(
                stats[:, 5:6], stats[:, 3:4], stats[:, 4:5], op=op.subtract),
                inc=s_ve)
            P.emit("scalar", lambda e: e.activation(
                stats[:, 6:7], stats[:, 5:6], AF.Sqrt, bias=cstb[:, 1:2]),
                waits=[(s_ve, s_ve.n)], inc=s_ac)
            P.emit("vector", lambda e: e.reciprocal(stats[:, 7:8], stats[:, 6:7]),
                   waits=[(s_ac, s_ac.n)])
            P.dr("vector")
            P.emit("vector", lambda e: e.tensor_tensor(
                stats[:, 9:10], stats[:, 28:29], stats[:, 7:8], op=op.mult),
                inc=s_ve)
            P.emit("scalar", lambda e: e.activation(
                lnf[:, :], ybuf[:, :], AF.Identity, bias=stats[:, 9:10],
                scale=stats[:, 7:8]), waits=[(s_ve, s_ve.n)], inc=s_ac)
            P.emit("vector", lambda e: e.tensor_tensor(
                lnf[:, :], lnf[:, :], gbb[:, :], op=op.mult),
                waits=[(s_ac, s_ac.n), (s_gp, gb_ready)])
            P.emit("vector", lambda e, xb6=xb6: e.tensor_tensor(
                t2buf[:, :], xb6[:, :], beb[:, :], op=op.add))
            P.dr("vector")
            for n in range(4):
                P.emit("vector", lambda e, fo=fo, n=n: e.tensor_tensor(
                    fo[:, n * 512:(n + 1) * 512], lnf[:, n * 512:(n + 1) * 512],
                    pa6[n][:, :], op=op.add),
                    waits=([(s_mm, mm_here)]
                           + ([out_done[i - 2]]
                              if (i - 2) in out_done else [])
                           if n == 0 else []))
            P.dr("vector")
            for n in range(4):
                P.emit("vector", lambda e, fo=fo, n=n: e.tensor_tensor(
                    fo[:, n * 512:(n + 1) * 512], fo[:, n * 512:(n + 1) * 512],
                    t2buf[:, n * 512:(n + 1) * 512], op=op.add),
                    inc=(s_ve if n == 3 else None))
            buf_free[i] = s_ve.n
            P.emit("sync", lambda e, fo=fo, i=i: e.dma_start(
                outt[i * 128:(i + 1) * 128, :], fo[:, :]),
                waits=[(s_ve, s_ve.n)], inc=s_do[i % 2], inck=16)
            out_done[i] = (s_do[i % 2], s_do[i % 2].n)
        P.emit("sync", lambda e: e.nop(),
               waits=[(s_do[0], s_do[0].n), (s_do[1], s_do[1].n)])
        quiesce()
        P.run_block(nc)

    es.close()
    return nc


def _prep_weights(inputs):
    Wq, Wk, Wv, Wo = (inputs["L0_attn_Wq"], inputs["L0_attn_Wk"],
                      inputs["L0_attn_Wv"], inputs["L0_attn_Wo"])
    h = np.arange(C) // 128
    j = np.arange(C) % 128
    perm = h * 128 + 2 * (j % 64) + (j // 64)

    def tile4(WT):
        return np.ascontiguousarray(
            WT.reshape(16, 128, 4, 512).transpose(2, 1, 0, 3).reshape(
                4, 128, 16 * 512))
    wqT = tile4(np.ascontiguousarray(Wq.T)[:, perm])
    wkT = tile4(np.ascontiguousarray(Wk.T)[:, perm])
    wvT = tile4(np.ascontiguousarray(Wv.T))
    woT = np.ascontiguousarray(Wo.T)
    wsT = np.ascontiguousarray(
        inputs["L0_ffn_W_syn"].T.reshape(16, 128, 64, 128)
        .transpose(2, 1, 0, 3).reshape(64, 128, C))
    woutT = np.ascontiguousarray(
        inputs["L0_ffn_W_out"].T.reshape(4, 16, 128, 16, 128)
        .transpose(3, 0, 2, 1, 4).reshape(64, 128, C))
    return dict(wqT=wqT, wkT=wkT, wvT=wvT, woT=woT, wsT=wsT, woutT=woutT,
                bsynt=inputs["L0_ffn_b_syn"].reshape(1, NH),
                boutt=inputs["L0_ffn_b_out"].reshape(1, C),
                gammat=inputs["L0_ffn_gamma"].reshape(1, C),
                betat=inputs["L0_ffn_beta"].reshape(1, C))


def kernel(**inputs):
    inputs = {k: np.asarray(v, dtype=np.float32) for k, v in inputs.items()}
    x = inputs["L0_attn_x"]
    wd = _prep_weights(inputs)
    nc = build_nc()
    in_maps = []
    for c in range(8):
        b, hh = c // 2, c % 2
        if hh == 0:
            xsh = np.concatenate([x[b, :HALO], x[b, :NTOK]], axis=0)
            mf = np.full((128, 1), NEG, dtype=np.float32)
        else:
            xsh = x[b, NTOK - HALO:2 * NTOK]
            mf = np.zeros((128, 1), dtype=np.float32)
        m = {"xs": np.ascontiguousarray(xsh), "maskft": mf}
        m.update(wd)
        in_maps.append(m)
    res = run_bass_kernel_spmd(nc, in_maps, core_ids=list(range(8)))
    outp = np.empty((Bsz, T, C), dtype=np.float32)
    for c in range(8):
        b, hh = c // 2, c % 2
        outp[b, hh * NTOK:(hh + 1) * NTOK] = res.results[c]["out"]
    return outp
